# version 14
# baseline (speedup 1.0000x reference)
import math
import numpy as np
from contextlib import ExitStack

import concourse.bass as bass
import concourse.mybir as mybir
from concourse.bass_utils import run_bass_kernel_spmd

F32 = mybir.dt.float32
BF16 = mybir.dt.bfloat16
I32 = mybir.dt.int32
ALU = mybir.AluOpType
AF = mybir.ActivationFunctionType
AX = mybir.AxisListType

LN2 = math.log(2.0)
B, S, H = 4, 2048, 2048
NH, HN = 16, 128
E, HID = 8, 2048
T = B * S
TOWN = 1024
NCORE = 8
I2L2 = 1.0 / (2.0 * LN2)
MAGIC = 12582912.0


class Sem:
    def __init__(self, st, nc, name):
        self.h = st.enter_context(nc.semaphore(name))
        self.n = 0


class Chain:

    def __init__(self, sems, sfx="j"):
        self.blk = None
        self.s = sems
        self.sfx = sfx
        self.prev = []

    def bind(self, blk):
        self.blk = blk
        return self

    def fork(self, sfx=None):
        c = Chain(self.s, sfx if sfx is not None else self.sfx)
        c.blk = self.blk
        c.prev = list(self.prev)
        return c

    def join(self, other):
        seen = {}
        for s, v in self.prev + other.prev:
            seen[id(s)] = (s, max(v, seen.get(id(s), (s, -1))[1]))
        self.prev = [sv for sv in seen.values()]
        return self

    def _wrap(self, kind, fn, dma=False):
        sem = self.s[kind + "d" + self.sfx] if dma else self.s[kind]
        prev = list(self.prev)

        def body(eng):
            for ps, pv in prev:
                eng.wait_ge(ps.h, pv)
            ins = fn(eng)
            assert ins is not None, "chain step must return its last instruction"
            ins.then_inc(sem.h, 16 if dma else 1)
            sem.n += 16 if dma else 1
        getattr(self.blk, {"v": "vector", "a": "scalar", "g": "gpsimd",
                           "t": "tensor", "sy": "sync"}[kind])(body)
        self.prev = [(sem, sem.n)]

    def v(self, fn):
        self._wrap("v", fn)

    def a(self, fn):
        self._wrap("a", fn)

    def g(self, fn):
        self._wrap("g", fn)

    def t(self, fn):
        self._wrap("t", fn)

    def dma(self, kind, fn):
        self._wrap(kind, fn, dma=True)

    def edger(self, kind):
        sem = self.s[kind + "e"]

        def E(eng, ins):
            ins.then_inc(sem.h, 1)
            sem.n += 1
            eng.wait_ge(sem.h, sem.n)
        return E

    def sched(self, kind, fn, dma=False):
        sem = self.s[kind + "d" + self.sfx] if dma else self.s[kind]
        entry = (kind, fn, dma, list(self.prev), sem, 16 if dma else 1)
        sem.n += entry[5]
        self.prev = [(sem, sem.n)]
        return entry

    def run(self, steps):
        for kind, fn, dma in steps:
            self._wrap(kind, fn, dma=dma)


KINDMAP = {"v": "vector", "a": "scalar", "g": "gpsimd", "t": "tensor",
           "sy": "sync"}


def emit_plan(blk, plan):
    kinds = []
    for e in plan:
        if e[0] not in kinds:
            kinds.append(e[0])
    for k in kinds:
        group = [e for e in plan if e[0] == k]

        def body(eng, group=group):
            for kind, fn, dma, prevs, sem, inc in group:
                for ps, pv in prevs:
                    eng.wait_ge(ps.h, pv)
                ins = fn(eng)
                assert ins is not None
                ins.then_inc(sem.h, inc)
        getattr(blk, KINDMAP[k])(body)


def interleave(blk, chains, steplists, stagger=None):
    if stagger:
        steplists = [[None] * s + list(sl)
                     for s, sl in zip(stagger, steplists)]
    plan = []
    n = max(len(s) for s in steplists)
    for i in range(n):
        for c, steps in zip(chains, steplists):
            if i < len(steps) and steps[i] is not None:
                kind, fn, dma = steps[i]
                plan.append(c.sched(kind, fn, dma=dma))
    emit_plan(blk, plan)


def build_program(debug=False):
    nc = bass.Bass("TRN2", target_bir_lowering=False, debug=False)

    x_d = nc.dram_tensor("x", [T, H], F32, kind="ExternalInput").ap()
    wqt_d = nc.dram_tensor("wqt", [H, 1536], F32, kind="ExternalInput").ap()
    wpt_d = nc.dram_tensor("wpt", [H, H], F32, kind="ExternalInput").ap()
    bproj_d = nc.dram_tensor("bproj", [1, H], F32, kind="ExternalInput").ap()
    gwt_d = nc.dram_tensor("gwt", [H, E], F32, kind="ExternalInput").ap()
    ebias_d = nc.dram_tensor("ebias", [128, E], F32, kind="ExternalInput").ap()
    sw1t_d = nc.dram_tensor("sw1t", [H, HID], F32, kind="ExternalInput").ap()
    sw3t_d = nc.dram_tensor("sw3t", [H, HID], F32, kind="ExternalInput").ap()
    sw2t_d = nc.dram_tensor("sw2t", [HID, H], F32, kind="ExternalInput").ap()
    w13_d = nc.dram_tensor("w13", [E, 2 * HID, H], F32, kind="ExternalInput").ap()
    w2_d = nc.dram_tensor("w2", [E, H, HID], F32, kind="ExternalInput").ap()
    out_d = nc.dram_tensor("out", [TOWN, H], F32, kind="ExternalOutput").ap()

    w13t_s = nc.dram_tensor("s_w13t", [E, H, 2 * HID], BF16, kind="Internal").ap()
    w2t_s = nc.dram_tensor("s_w2t", [E, HID, H], BF16, kind="Internal").ap()
    kf_s = nc.dram_tensor("s_kf", [4, HN, T], BF16, kind="Internal").ap()
    vt_s = nc.dram_tensor("s_vt", [4, T, HN], BF16, kind="Internal").ap()
    qf_s = nc.dram_tensor("s_qf", [4, HN, 4096], BF16, kind="Internal").ap()
    xno_s = nc.dram_tensor("s_xno", [TOWN, H], F32, kind="Internal").ap()
    dbg = {}
    if debug:
        for nm, shp, dt in [("d_xno", [TOWN, H], F32),
                            ("d_m", [128, 8 * E], F32),
                            ("d_sc", [128, E], F32),
                            ("d_kf", [HN, T], BF16),
                            ("d_qf", [HN, 4096], BF16),
                            ("d_vt", [T, HN], BF16),
                            ("d_w13t", [H, 2 * HID], BF16),
                            ("d_w2t", [HID, H], BF16)]:
            dbg[nm] = nc.dram_tensor(nm, shp, dt, kind="ExternalOutput").ap()

    st = ExitStack()
    with st:
        def sb(name, shape, dt=F32):
            return st.enter_context(nc.sbuf_tensor(name, shape, dt))

        sems = {k: Sem(st, nc, "s_" + k)
                for k in ["v", "a", "g", "t", "sy", "ve", "ae", "ge"]}
        for q in ["vd", "ad", "gd", "syd"]:
            for sfx in ["j", "0", "1", "2"]:
                sems[q + sfx] = Sem(st, nc, "s_" + q + sfx)
        CHAIN = Chain(sems)

        ph03 = st.enter_context(ExitStack())
        ident_bf = sb("ident_bf", [128, 128], BF16)
        ident_f = sb("ident_f", [128, 128], F32)
        iota_t = sb("iota_t", [128, 128], F32)
        iota_e = sb("iota_e", [128, E], F32)
        iota_b = sb("iota_b", [128, E], F32)
        ones1 = sb("ones1", [1, 128], F32)
        ebias = sb("ebias_t", [128, E], F32)
        cst = {}
        for nm, val in [("ln2", LN2), ("nln2", -LN2), ("n9ln2", 9 * LN2),
                        ("i2048", 1.0 / 2048.0), ("eps", 1e-5),
                        ("nhalf", -0.5), ("anorm", 1.0 / math.sqrt(HN)),
                        ("i2l2", I2L2), ("tiny", 2.0 ** -100)]:
            cst[nm] = sb("c_" + nm, [128, 1], F32)
        bprj = ph03.enter_context(nc.sbuf_tensor("bprj", [1, H], F32))

        LNEXP_SET = 6

        def preload_lnexp(a):
            return a.add_instruction(mybir.InstLoadActFuncSet(
                name=nc.get_next_instruction_name(),
                act_func_set_id=LNEXP_SET, ins=[], outs=[]))

        with nc.Block() as blk:
            C = CHAIN.bind(blk)
            C.g(lambda g: (g.iota(iota_t[:, :], [[1, 128]], channel_multiplier=-1,
                                  allow_small_or_imprecise_dtypes=True),
                           g.iota(iota_e[:, :], [[1, E]], channel_multiplier=0,
                                  allow_small_or_imprecise_dtypes=True))[-1])

            Ev0 = C.edger("v")

            def p0v(v):
                for nm, val in [("ln2", LN2), ("nln2", -LN2), ("n9ln2", 9 * LN2),
                                ("i2048", 1.0 / 2048.0), ("eps", 1e-5),
                                ("nhalf", -0.5), ("anorm", 1.0 / math.sqrt(HN)),
                                ("i2l2", I2L2), ("tiny", 2.0 ** -100)]:
                    v.memset(cst[nm][:, :], val)
                v.memset(ones1[:, :], 1.0)
                Ev0(v, v.tensor_single_scalar(ident_f[:, :], iota_t[:, :], 0.0,
                                              op=ALU.is_equal))
                v.tensor_single_scalar(iota_b[:, :], iota_e[:, :], 1.0e4,
                                       op=ALU.add)
                return v.tensor_copy(ident_bf[:, :], ident_f[:, :])
            C.v(p0v)
            C.dma("sy", lambda sy: sy.dma_start(out=bprj[:, :], in_=bproj_d))
            C.dma("sy", lambda sy: sy.dma_start(out=ebias[:, :], in_=ebias_d))



        def qdq_bufs(prefix, nfree, nbatch, alloc):
            nb = nfree // 32
            d = {"nfree": nfree, "nb": nb, "nbatch": nbatch}
            for nm in ["amx", "e0i", "bex", "aex", "Bi", "Ai", "tc", "cl",
                       "ebf", "jf", "inv"]:
                dt = I32 if nm in ("e0i", "Bi", "Ai") else F32
                d[nm] = alloc(prefix + nm, [128, nb * nbatch], dt)
            d["S1i"] = d["Bi"]
            d["S2i"] = d["Ai"]
            d["j1"] = d["bex"]
            d["j2"] = d["aex"]
            d["ebm"] = d["tc"]
            for nm, dt in [("u", F32), ("ei", I32), ("iv", I32),
                           ("ff", F32)]:
                d[nm] = alloc(prefix + nm, [128, nfree * nbatch], dt)
            d["es"] = d["u"]
            d["ri"] = d["iv"]
            return d

        def qdq_steps(items, bufs, c0, w):
            nfree, NB = bufs["nfree"], bufs["nb"]
            nb = w // 32
            b0 = c0 // 32
            ks = list(range(len(items)))

            def sm(name, k):
                return bufs[name][:, k * NB + b0:k * NB + b0 + nb]

            def fl(name, k):
                return bufs[name][:, k * nfree + c0:k * nfree + c0 + w]

            def xs(k):
                return items[k][0][:, c0:c0 + w]

            def mk_step(kind, groups):
                sem = sems[kind + "e"]
                total = sum(len(g) for g in groups)

                def fn(eng):
                    ins = None
                    done = 0
                    for gi, grp in enumerate(groups):
                        if gi > 0:
                            eng.wait_ge(sem.h, sem.n)
                        for emit in grp:
                            ins = emit(eng)
                            done += 1
                            if done < total:
                                ins.then_inc(sem.h, 1)
                                sem.n += 1
                    return ins
                return (kind, fn, False)

            S = []
            g1 = [lambda v, k=k: v.tensor_reduce(
                sm("amx", k).unsqueeze(2),
                xs(k).rearrange("p (a b) -> p a b", b=32), axis=AX.X,
                op=ALU.max, apply_absolute_value=True) for k in ks]
            g2 = []
            for k in ks:
                if items[k][1] is not None:
                    g2.append(lambda v, k=k: v.tensor_single_scalar(
                        sm("amx", k), sm("amx", k), items[k][1], op=ALU.mult))
            g3 = [lambda v, k=k: v.tensor_single_scalar(
                sm("amx", k), sm("amx", k), 2.0 ** -40, op=ALU.max) for k in ks]
            S.append(mk_step("v", [g1] + ([g2] if g2 else []) + [g3]))
            S.append(mk_step("a", [
                [lambda a, k=k: a.activation(sm("tc", k), sm("amx", k),
                                             AF.Square) for k in ks],
                [lambda a, k=k: a.activation(sm("tc", k), sm("tc", k), AF.Ln)
                 for k in ks]]))
            S.append(mk_step("v", [
                [lambda v, k=k: v.tensor_scalar(sm("tc", k), sm("tc", k), I2L2,
                                                -0.5, op0=ALU.mult, op1=ALU.add)
                 for k in ks],
                [lambda v, k=k: v.tensor_scalar(sm("tc", k), sm("tc", k),
                                                -200.0, MAGIC, op0=ALU.max,
                                                op1=ALU.add) for k in ks],
                [lambda v, k=k: v.tensor_single_scalar(sm("e0i", k),
                                                       sm("tc", k), -MAGIC,
                                                       op=ALU.add) for k in ks],
                [lambda v, k=k: v.tensor_scalar(sm("bex", k), sm("e0i", k),
                                                0.0, 14.0, op0=ALU.max,
                                                op1=ALU.min) for k in ks],
                [lambda v, k=k: v.tensor_tensor(sm("aex", k), sm("bex", k),
                                                sm("e0i", k), op=ALU.subtract)
                 for k in ks],
                [lambda v, k=k: v.tensor_single_scalar(sm("aex", k),
                                                       sm("aex", k), 14.0,
                                                       op=ALU.min)
                 for k in ks]]))
            S.append(mk_step("a", [
                [lambda a, k=k: a.activation(sm("cl", k), sm("bex", k), AF.Exp,
                                             scale=cst["ln2"][:, :])
                 for k in ks],
                [lambda a, k=k: a.activation(sm("jf", k), sm("aex", k), AF.Exp,
                                             scale=cst["ln2"][:, :])
                 for k in ks]]))
            S.append(mk_step("v", [
                [lambda v, k=k: v.tensor_scalar(sm("Bi", k), sm("cl", k),
                                                MAGIC, -MAGIC, op0=ALU.add,
                                                op1=ALU.add) for k in ks],
                [lambda v, k=k: v.tensor_scalar(sm("Ai", k), sm("jf", k),
                                                MAGIC, -MAGIC, op0=ALU.add,
                                                op1=ALU.add) for k in ks],
                [lambda v, k=k: v.tensor_tensor(sm("tc", k), sm("amx", k),
                                                sm("Ai", k), op=ALU.mult)
                 for k in ks],
                [lambda v, k=k: v.tensor_tensor(sm("cl", k), sm("tc", k),
                                                sm("Bi", k), op=ALU.is_lt)
                 for k in ks] +
                [lambda v, k=k: v.scalar_tensor_tensor(sm("jf", k),
                                                       sm("tc", k), 0.5,
                                                       sm("Bi", k),
                                                       op0=ALU.mult,
                                                       op1=ALU.is_ge)
                 for k in ks],
                [lambda v, k=k: v.tensor_tensor(sm("ebf", k), sm("e0i", k),
                                                sm("cl", k), op=ALU.subtract)
                 for k in ks],
                [lambda v, k=k: v.tensor_tensor(sm("ebf", k), sm("ebf", k),
                                                sm("jf", k), op=ALU.add)
                 for k in ks],
                [lambda v, k=k: v.tensor_scalar(sm("jf", k), sm("ebf", k),
                                                -1.0, 8.0, op0=ALU.mult,
                                                op1=ALU.add) for k in ks],
                [lambda v, k=k: v.tensor_scalar(sm("jf", k), sm("jf", k), 0.0,
                                                28.0, op0=ALU.max, op1=ALU.min)
                 for k in ks],
                [lambda v, k=k: v.tensor_single_scalar(sm("j1", k),
                                                       sm("jf", k), 14.0,
                                                       op=ALU.min)
                 for k in ks],
                [lambda v, k=k: v.tensor_tensor(sm("j2", k), sm("jf", k),
                                                sm("j1", k), op=ALU.subtract)
                 for k in ks]]))
            S.append(mk_step("a", [
                [lambda a, k=k: a.activation(sm("cl", k), sm("j1", k), AF.Exp,
                                             scale=cst["ln2"][:, :])
                 for k in ks],
                [lambda a, k=k: a.activation(sm("jf", k), sm("j2", k), AF.Exp,
                                             scale=cst["ln2"][:, :])
                 for k in ks]]))
            g7 = [
                [lambda v, k=k: v.tensor_scalar(sm("S1i", k), sm("cl", k),
                                                MAGIC, -MAGIC, op0=ALU.add,
                                                op1=ALU.add) for k in ks],
                [lambda v, k=k: v.tensor_scalar(sm("S2i", k), sm("jf", k),
                                                MAGIC, -MAGIC, op0=ALU.add,
                                                op1=ALU.add) for k in ks],
                [lambda v, k=k: v.tensor_tensor(sm("inv", k), sm("S1i", k),
                                                sm("S2i", k), op=ALU.mult)
                 for k in ks]]
            g7m = [lambda v, k=k: v.tensor_single_scalar(
                sm("inv", k), sm("inv", k), items[k][1], op=ALU.mult)
                for k in ks if items[k][1] is not None]
            g7b = [lambda v, k=k: v.tensor_single_scalar(
                sm("ebm", k), sm("ebf", k), -11.0, op=ALU.add) for k in ks]
            S.append(mk_step("v", g7 + ([g7m] if g7m else []) + [g7b]))
            S.append(mk_step("v", [
                [lambda v, k=k: v.tensor_tensor(
                    fl("u", k).rearrange("p (a b) -> p b a", b=32),
                    xs(k).rearrange("p (a b) -> p b a", b=32),
                    sm("inv", k).unsqueeze(1).broadcast_to([128, 32, nb]),
                    op=ALU.mult) for k in ks]]))
            S.append(mk_step("g", [
                [lambda g, k=k: g.tensor_scalar(fl("u", k), fl("u", k), 448.0,
                                                -448.0, op0=ALU.min,
                                                op1=ALU.max) for k in ks],
                [lambda g, k=k: g.tensor_tensor(fl("ff", k), fl("u", k),
                                                fl("u", k), op=ALU.mult)
                 for k in ks]]))
            S.append(mk_step("a", [
                [lambda a, k=k: a.activation(fl("ff", k), fl("ff", k), AF.Ln,
                                             bias=cst["tiny"][:, :])
                 for k in ks]]))
            S.append(mk_step("g", [
                [lambda g, k=k: g.tensor_scalar(fl("ff", k), fl("ff", k), I2L2,
                                                -0.5, op0=ALU.mult,
                                                op1=ALU.add) for k in ks]]))
            S.append(mk_step("v", [
                [lambda v, k=k: v.tensor_scalar(fl("ff", k), fl("ff", k),
                                                -6.499, MAGIC, op0=ALU.max,
                                                op1=ALU.add) for k in ks],
                [lambda v, k=k: v.tensor_single_scalar(fl("ei", k),
                                                       fl("ff", k), -MAGIC,
                                                       op=ALU.add)
                 for k in ks]]))
            S.append(mk_step("a", [
                [lambda a, k=k: a.activation(fl("ff", k), fl("ei", k), AF.Exp,
                                             scale=cst["nln2"][:, :],
                                             bias=cst["n9ln2"][:, :])
                 for k in ks]]))
            S.append(mk_step("v", [
                [lambda v, k=k: v.tensor_scalar(fl("iv", k), fl("ff", k),
                                                MAGIC, -MAGIC, op0=ALU.add,
                                                op1=ALU.add) for k in ks],
                [lambda v, k=k: v.scalar_tensor_tensor(fl("ff", k),
                                                       fl("iv", k), 2.0 ** -6,
                                                       fl("u", k),
                                                       op0=ALU.mult,
                                                       op1=ALU.mult)
                 for k in ks],
                [lambda v, k=k: v.tensor_scalar(fl("ri", k), fl("ff", k),
                                                MAGIC, -MAGIC, op0=ALU.add,
                                                op1=ALU.add) for k in ks],
                [lambda v, k=k: v.tensor_tensor(
                    fl("es", k).rearrange("p (a b) -> p b a", b=32),
                    fl("ei", k).rearrange("p (a b) -> p b a", b=32),
                    sm("ebm", k).unsqueeze(1).broadcast_to([128, 32, nb]),
                    op=ALU.add) for k in ks]]))
            S.append(mk_step("a", [
                [lambda a, k=k: a.activation(fl("es", k), fl("es", k), AF.Exp,
                                             scale=cst["ln2"][:, :])
                 for k in ks]]))
            S.append(mk_step("v", [
                [lambda v, k=k: v.tensor_tensor(
                    items[k][2][:, c0:c0 + w], fl("ri", k), fl("es", k),
                    op=ALU.mult) for k in ks]]))
            return S

        def emit_qdq(C, items, bufs):
            C.run(qdq_steps(items, bufs, 0, bufs["nfree"]))

        with ExitStack() as ph, nc.Block() as blk:
            C = CHAIN.bind(blk)

            def p1alloc(name, shape, dt=F32):
                return ph.enter_context(nc.sbuf_tensor(name, shape, dt))
            wld = [p1alloc(f"p1w{i}", [128, H]) for i in range(4)]
            dq = [p1alloc(f"p1d{i}", [128, H], BF16) for i in range(4)]
            stg = [p1alloc(f"p1s{i}", [128, H], BF16) for i in range(2)]
            qb = [qdq_bufs(f"p1q{i}", H, 2, p1alloc) for i in range(2)]
            tps = [ph.enter_context(nc.psum_tensor(f"p1tp{i}", [128, H], BF16))
                   for i in range(2)]

            tiles = []
            for e in range(E):
                for ot in range(2 * HID // 128):
                    tiles.append((w13_d[e, ot * 128:(ot + 1) * 128, :],
                                  w13t_s[e].rearrange("(ic r) o -> r ic o", r=128)
                                  [:, :, ot * 128:(ot + 1) * 128]))
                for ot in range(H // 128):
                    tiles.append((w2_d[e, ot * 128:(ot + 1) * 128, :],
                                  w2t_s[e].rearrange("(ic r) o -> r ic o", r=128)
                                  [:, :, ot * 128:(ot + 1) * 128]))

            def tile_steps(i, src_ap, dst_ap):
                tr = i % 2
                S = [("sy", lambda sy: sy.dma_start(out=wld[tr][:, :],
                                                    in_=src_ap), True)]
                S += qdq_steps([(wld[tr][:, :], None, dq[tr][:, :])], qb[tr],
                               0, H)

                def trf(t):
                    for ic in range(16):
                        ins = t.matmul(tps[tr][:, ic * 128:(ic + 1) * 128],
                                       dq[tr][:, ic * 128:(ic + 1) * 128],
                                       ident_bf[:, :], is_transpose=True,
                                       start=(ic % 8 == 0), stop=(ic % 8 == 7))
                    return ins
                S.append(("t", trf, False))
                S.append(("a", lambda a: a.copy(stg[tr][:, :], tps[tr][:, :]),
                          False))
                S.append(("a", lambda a: a.dma_start(
                    out=dst_ap,
                    in_=stg[tr].rearrange("r (ic o) -> r ic o", o=128)), True))
                return S

            tracks = [C.fork("0"), C.fork("1")]
            for base in range(0, len(tiles), 2):
                pair = tiles[base:base + 2]
                lists = [tile_steps(base + j, s, d)
                         for j, (s, d) in enumerate(pair)]

                interleave(blk, tracks, lists)
            C.prev = tracks[0].join(tracks[1]).prev

        with ExitStack() as ph, nc.Block() as blk:
            C = CHAIN.bind(blk)
            xld = ph.enter_context(nc.sbuf_tensor("p2x", [128, H], F32))
            xnf = ph.enter_context(nc.sbuf_tensor("p2xn", [128, H], F32))
            xnb = ph.enter_context(nc.sbuf_tensor("p2xnb", [128, H], BF16))
            sqd = ph.enter_context(nc.sbuf_tensor("p2sqd", [128, H], F32))
            wqbf = ph.enter_context(nc.sbuf_tensor("p2wq", [128, 16 * 1536], BF16))
            xnfm = ph.enter_context(nc.sbuf_tensor("p2fm", [128, 16 * 512], BF16))
            kst = ph.enter_context(nc.sbuf_tensor("p2k", [128, 512], BF16))
            mcol = ph.enter_context(nc.sbuf_tensor("p2mc", [128, 1], F32))
            rstd = ph.enter_context(nc.sbuf_tensor("p2rs", [128, 1], F32))
            nmr = ph.enter_context(nc.sbuf_tensor("p2nmr", [128, 1], F32))
            ssd = ph.enter_context(nc.sbuf_tensor("p2ssd", [128, 1], F32))
            tpx = ph.enter_context(nc.psum_tensor("p2tp", [128, H], BF16))
            pmm = ph.enter_context(nc.psum_tensor("p2pm", [128, 512], F32))

            C.dma("g", lambda g: g.dma_start(
                out=wqbf.rearrange("r (ic o) -> r ic o", o=1536),
                in_=wqt_d.rearrange("(ic r) o -> r ic o", r=128)))

            wq3 = wqbf.rearrange("r (ic o) -> r ic o", o=1536)
            for gi in range(16):
                for tt in range(4):
                    i = gi * 4 + tt
                    C.dma("sy", lambda sy, i=i: sy.dma_start(
                        out=xld[:, :], in_=x_d[i * 128:(i + 1) * 128, :]))

                    Ev = C.edger("v")
                    Ea = C.edger("a")

                    def lnv(v):
                        Ev(v, v.tensor_reduce(mcol[:, :], xld[:, :], axis=AX.X,
                                              op=ALU.add))
                        return v.tensor_single_scalar(
                            mcol[:, :], mcol[:, :], -1.0 / 2048.0, op=ALU.mult)
                    C.v(lnv)

                    def lna(a):
                        Ea(a, a.activation(sqd[:, :], xld[:, :], AF.Square,
                                           bias=mcol[:, :],
                                           accum_out=ssd[:, :]))
                        Ea(a, a.activation(rstd[:, :], ssd[:, :], AF.Ln,
                                           scale=cst["i2048"][:, :],
                                           bias=cst["eps"][:, :]))
                        return a.activation(rstd[:, :], rstd[:, :], AF.Exp,
                                            scale=cst["nhalf"][:, :])
                    C.a(lna)
                    C.v(lambda v: v.tensor_tensor(nmr[:, :], mcol[:, :],
                                                  rstd[:, :], op=ALU.mult))
                    if i < 8:
                        C.a(lambda a: a.activation(xnf[:, :], xld[:, :],
                                                   AF.Identity, scale=rstd[:, :],
                                                   bias=nmr[:, :]))
                        C.a(lambda a: a.copy(xnb[:, :], xnf[:, :]))
                        C.dma("g", lambda g, i=i: g.dma_start(
                            out=xno_s[i * 128:(i + 1) * 128, :], in_=xnf[:, :]))
                    else:
                        C.a(lambda a: a.activation(xnb[:, :], xld[:, :],
                                                   AF.Identity, scale=rstd[:, :],
                                                   bias=nmr[:, :]))

                    def trx(t):
                        for ic in range(16):
                            ins = t.matmul(tpx[:, ic * 128:(ic + 1) * 128],
                                           xnb[:, ic * 128:(ic + 1) * 128],
                                           ident_bf[:, :], is_transpose=True,
                                           start=(ic % 8 == 0),
                                           stop=(ic % 8 == 7))
                        return ins
                    C.t(trx)

                    def evx(a, tt=tt):
                        dst = xnfm.rearrange("r (ic c) -> r ic c", c=512)[
                            :, :, tt * 128:(tt + 1) * 128]
                        return a.copy(dst, tpx.rearrange("r (ic c) -> r ic c",
                                                         c=128))
                    C.a(evx)

                b2p, g4 = gi // 4, gi % 4
                outs = [("k", hi) for hi in range(4)]
                outs += [("v", s4) for s4 in range(4)]
                if g4 < 2:
                    outs += [("q", hi) for hi in range(4)]
                for kind, idx in outs:
                    def mmf(t, kind=kind, idx=idx):
                        if kind in ("k", "q"):
                            col0 = idx * 128 + (512 if kind == "k" else 0)
                            for ic in range(16):
                                ins = t.matmul(pmm[:, :],
                                               wq3[:, ic, col0:col0 + 128],
                                               xnfm[:, ic * 512:(ic + 1) * 512],
                                               start=(ic == 0), stop=(ic == 15))
                        else:
                            for ic in range(16):
                                ins = t.matmul(
                                    pmm[:, :],
                                    xnfm[:, ic * 512 + idx * 128:
                                         ic * 512 + (idx + 1) * 128],
                                    wq3[:, ic, 1024:1536],
                                    start=(ic == 0), stop=(ic == 15))
                        return ins
                    C.t(mmf)
                    C.v(lambda v: v.tensor_copy(kst[:, :], pmm[:, :]))

                    def wrk(a, kind=kind, idx=idx, gi=gi, b2p=b2p, g4=g4):
                        if kind == "k":
                            return a.dma_start(
                                out=kf_s[idx][:, gi * 512:(gi + 1) * 512],
                                in_=kst[:, :])
                        if kind == "q":
                            return a.dma_start(
                                out=qf_s[idx][:, b2p * 1024 + g4 * 512:
                                              b2p * 1024 + (g4 + 1) * 512],
                                in_=kst[:, :])
                        return a.dma_start(
                            out=vt_s.rearrange("h t c -> t h c")[
                                gi * 512 + idx * 128:gi * 512 + (idx + 1) * 128],
                            in_=kst.rearrange("r (h c) -> r h c", c=128))
                    C.dma("a", wrk)

        with ExitStack() as ph, nc.Block() as blk:
            C = CHAIN.bind(blk)
            ksb = ph.enter_context(nc.sbuf_tensor("p3k", [128, 2048], BF16))
            vsb = ph.enter_context(nc.sbuf_tensor("p3v", [128, 16 * 128], BF16))
            qsb = ph.enter_context(nc.sbuf_tensor("p3q", [128, 1024], BF16))
            pbf = ph.enter_context(nc.sbuf_tensor("p3p", [128, 2048], BF16))
            p2b = ph.enter_context(nc.sbuf_tensor("p3p2", [128, 2048], BF16))
            pts = ph.enter_context(nc.sbuf_tensor("p3pt", [128, 16 * 512], BF16))
            ctx = ph.enter_context(nc.sbuf_tensor("p3ctx", [128, 16 * 1024], BF16))
            nmx = ph.enter_context(nc.sbuf_tensor("p3nm", [128, 1], F32))
            ssum = ph.enter_context(nc.sbuf_tensor("p3ss", [128, 1], F32))
            rec = ph.enter_context(nc.sbuf_tensor("p3rc", [128, 1], F32))
            ostg = ph.enter_context(nc.sbuf_tensor("p3o", [128, 512], F32))
            wpbf = ph.enter_context(nc.sbuf_tensor("p3wp", [128, 16 * 2048], BF16))
            psc = ph.enter_context(nc.psum_tensor("p3sc", [128, 2048], F32))
            ppt = ph.enter_context(nc.psum_tensor("p3ppt", [128, 2048], BF16))
            pcx = ph.enter_context(nc.psum_tensor("p3cx", [128, 512], F32))

            C.dma("g", lambda g: g.dma_start(
                out=wpbf.rearrange("r (ic o) -> r ic o", o=2048),
                in_=wpt_d.rearrange("(ic r) o -> r ic o", r=128)))
            wp3 = wpbf.rearrange("r (ic o) -> r ic o", o=2048)

            for np_ in range(16):
                bqp, hi = np_ // 4, np_ % 4
                C.dma("sy", lambda sy, hi=hi, bqp=bqp: sy.dma_start(
                    out=ksb[:, :], in_=kf_s[hi][:, bqp * 2048:(bqp + 1) * 2048]))
                C.dma("sy", lambda sy, hi=hi, bqp=bqp: sy.dma_start(
                    out=vsb.rearrange("r (j c) -> r j c", c=128),
                    in_=vt_s[hi].rearrange("(j r) c -> r j c", r=128)
                    [:, bqp * 16:(bqp + 1) * 16, :]))
                C.dma("sy", lambda sy, hi=hi, bqp=bqp: sy.dma_start(
                    out=qsb[:, :], in_=qf_s[hi][:, bqp * 1024:(bqp + 1) * 1024]))

                for qt in range(8):
                    def qk(t, qt=qt):
                        for j in range(4):
                            ins = t.matmul(psc[:, j * 512:(j + 1) * 512],
                                           qsb[:, qt * 128:(qt + 1) * 128],
                                           ksb[:, j * 512:(j + 1) * 512],
                                           start=True, stop=True)
                        return ins
                    C.t(qk)

                    def nmax(v):
                        C.edger("v")(v, v.tensor_reduce(
                            nmx[:, :], psc[:, :], axis=AX.X,
                            op=ALU.max, negate=True))
                        return v.tensor_single_scalar(
                            nmx[:, :], nmx[:, :], 1.0 / math.sqrt(HN),
                            op=ALU.mult)
                    C.v(nmax)
                    C.a(lambda a: a.activation(pbf[:, :], psc[:, :], AF.Exp,
                                               scale=cst["anorm"][:, :],
                                               bias=nmx[:, :],
                                               accum_out=ssum[:, :]))
                    C.v(lambda v: v.reciprocal(rec[:, :], ssum[:, :]))
                    C.v(lambda v: v.tensor_single_scalar(p2b[:, :], pbf[:, :],
                                                         rec[:, :], op=ALU.mult))

                    def trp(t):
                        for j in range(16):
                            ins = t.matmul(ppt[:, j * 128:(j + 1) * 128],
                                           p2b[:, j * 128:(j + 1) * 128],
                                           ident_bf[:, :], is_transpose=True,
                                           start=(j % 8 == 0), stop=(j % 8 == 7))
                        return ins
                    C.t(trp)

                    def evp(v, qt=qt):
                        dst = pts.rearrange("r (j c) -> r j c", c=512)[
                            :, :, (qt % 4) * 128:(qt % 4 + 1) * 128]
                        return v.tensor_copy(dst, ppt.rearrange(
                            "r (j c) -> r j c", c=128))
                    C.v(evp)

                    if qt % 4 == 3:
                        def av(t):
                            for j in range(16):
                                ins = t.matmul(pcx[:, :],
                                               vsb[:, j * 128:(j + 1) * 128],
                                               pts[:, j * 512:(j + 1) * 512],
                                               start=(j == 0), stop=(j == 15))
                            return ins
                        C.t(av)

                        def evc(a, qt=qt, np_=np_):
                            dst = ctx[:, np_ * 1024 + (qt // 4) * 512:
                                      np_ * 1024 + (qt // 4 + 1) * 512]
                            return a.copy(dst, pcx[:, :])
                        C.a(evc)

            for qt in range(8):
                for dc in range(4):
                    def wpj(t, qt=qt, dc=dc):
                        for n2 in range(16):
                            t.matmul(pcx[:, :],
                                     ctx[:, n2 * 1024 + qt * 128:
                                         n2 * 1024 + (qt + 1) * 128],
                                     wp3[:, n2, dc * 512:(dc + 1) * 512],
                                     start=(n2 == 0), stop=False)
                        return t.matmul(pcx[:, :], ones1[0:1, 0:128],
                                        bprj[0:1, dc * 512:(dc + 1) * 512],
                                        start=False, stop=True)
                    C.t(wpj)
                    C.a(lambda a: a.copy(ostg[:, :], pcx[:, :]))
                    C.dma("g", lambda g, qt=qt, dc=dc: g.dma_start(
                        out=out_d[qt * 128:(qt + 1) * 128,
                                  dc * 512:(dc + 1) * 512],
                        in_=ostg[:, :]))

        ph03.close()
        mker = sb("mker", [128, 8 * E], F32)
        ph45 = st.enter_context(ExitStack())
        xnbfm = ph45.enter_context(nc.sbuf_tensor(
            "xnbfm", [128, 8 * H], BF16))
        with ExitStack() as ph, nc.Block() as blk:
            C = CHAIN.bind(blk)
            xot = ph.enter_context(nc.sbuf_tensor("p4x", [128, H], F32))
            gwsb = ph.enter_context(nc.sbuf_tensor("p4gw", [128, 16 * E], F32))
            xn32 = ph.enter_context(nc.sbuf_tensor("p4fm", [128, 2048], F32))
            sc8 = ph.enter_context(nc.sbuf_tensor("p4sc", [128, E], F32))
            bb8 = ph.enter_context(nc.sbuf_tensor("p4bb", [128, E], F32))
            t8 = ph.enter_context(nc.sbuf_tensor("p4t8", [128, E], F32))
            m1 = ph.enter_context(nc.sbuf_tensor("p4m1", [128, E], F32))
            m2 = ph.enter_context(nc.sbuf_tensor("p4m2", [128, E], F32))
            mx1 = ph.enter_context(nc.sbuf_tensor("p4mx", [128, 1], F32))
            sv1 = ph.enter_context(nc.sbuf_tensor("p4sv1", [128, 1], F32))
            sv2 = ph.enter_context(nc.sbuf_tensor("p4sv2", [128, 1], F32))
            den = ph.enter_context(nc.sbuf_tensor("p4den", [128, 1], F32))
            tpf = ph.enter_context(nc.psum_tensor("p4tp", [128, 2048], F32))
            pg = ph.enter_context(nc.psum_tensor("p4pg", [128, E], F32))

            C.dma("sy", lambda sy: sy.dma_start(
                out=gwsb.rearrange("r (ic e) -> r ic e", e=E),
                in_=gwt_d.rearrange("(ic r) e -> r ic e", r=128)))
            gw3 = gwsb.rearrange("r (ic e) -> r ic e", e=E)

            for tt in range(8):
                C.dma("sy", lambda sy, tt=tt: sy.dma_start(
                    out=xot[:, :], in_=xno_s[tt * 128:(tt + 1) * 128, :]))

                def trf(t):
                    for ic in range(16):
                        ins = t.matmul(tpf[:, ic * 128:(ic + 1) * 128],
                                       xot[:, ic * 128:(ic + 1) * 128],
                                       ident_f[:, :], is_transpose=True,
                                       start=(ic % 4 == 0), stop=(ic % 4 == 3))
                    return ins
                C.t(trf)

                def evf(a, tt=tt):
                    a.copy(xn32[:, :], tpf[:, :])
                    return a.copy(xnbfm[:, tt * H:(tt + 1) * H], tpf[:, :])
                C.a(evf)

                def gmm(t):
                    for ic in range(16):
                        ins = t.matmul(pg[:, :], xn32[:, ic * 128:(ic + 1) * 128],
                                       gw3[:, ic, :],
                                       start=(ic == 0), stop=(ic == 15))
                    return ins
                C.t(gmm)
                C.a(lambda a: a.activation(sc8[:, :], pg[:, :], AF.Sigmoid))

                def top2(v, tt=tt):
                    BIG = 1.0e4
                    Ev = C.edger("v")
                    Mt = mker[:, tt * E:(tt + 1) * E]
                    Ev(v, v.tensor_tensor(bb8[:, :], sc8[:, :], ebias[:, :],
                                          op=ALU.add))
                    Ev(v, v.tensor_reduce(mx1[:, :], bb8[:, :], axis=AX.X,
                                          op=ALU.max))
                    Ev(v, v.tensor_single_scalar(m1[:, :], bb8[:, :], mx1[:, :],
                                                 op=ALU.is_ge))
                    Ev(v, v.scalar_tensor_tensor(t8[:, :], m1[:, :], -BIG,
                                                 iota_b[:, :], op0=ALU.mult,
                                                 op1=ALU.add))
                    Ev(v, v.tensor_reduce(mx1[:, :], t8[:, :], axis=AX.X,
                                          op=ALU.min))
                    Ev(v, v.tensor_single_scalar(m1[:, :], iota_e[:, :],
                                                 mx1[:, :], op=ALU.is_equal))
                    Ev(v, v.tensor_tensor(t8[:, :], m1[:, :], sc8[:, :],
                                          op=ALU.mult))
                    Ev(v, v.tensor_reduce(sv1[:, :], t8[:, :], axis=AX.X,
                                          op=ALU.add))
                    Ev(v, v.scalar_tensor_tensor(t8[:, :], m1[:, :], -BIG,
                                                 bb8[:, :], op0=ALU.mult,
                                                 op1=ALU.add))
                    Ev(v, v.tensor_reduce(mx1[:, :], t8[:, :], axis=AX.X,
                                          op=ALU.max))
                    Ev(v, v.tensor_single_scalar(m2[:, :], t8[:, :], mx1[:, :],
                                                 op=ALU.is_ge))
                    Ev(v, v.scalar_tensor_tensor(t8[:, :], m2[:, :], -BIG,
                                                 iota_b[:, :], op0=ALU.mult,
                                                 op1=ALU.add))
                    Ev(v, v.tensor_reduce(mx1[:, :], t8[:, :], axis=AX.X,
                                          op=ALU.min))
                    Ev(v, v.tensor_single_scalar(m2[:, :], iota_e[:, :],
                                                 mx1[:, :], op=ALU.is_equal))
                    Ev(v, v.tensor_tensor(t8[:, :], m2[:, :], sc8[:, :],
                                          op=ALU.mult))
                    Ev(v, v.tensor_reduce(sv2[:, :], t8[:, :], axis=AX.X,
                                          op=ALU.add))
                    Ev(v, v.tensor_tensor(den[:, :], sv1[:, :], sv2[:, :],
                                          op=ALU.add))
                    Ev(v, v.tensor_single_scalar(den[:, :], den[:, :], 1e-20,
                                                 op=ALU.add))
                    Ev(v, v.reciprocal(den[:, :], den[:, :]))
                    Ev(v, v.tensor_single_scalar(sv1[:, :], sv1[:, :],
                                                 den[:, :], op=ALU.mult))
                    Ev(v, v.tensor_single_scalar(sv2[:, :], sv2[:, :],
                                                 den[:, :], op=ALU.mult))
                    Ev(v, v.tensor_single_scalar(m1[:, :], m1[:, :], sv1[:, :],
                                                 op=ALU.mult))
                    Ev(v, v.tensor_single_scalar(m2[:, :], m2[:, :], sv2[:, :],
                                                 op=ALU.mult))
                    return v.tensor_tensor(Mt, m1[:, :], m2[:, :], op=ALU.add)
                C.v(top2)

            if debug:
                C.dma("a", lambda a: a.dma_start(out=dbg["d_m"], in_=mker[:, :]))
                C.dma("a", lambda a: a.dma_start(out=dbg["d_sc"], in_=sc8[:, :]))

        with ExitStack() as ph, nc.Block() as blk:
            C = CHAIN.bind(blk)
            w1c = ph.enter_context(nc.sbuf_tensor("p5w1", [128, 16 * 512], BF16))
            w3c = ph.enter_context(nc.sbuf_tensor("p5w3", [128, 16 * 512], BF16))
            w2c = ph.enter_context(nc.sbuf_tensor("p5w2", [128, 16 * 512], BF16))
            s1f = ph.enter_context(nc.sbuf_tensor("p5s1", [128, 512], F32))
            h3f = ph.enter_context(nc.sbuf_tensor("p5h3", [128, 512], F32))
            hhb = ph.enter_context(nc.sbuf_tensor("p5hh", [128, 512], BF16))
            hhf = ph.enter_context(nc.sbuf_tensor("p5hf", [128, 8 * 2048], BF16))
            ostg = ph.enter_context(nc.sbuf_tensor("p5o", [128, 512], F32))
            ph1 = ph.enter_context(nc.psum_tensor("p5p1", [128, 512], F32))
            ph3 = ph.enter_context(nc.psum_tensor("p5p3", [128, 512], F32))
            ptp = ph.enter_context(nc.psum_tensor("p5tp", [128, 512], BF16))
            pyo = ph.enter_context(nc.psum_tensor("p5py", [128, 512], F32))

            for hc in range(4):
                C.dma("g", lambda sy, hc=hc: sy.dma_start(
                    out=w1c.rearrange("r (ic o) -> r ic o", o=512),
                    in_=sw1t_d.rearrange("(ic r) o -> r ic o", r=128)
                    [:, :, hc * 512:(hc + 1) * 512]))
                C.dma("g", lambda sy, hc=hc: sy.dma_start(
                    out=w3c.rearrange("r (ic o) -> r ic o", o=512),
                    in_=sw3t_d.rearrange("(ic r) o -> r ic o", r=128)
                    [:, :, hc * 512:(hc + 1) * 512]))
                for tt in range(8):
                    def mm13(t, tt=tt):
                        for wch, pp in ((w1c, ph1), (w3c, ph3)):
                            w3_ = wch.rearrange("r (ic o) -> r ic o", o=512)
                            for ic in range(16):
                                ins = t.matmul(
                                    pp[:, :],
                                    xnbfm[:, tt * H + ic * 128:
                                          tt * H + (ic + 1) * 128],
                                    w3_[:, ic, :],
                                    start=(ic == 0), stop=(ic == 15))
                        return ins
                    C.t(mm13)
                    C.a(lambda a: a.activation(s1f[:, :], ph1[:, :], AF.Silu))
                    C.v(lambda v: v.tensor_copy(h3f[:, :], ph3[:, :]))
                    C.g(lambda g: g.tensor_tensor(hhb[:, :], s1f[:, :],
                                                  h3f[:, :], op=ALU.mult))

                    def trh(t):
                        for j in range(4):
                            ins = t.matmul(ptp[:, j * 128:(j + 1) * 128],
                                           hhb[:, j * 128:(j + 1) * 128],
                                           ident_bf[:, :], is_transpose=True,
                                           start=(j == 0), stop=(j == 3))
                        return ins
                    C.t(trh)
                    C.v(lambda v, hc=hc, tt=tt: v.tensor_copy(
                        hhf[:, tt * 2048 + hc * 512:tt * 2048 + (hc + 1) * 512],
                        ptp[:, :]))

            for dc in range(4):
                C.dma("g", lambda sy, dc=dc: sy.dma_start(
                    out=w2c.rearrange("r (ic o) -> r ic o", o=512),
                    in_=sw2t_d.rearrange("(ic r) o -> r ic o", r=128)
                    [:, :, dc * 512:(dc + 1) * 512]))
                for tt in range(8):
                    def mm2(t, tt=tt):
                        w3_ = w2c.rearrange("r (ic o) -> r ic o", o=512)
                        for j in range(16):
                            ins = t.matmul(
                                pyo[:, :],
                                hhf[:, tt * 2048 + j * 128:
                                    tt * 2048 + (j + 1) * 128],
                                w3_[:, j, :],
                                start=(j == 0), stop=(j == 15))
                        return ins
                    C.t(mm2)
                    C.a(lambda a: a.copy(ostg[:, :], pyo[:, :]))
                    C.dma("g", lambda g, dc=dc, tt=tt: g.dma_start(
                        out=out_d[tt * 128:(tt + 1) * 128,
                                  dc * 512:(dc + 1) * 512],
                        in_=ostg[:, :], accum_op=ALU.add))

        ph45.close()

        with ExitStack() as ph, nc.Block() as blk:
            C = CHAIN.bind(blk)

            def pb(name, shape, dt=F32):
                return ph.enter_context(nc.sbuf_tensor(name, shape, dt))
            xot = [pb(f"p6x{i}", [128, H]) for i in range(2)]
            qbx = [qdq_bufs(f"p6q{i}", H, 1, pb) for i in range(2)]
            qbh = []
            for i in range(2):
                d = dict(qbx[i])
                d["nfree"], d["nb"] = 512, 16
                qbh.append(d)
            xqb = [pb(f"p6xq{i}", [128, H], BF16) for i in range(2)]
            xqf = pb("p6xqf", [128, 8 * H], BF16)
            w13c = [pb(f"p6w13{i}", [128, 16 * 512], BF16) for i in range(2)]

            s1f = [pb(f"p6s1{i}", [128, 512]) for i in range(2)]
            h3f = [pb(f"p6h3{i}", [128, 512]) for i in range(2)]
            hf = [pb(f"p6h{i}", [128, 512]) for i in range(2)]
            hqb = [pb(f"p6hq{i}", [128, 512], BF16) for i in range(2)]
            hqf = pb("p6hqf", [128, 8 * H], BF16)
            ostg = s1f

            for e in range(E):
                with ExitStack() as pp:
                    tpb = [pp.enter_context(nc.psum_tensor(
                        f"p6tp{e}_{i}", [128, H], BF16)) for i in range(2)]
                    tracks = [C.fork("0"), C.fork("1")]

                    def xsteps(tt, e=e):
                        tr = tt % 2
                        S = [("sy", lambda sy, tt=tt: sy.dma_start(
                            out=xot[tr][:, :],
                            in_=xno_s[tt * 128:(tt + 1) * 128, :]), True)]
                        S += qdq_steps([(xot[tr][:, :],
                                         mker[:, tt * E + e:tt * E + e + 1],
                                         xqb[tr][:, :])], qbx[tr], 0, H)

                        def trf(t, tr=tr):
                            for ic in range(16):
                                ins = t.matmul(
                                    tpb[tr][:, ic * 128:(ic + 1) * 128],
                                    xqb[tr][:, ic * 128:(ic + 1) * 128],
                                    ident_bf[:, :], is_transpose=True,
                                    start=(ic % 8 == 0), stop=(ic % 8 == 7))
                            return ins
                        S.append(("t", trf, False))
                        S.append(("a", lambda a, tt=tt, tr=tr: a.copy(
                            xqf[:, tt * H:(tt + 1) * H], tpb[tr][:, :]), False))
                        return S

                    for q in range(4):
                        lists = [xsteps(2 * q), xsteps(2 * q + 1)]
                        interleave(blk, tracks, lists, stagger=[0, 9])
                    C.prev = tracks[0].join(tracks[1]).prev

                with ExitStack() as pp:
                    ph1 = [[pp.enter_context(nc.psum_tensor(
                        f"p6p1{e}_{i}_{w}", [128, 512], F32))
                        for w in range(2)] for i in range(2)]
                    ptp2 = [pp.enter_context(nc.psum_tensor(
                        f"p6t2{e}_{i}", [128, 512], BF16)) for i in range(2)]
                    for oc in range(4):
                        C.dma("sy", lambda sy, e=e, oc=oc: sy.dma_start(
                            out=w13c[0].rearrange("r (ic o) -> r ic o", o=512),
                            in_=w13t_s[e].rearrange("(ic r) o -> r ic o", r=128)
                            [:, :, oc * 512:(oc + 1) * 512]))
                        C.dma("sy", lambda sy, e=e, oc=oc: sy.dma_start(
                            out=w13c[1].rearrange("r (ic o) -> r ic o", o=512),
                            in_=w13t_s[e].rearrange("(ic r) o -> r ic o", r=128)
                            [:, :, 2048 + oc * 512:2048 + (oc + 1) * 512]))
                        tracks = [C.fork("0"), C.fork("1")]

                        def hsteps(tt, oc=oc):
                            tr = tt % 2
                            S = []

                            def mm1(t, tt=tt, tr=tr):
                                for w in range(2):
                                    w3_ = w13c[w].rearrange(
                                        "r (ic o) -> r ic o", o=512)
                                    for ic in range(16):
                                        ins = t.matmul(
                                            ph1[tr][w][:, :],
                                            xqf[:, tt * H + ic * 128:
                                                tt * H + (ic + 1) * 128],
                                            w3_[:, ic, :],
                                            start=(ic == 0), stop=(ic == 15))
                                return ins
                            S.append(("t", mm1, False))
                            S.append(("a", lambda a, tr=tr: a.activation(
                                s1f[tr][:, :], ph1[tr][0][:, :], AF.Silu),
                                False))
                            S.append(("v", lambda v, tr=tr: v.tensor_copy(
                                h3f[tr][:, :], ph1[tr][1][:, :]), False))
                            S.append(("g", lambda g, tr=tr: g.tensor_tensor(
                                hf[tr][:, :], s1f[tr][:, :], h3f[tr][:, :],
                                op=ALU.mult), False))
                            S += qdq_steps([(hf[tr][:, :], None,
                                             hqb[tr][:, :])], qbh[tr], 0, 512)

                            def trh(t, tr=tr):
                                for j in range(4):
                                    ins = t.matmul(
                                        ptp2[tr][:, j * 128:(j + 1) * 128],
                                        hqb[tr][:, j * 128:(j + 1) * 128],
                                        ident_bf[:, :], is_transpose=True,
                                        start=(j == 0), stop=(j == 3))
                                return ins
                            S.append(("t", trh, False))
                            S.append(("v", lambda v, tt=tt, oc=oc, tr=tr:
                                      v.tensor_copy(
                                          hqf[:, tt * H + oc * 512:
                                              tt * H + (oc + 1) * 512],
                                          ptp2[tr][:, :]), False))
                            return S

                        for q in range(4):
                            lists = [hsteps(2 * q), hsteps(2 * q + 1)]
                            interleave(blk, tracks, lists, stagger=[0, 11])
                        C.prev = tracks[0].join(tracks[1]).prev

                with ExitStack() as pp:
                    pyo = [pp.enter_context(nc.psum_tensor(
                        f"p6py{e}_{i}", [128, 512], F32)) for i in range(2)]
                    for dc in range(4):
                        w2cb = w13c[0]
                        C.dma("sy", lambda sy, e=e, dc=dc: sy.dma_start(
                            out=w2cb.rearrange("r (ic o) -> r ic o", o=512),
                            in_=w2t_s[e].rearrange("(ic r) o -> r ic o", r=128)
                            [:, :, dc * 512:(dc + 1) * 512]))
                        tracks = [C.fork("0"), C.fork("1")]

                        def ysteps(tt, dc=dc):
                            tr = tt % 2
                            S = []

                            def mm2(t, tt=tt, tr=tr):
                                w3_ = w2cb.rearrange("r (ic o) -> r ic o", o=512)
                                for j in range(16):
                                    ins = t.matmul(
                                        pyo[tr][:, :],
                                        hqf[:, tt * H + j * 128:
                                            tt * H + (j + 1) * 128],
                                        w3_[:, j, :],
                                        start=(j == 0), stop=(j == 15))
                                return ins
                            S.append(("t", mm2, False))
                            S.append(("a", lambda a, tr=tr: a.copy(
                                ostg[tr][:, :], pyo[tr][:, :]), False))
                            S.append(("g", lambda g, tt=tt, dc=dc, tr=tr:
                                      g.dma_start(
                                          out=out_d[tt * 128:(tt + 1) * 128,
                                                    dc * 512:(dc + 1) * 512],
                                          in_=ostg[tr][:, :],
                                          accum_op=ALU.add), True))
                            return S

                        for q in range(4):
                            lists = [ysteps(2 * q), ysteps(2 * q + 1)]
                            interleave(blk, tracks, lists, stagger=[0, 2])
                        C.prev = tracks[0].join(tracks[1]).prev

        with ExitStack() as ph, nc.Block() as blk:
            C = CHAIN.bind(blk)
            xrt = ph.enter_context(nc.sbuf_tensor("p7x", [128, H], F32))
            for tt in range(8):
                C.dma("sy", lambda sy, tt=tt: sy.dma_start(
                    out=xrt[:, :], in_=x_d[tt * 128:(tt + 1) * 128, :]))
                C.dma("g", lambda g, tt=tt: g.dma_start(
                    out=out_d[tt * 128:(tt + 1) * 128, :], in_=xrt[:, :],
                    accum_op=ALU.add))
            if debug:
                C.dma("a", lambda a: a.dma_start(out=dbg["d_xno"], in_=xno_s))
                C.dma("a", lambda a: a.dma_start(out=dbg["d_kf"], in_=kf_s[0]))
                C.dma("a", lambda a: a.dma_start(out=dbg["d_qf"], in_=qf_s[0]))
                C.dma("a", lambda a: a.dma_start(out=dbg["d_vt"], in_=vt_s[0]))
                C.dma("a", lambda a: a.dma_start(out=dbg["d_w13t"],
                                                 in_=w13t_s[0]))
                C.dma("a", lambda a: a.dma_start(out=dbg["d_w2t"],
                                                 in_=w2t_s[0]))
            prev = list(C.prev)

            def fin(g):
                for ps, pv in prev:
                    g.wait_ge(ps.h, pv)
            blk.gpsimd(fin)

    return nc


_NC_CACHE = {}


def kernel(L0_attn_query, L0_attn_key, L0_attn_value, L0_attn_Wqkv,
           L0_attn_Wproj, L0_attn_bproj, L0_ffn_gate_w, L0_ffn_w13, L0_ffn_w2,
           L0_ffn_sw1, L0_ffn_sw2, L0_ffn_sw3, L0_ffn_expert_bias,
           _debug=False):
    x = np.ascontiguousarray(np.asarray(L0_attn_query, dtype=np.float32))
    wqkv = np.asarray(L0_attn_Wqkv, dtype=np.float32)
    wproj = np.asarray(L0_attn_Wproj, dtype=np.float32)
    bproj = np.asarray(L0_attn_bproj, dtype=np.float32).reshape(1, H)
    gate = np.asarray(L0_ffn_gate_w, dtype=np.float32)
    w13 = np.ascontiguousarray(np.asarray(L0_ffn_w13, dtype=np.float32))
    w2 = np.ascontiguousarray(np.asarray(L0_ffn_w2, dtype=np.float32))
    sw1 = np.asarray(L0_ffn_sw1, dtype=np.float32)
    sw2 = np.asarray(L0_ffn_sw2, dtype=np.float32)
    sw3 = np.asarray(L0_ffn_sw3, dtype=np.float32)
    eb = np.asarray(L0_ffn_expert_bias, dtype=np.float32)

    wqkv_t = np.ascontiguousarray(wqkv.T)
    wproj_t = np.ascontiguousarray(wproj.T)
    gate_t = np.ascontiguousarray(gate.T)
    sw1_t = np.ascontiguousarray(sw1.T)
    sw3_t = np.ascontiguousarray(sw3.T)
    sw2_t = np.ascontiguousarray(sw2.T)
    ebias_r = np.ascontiguousarray(np.broadcast_to(eb[None, :], (128, E)))

    in_maps = []
    for c in range(NCORE):
        b2, half = c // 2, c % 2
        bidx = (np.arange(B) + b2) % B
        sidx = (np.arange(S) + half * 1024) % S
        x_c = x[bidx][:, sidx, :].reshape(T, H)
        heads = [b2 + 4 * hi for hi in range(4)]
        cols = []
        for part in range(3):
            for h in heads:
                cols.extend(range(h * 384 + part * 128,
                                  h * 384 + part * 128 + 128))
        wqt_c = np.ascontiguousarray(wqkv_t[:, cols])
        rows = []
        for npb in range(16):
            bqp, hi = npb // 4, npb % 4
            n = 4 * ((bqp + b2) % 4) + hi
            rows.extend(range(n * 128, (n + 1) * 128))
        wpt_c = np.ascontiguousarray(wproj_t[rows, :])
        in_maps.append({
            "x": np.ascontiguousarray(x_c), "wqt": wqt_c, "wpt": wpt_c,
            "bproj": np.ascontiguousarray(bproj), "gwt": gate_t,
            "ebias": ebias_r, "sw1t": sw1_t, "sw3t": sw3_t, "sw2t": sw2_t,
            "w13": w13, "w2": w2,
        })

    key = "dbg" if _debug else "nc"
    if key not in _NC_CACHE:
        _NC_CACHE[key] = build_program(debug=_debug)
    nc = _NC_CACHE[key]
    res = run_bass_kernel_spmd(nc, in_maps, core_ids=list(range(NCORE)))

    out = np.empty((B, S, H), dtype=np.float32)
    for c in range(NCORE):
        b2, half = c // 2, c % 2
        out[b2, half * 1024:(half + 1) * 1024, :] = np.asarray(
            res.results[c]["out"])
    if _debug:
        return out, res
    return out


# version 16
# speedup vs baseline: 1.1208x; 1.1208x over previous
import math
import numpy as np
from contextlib import ExitStack

import concourse.bass as bass
import concourse.mybir as mybir
from concourse.bass_utils import run_bass_kernel_spmd

F32 = mybir.dt.float32
BF16 = mybir.dt.bfloat16
I32 = mybir.dt.int32
ALU = mybir.AluOpType
AF = mybir.ActivationFunctionType
AX = mybir.AxisListType

LN2 = math.log(2.0)
B, S, H = 4, 2048, 2048
NH, HN = 16, 128
E, HID = 8, 2048
T = B * S
TOWN = 1024
NCORE = 8
I2L2 = 1.0 / (2.0 * LN2)
MAGIC = 12582912.0


class Sem:
    def __init__(self, st, nc, name):
        self.h = st.enter_context(nc.semaphore(name))
        self.n = 0


class Chain:

    def __init__(self, sems, sfx="j"):
        self.blk = None
        self.s = sems
        self.sfx = sfx
        self.prev = []

    def bind(self, blk):
        self.blk = blk
        return self

    def fork(self, sfx=None):
        c = Chain(self.s, sfx if sfx is not None else self.sfx)
        c.blk = self.blk
        c.prev = list(self.prev)
        return c

    def join(self, other):
        seen = {}
        for s, v in self.prev + other.prev:
            seen[id(s)] = (s, max(v, seen.get(id(s), (s, -1))[1]))
        self.prev = [sv for sv in seen.values()]
        return self

    def _wrap(self, kind, fn, dma=False):
        sem = self.s[kind + "d" + self.sfx] if dma else self.s[kind]
        prev = list(self.prev)

        def body(eng):
            for ps, pv in prev:
                eng.wait_ge(ps.h, pv)
            ins = fn(eng)
            assert ins is not None, "chain step must return its last instruction"
            ins.then_inc(sem.h, 16 if dma else 1)
            sem.n += 16 if dma else 1
        getattr(self.blk, {"v": "vector", "a": "scalar", "g": "gpsimd",
                           "t": "tensor", "sy": "sync"}[kind])(body)
        self.prev = [(sem, sem.n)]

    def v(self, fn):
        self._wrap("v", fn)

    def a(self, fn):
        self._wrap("a", fn)

    def g(self, fn):
        self._wrap("g", fn)

    def t(self, fn):
        self._wrap("t", fn)

    def dma(self, kind, fn):
        self._wrap(kind, fn, dma=True)

    def edger(self, kind):
        sem = self.s[kind + "e"]

        def E(eng, ins):
            ins.then_inc(sem.h, 1)
            sem.n += 1
            eng.wait_ge(sem.h, sem.n)
        return E

    def sched(self, kind, fn, dma=False):
        sem = self.s[kind + "d" + self.sfx] if dma else self.s[kind]
        entry = (kind, fn, dma, list(self.prev), sem, 16 if dma else 1)
        sem.n += entry[5]
        self.prev = [(sem, sem.n)]
        return entry

    def run(self, steps):
        for kind, fn, dma in steps:
            self._wrap(kind, fn, dma=dma)


KINDMAP = {"v": "vector", "a": "scalar", "g": "gpsimd", "t": "tensor",
           "sy": "sync"}


def emit_plan(blk, plan):
    kinds = []
    for e in plan:
        if e[0] not in kinds:
            kinds.append(e[0])
    for k in kinds:
        group = [e for e in plan if e[0] == k]

        def body(eng, group=group):
            for kind, fn, dma, prevs, sem, inc in group:
                for ps, pv in prevs:
                    eng.wait_ge(ps.h, pv)
                ins = fn(eng)
                assert ins is not None
                ins.then_inc(sem.h, inc)
        getattr(blk, KINDMAP[k])(body)


def interleave(blk, chains, steplists):
    plan = []
    n = max(len(s) for s in steplists)
    for i in range(n):
        for c, steps in zip(chains, steplists):
            if i < len(steps) and steps[i] is not None:
                kind, fn, dma = steps[i]
                plan.append(c.sched(kind, fn, dma=dma))
    emit_plan(blk, plan)


def build_program(debug=False):
    nc = bass.Bass("TRN2", target_bir_lowering=False, debug=False)

    x_d = nc.dram_tensor("x", [T, H], F32, kind="ExternalInput").ap()
    wqt_d = nc.dram_tensor("wqt", [H, 1536], F32, kind="ExternalInput").ap()
    wpt_d = nc.dram_tensor("wpt", [H, H], F32, kind="ExternalInput").ap()
    bproj_d = nc.dram_tensor("bproj", [1, H], F32, kind="ExternalInput").ap()
    gwt_d = nc.dram_tensor("gwt", [H, E], F32, kind="ExternalInput").ap()
    ebias_d = nc.dram_tensor("ebias", [128, E], F32, kind="ExternalInput").ap()
    sw1t_d = nc.dram_tensor("sw1t", [H, HID], F32, kind="ExternalInput").ap()
    sw3t_d = nc.dram_tensor("sw3t", [H, HID], F32, kind="ExternalInput").ap()
    sw2t_d = nc.dram_tensor("sw2t", [HID, H], F32, kind="ExternalInput").ap()
    w13_d = nc.dram_tensor("w13", [E, 2 * HID, H], F32, kind="ExternalInput").ap()
    w2_d = nc.dram_tensor("w2", [E, H, HID], F32, kind="ExternalInput").ap()
    out_d = nc.dram_tensor("out", [TOWN, H], F32, kind="ExternalOutput").ap()

    w13t_s = nc.dram_tensor("s_w13t", [E, H, 2 * HID], BF16, kind="Internal").ap()
    w2t_s = nc.dram_tensor("s_w2t", [E, HID, H], BF16, kind="Internal").ap()
    kf_s = nc.dram_tensor("s_kf", [4, HN, T], BF16, kind="Internal").ap()
    vt_s = nc.dram_tensor("s_vt", [4, T, HN], BF16, kind="Internal").ap()
    qf_s = nc.dram_tensor("s_qf", [4, HN, 4096], BF16, kind="Internal").ap()
    xno_s = nc.dram_tensor("s_xno", [TOWN, H], F32, kind="Internal").ap()
    dbg = {}
    if debug:
        for nm, shp, dt in [("d_xno", [TOWN, H], F32),
                            ("d_m", [128, 8 * E], F32),
                            ("d_sc", [128, E], F32),
                            ("d_kf", [HN, T], BF16),
                            ("d_qf", [HN, 4096], BF16),
                            ("d_vt", [T, HN], BF16),
                            ("d_w13t", [H, 2 * HID], BF16),
                            ("d_w2t", [HID, H], BF16)]:
            dbg[nm] = nc.dram_tensor(nm, shp, dt, kind="ExternalOutput").ap()

    st = ExitStack()
    with st:
        def sb(name, shape, dt=F32):
            return st.enter_context(nc.sbuf_tensor(name, shape, dt))

        sems = {k: Sem(st, nc, "s_" + k)
                for k in ["v", "a", "g", "t", "sy", "ve", "ae", "ge"]}
        for q in ["vd", "ad", "gd", "syd"]:
            for sfx in ["j", "0", "1"]:
                sems[q + sfx] = Sem(st, nc, "s_" + q + sfx)
        CHAIN = Chain(sems)

        ph03 = st.enter_context(ExitStack())
        ident_bf = sb("ident_bf", [128, 128], BF16)
        ident_f = sb("ident_f", [128, 128], F32)
        iota_t = sb("iota_t", [128, 128], F32)
        iota_e = sb("iota_e", [128, E], F32)
        iota_b = sb("iota_b", [128, E], F32)
        ebias = sb("ebias_t", [128, E], F32)
        cst = {}
        for nm, val in [("ln2", LN2), ("nln2", -LN2), ("n9ln2", 9 * LN2),
                        ("i2048", 1.0 / 2048.0), ("eps", 1e-5),
                        ("nhalf", -0.5), ("anorm", 1.0 / math.sqrt(HN)),
                        ("i2l2", I2L2), ("tiny", 2.0 ** -100)]:
            cst[nm] = sb("c_" + nm, [128, 1], F32)
        ones1 = ph03.enter_context(nc.sbuf_tensor("ones1", [1, 128], F32))
        bprj = ph03.enter_context(nc.sbuf_tensor("bprj", [1, H], F32))

        LNEXP_SET = 6

        def preload_lnexp(a):
            return a.add_instruction(mybir.InstLoadActFuncSet(
                name=nc.get_next_instruction_name(),
                act_func_set_id=LNEXP_SET, ins=[], outs=[]))

        with nc.Block() as blk:
            C = CHAIN.bind(blk)
            C.g(lambda g: (g.iota(iota_t[:, :], [[1, 128]], channel_multiplier=-1,
                                  allow_small_or_imprecise_dtypes=True),
                           g.iota(iota_e[:, :], [[1, E]], channel_multiplier=0,
                                  allow_small_or_imprecise_dtypes=True))[-1])

            Ev0 = C.edger("v")

            def p0v(v):
                for nm, val in [("ln2", LN2), ("nln2", -LN2), ("n9ln2", 9 * LN2),
                                ("i2048", 1.0 / 2048.0), ("eps", 1e-5),
                                ("nhalf", -0.5), ("anorm", 1.0 / math.sqrt(HN)),
                                ("i2l2", I2L2), ("tiny", 2.0 ** -100)]:
                    v.memset(cst[nm][:, :], val)
                v.memset(ones1[:, :], 1.0)
                Ev0(v, v.tensor_single_scalar(ident_f[:, :], iota_t[:, :], 0.0,
                                              op=ALU.is_equal))
                v.tensor_single_scalar(iota_b[:, :], iota_e[:, :], 1.0e4,
                                       op=ALU.add)
                return v.tensor_copy(ident_bf[:, :], ident_f[:, :])
            C.v(p0v)
            C.dma("sy", lambda sy: sy.dma_start(out=bprj[:, :], in_=bproj_d))
            C.dma("sy", lambda sy: sy.dma_start(out=ebias[:, :], in_=ebias_d))



        def qdq_bufs(prefix, nfree, nbatch, alloc):
            nb = nfree // 32
            d = {"nfree": nfree, "nb": nb, "nbatch": nbatch}
            for nm in ["amx", "e0i", "bex", "aex", "Bi", "Ai", "tc", "cl",
                       "ebf", "jf", "inv"]:
                dt = I32 if nm in ("e0i", "Bi", "Ai") else F32
                d[nm] = alloc(prefix + nm, [128, nb * nbatch], dt)
            d["S1i"] = d["Bi"]
            d["S2i"] = d["Ai"]
            d["j1"] = d["bex"]
            d["j2"] = d["aex"]
            d["ebm"] = d["tc"]
            for nm, dt in [("u", F32), ("ei", I32), ("iv", I32),
                           ("ff", F32)]:
                d[nm] = alloc(prefix + nm, [128, nfree * nbatch], dt)
            d["es"] = d["u"]
            d["ri"] = d["iv"]
            return d

        def qdq_steps(items, bufs, c0, w):
            nfree, NB = bufs["nfree"], bufs["nb"]
            nb = w // 32
            b0 = c0 // 32
            EV = CHAIN.edger("v")
            EA = CHAIN.edger("a")
            EG = CHAIN.edger("g")

            def sm(name, k):
                return bufs[name][:, k * NB + b0:k * NB + b0 + nb]

            def fl(name, k):
                return bufs[name][:, k * nfree + c0:k * nfree + c0 + w]

            def xs(X):
                return X[:, c0:c0 + w]

            ks = list(range(len(items)))
            S = []

            def s1(v):
                for k in ks:
                    X, m, _ = items[k]
                    X3 = xs(X).rearrange("p (a b) -> p a b", b=32)
                    EV(v, v.tensor_reduce(sm("amx", k).unsqueeze(2), X3,
                                          axis=AX.X, op=ALU.max,
                                          apply_absolute_value=True))
                    if m is not None:
                        EV(v, v.tensor_single_scalar(sm("amx", k), sm("amx", k),
                                                     m, op=ALU.mult))
                    ins = v.tensor_single_scalar(sm("amx", k), sm("amx", k),
                                                 2.0 ** -40, op=ALU.max)
                return ins
            S.append(("v", s1, False))

            def s2(a):
                for k in ks:
                    EA(a, a.activation(sm("tc", k), sm("amx", k), AF.Square))
                    EA(a, a.activation(sm("tc", k), sm("tc", k), AF.Ln))
                    ins = a.activation(sm("tc", k), sm("tc", k), AF.Identity,
                                       scale=cst["i2l2"][:, :],
                                       bias=cst["nhalf"][:, :])
                return ins
            S.append(("a", s2, False))

            def s3(v):
                for k in ks:
                    EV(v, v.tensor_single_scalar(sm("tc", k), sm("tc", k),
                                                 -200.0, op=ALU.max))
                    EV(v, v.tensor_scalar(sm("e0i", k), sm("tc", k), MAGIC,
                                          -MAGIC, op0=ALU.add, op1=ALU.add))
                    EV(v, v.tensor_scalar(sm("bex", k), sm("e0i", k), 0.0, 14.0,
                                          op0=ALU.max, op1=ALU.min))
                    EV(v, v.tensor_tensor(sm("aex", k), sm("bex", k),
                                          sm("e0i", k), op=ALU.subtract))
                    ins = v.tensor_single_scalar(sm("aex", k), sm("aex", k),
                                                 14.0, op=ALU.min)
                return ins
            S.append(("v", s3, False))

            def s4(a):
                for k in ks:
                    EA(a, a.activation(sm("cl", k), sm("bex", k), AF.Exp,
                                       scale=cst["ln2"][:, :]))
                    ins = a.activation(sm("jf", k), sm("aex", k), AF.Exp,
                                       scale=cst["ln2"][:, :])
                return ins
            S.append(("a", s4, False))

            def s5(v):
                for k in ks:
                    X, m, _ = items[k]
                    EV(v, v.tensor_scalar(sm("Bi", k), sm("cl", k), MAGIC,
                                          -MAGIC, op0=ALU.add, op1=ALU.add))
                    EV(v, v.tensor_scalar(sm("Ai", k), sm("jf", k), MAGIC,
                                          -MAGIC, op0=ALU.add, op1=ALU.add))
                    EV(v, v.tensor_tensor(sm("tc", k), sm("amx", k),
                                          sm("Ai", k), op=ALU.mult))
                    EV(v, v.tensor_tensor(sm("cl", k), sm("tc", k), sm("Bi", k),
                                          op=ALU.is_lt))
                    EV(v, v.scalar_tensor_tensor(sm("jf", k), sm("tc", k), 0.5,
                                                 sm("Bi", k), op0=ALU.mult,
                                                 op1=ALU.is_ge))
                    EV(v, v.tensor_tensor(sm("ebf", k), sm("e0i", k),
                                          sm("cl", k), op=ALU.subtract))
                    EV(v, v.tensor_tensor(sm("ebf", k), sm("ebf", k),
                                          sm("jf", k), op=ALU.add))
                    EV(v, v.tensor_scalar(sm("jf", k), sm("ebf", k), -1.0, 8.0,
                                          op0=ALU.mult, op1=ALU.add))
                    EV(v, v.tensor_scalar(sm("jf", k), sm("jf", k), 0.0, 28.0,
                                          op0=ALU.max, op1=ALU.min))
                    EV(v, v.tensor_single_scalar(sm("j1", k), sm("jf", k), 14.0,
                                                 op=ALU.min))
                    ins = v.tensor_tensor(sm("j2", k), sm("jf", k), sm("j1", k),
                                          op=ALU.subtract)
                return ins
            S.append(("v", s5, False))

            def s6(a):
                for k in ks:
                    EA(a, a.activation(sm("cl", k), sm("j1", k), AF.Exp,
                                       scale=cst["ln2"][:, :]))
                    ins = a.activation(sm("jf", k), sm("j2", k), AF.Exp,
                                       scale=cst["ln2"][:, :])
                return ins
            S.append(("a", s6, False))

            def s7(v):
                for k in ks:
                    X, m, _ = items[k]
                    EV(v, v.tensor_scalar(sm("S1i", k), sm("cl", k), MAGIC,
                                          -MAGIC, op0=ALU.add, op1=ALU.add))
                    EV(v, v.tensor_scalar(sm("S2i", k), sm("jf", k), MAGIC,
                                          -MAGIC, op0=ALU.add, op1=ALU.add))
                    EV(v, v.tensor_tensor(sm("inv", k), sm("S1i", k),
                                          sm("S2i", k), op=ALU.mult))
                    if m is not None:
                        EV(v, v.tensor_single_scalar(sm("inv", k), sm("inv", k),
                                                     m, op=ALU.mult))
                    ins = v.tensor_single_scalar(sm("ebm", k), sm("ebf", k),
                                                 -11.0, op=ALU.add)
                return ins
            S.append(("v", s7, False))

            def s8(v):
                for k in ks:
                    X, m, _ = items[k]
                    uT = fl("u", k).rearrange("p (a b) -> p b a", b=32)
                    XT = xs(X).rearrange("p (a b) -> p b a", b=32)
                    invB = sm("inv", k).unsqueeze(1).broadcast_to([128, 32, nb])
                    ins = v.tensor_tensor(uT, XT, invB, op=ALU.mult)
                return ins
            S.append(("v", s8, False))

            def s9(g):
                for k in ks:
                    ins = g.tensor_scalar(fl("u", k), fl("u", k), 448.0, -448.0,
                                          op0=ALU.min, op1=ALU.max)
                return ins
            S.append(("g", s9, False))

            def s10(a):
                for k in ks:
                    EA(a, a.activation(fl("ff", k), fl("u", k), AF.Square))
                    ins = a.activation(fl("ff", k), fl("ff", k), AF.Ln,
                                       bias=cst["tiny"][:, :])
                return ins
            S.append(("a", s10, False))

            def s11(g):
                for k in ks:
                    ins = g.tensor_scalar(fl("ff", k), fl("ff", k), I2L2, -0.5,
                                          op0=ALU.mult, op1=ALU.add)
                return ins
            S.append(("g", s11, False))

            def s12(v):
                for k in ks:
                    EV(v, v.tensor_scalar(fl("ff", k), fl("ff", k), -6.499,
                                          MAGIC, op0=ALU.max, op1=ALU.add))
                    ins = v.tensor_single_scalar(fl("ei", k), fl("ff", k),
                                                 -MAGIC, op=ALU.add)
                return ins
            S.append(("v", s12, False))

            def s13(a):
                for k in ks:
                    ins = a.activation(fl("ff", k), fl("ei", k), AF.Exp,
                                       scale=cst["nln2"][:, :],
                                       bias=cst["n9ln2"][:, :])
                return ins
            S.append(("a", s13, False))

            def s14(v):
                for k in ks:
                    EV(v, v.tensor_scalar(fl("iv", k), fl("ff", k), MAGIC,
                                          -MAGIC, op0=ALU.add, op1=ALU.add))
                    EV(v, v.scalar_tensor_tensor(fl("ff", k), fl("iv", k),
                                                 2.0 ** -6, fl("u", k),
                                                 op0=ALU.mult, op1=ALU.mult))
                    EV(v, v.tensor_scalar(fl("ri", k), fl("ff", k), MAGIC,
                                          -MAGIC, op0=ALU.add, op1=ALU.add))
                    esT = fl("es", k).rearrange("p (a b) -> p b a", b=32)
                    eiT = fl("ei", k).rearrange("p (a b) -> p b a", b=32)
                    ebmB = sm("ebm", k).unsqueeze(1).broadcast_to([128, 32, nb])
                    ins = v.tensor_tensor(esT, eiT, ebmB, op=ALU.add)
                return ins
            S.append(("v", s14, False))

            def s16(a):
                for k in ks:
                    ins = a.activation(fl("es", k), fl("es", k), AF.Exp,
                                       scale=cst["ln2"][:, :])
                return ins
            S.append(("a", s16, False))

            def s17(v):
                for k in ks:
                    X, m, out_bf = items[k]
                    ins = v.tensor_tensor(out_bf[:, c0:c0 + w], fl("ri", k),
                                          fl("es", k), op=ALU.mult)
                return ins
            S.append(("v", s17, False))
            return S

        def emit_qdq(C, items, bufs):
            C.run(qdq_steps(items, bufs, 0, bufs["nfree"]))

        with ExitStack() as ph, nc.Block() as blk:
            C = CHAIN.bind(blk)

            def p1alloc(name, shape, dt=F32):
                return ph.enter_context(nc.sbuf_tensor(name, shape, dt))
            wld = [p1alloc(f"p1w{i}", [128, H]) for i in range(4)]
            dq = [p1alloc(f"p1d{i}", [128, H], BF16) for i in range(4)]
            stg = [p1alloc(f"p1s{i}", [128, H], BF16) for i in range(2)]
            qb = [qdq_bufs(f"p1q{i}", H, 2, p1alloc) for i in range(2)]
            tps = [ph.enter_context(nc.psum_tensor(f"p1tp{i}", [128, H], BF16))
                   for i in range(2)]

            tiles = []
            for e in range(E):
                for ot in range(2 * HID // 128):
                    tiles.append((w13_d[e, ot * 128:(ot + 1) * 128, :],
                                  w13t_s[e].rearrange("(ic r) o -> r ic o", r=128)
                                  [:, :, ot * 128:(ot + 1) * 128]))
                for ot in range(H // 128):
                    tiles.append((w2_d[e, ot * 128:(ot + 1) * 128, :],
                                  w2t_s[e].rearrange("(ic r) o -> r ic o", r=128)
                                  [:, :, ot * 128:(ot + 1) * 128]))

            def tile_steps(i, src_ap, dst_ap):
                tr = i % 2
                S = [("sy", lambda sy: sy.dma_start(out=wld[tr][:, :],
                                                    in_=src_ap), True)]
                S += qdq_steps([(wld[tr][:, :], None, dq[tr][:, :])], qb[tr],
                               0, H)

                def trf(t):
                    for ic in range(16):
                        ins = t.matmul(tps[tr][:, ic * 128:(ic + 1) * 128],
                                       dq[tr][:, ic * 128:(ic + 1) * 128],
                                       ident_bf[:, :], is_transpose=True,
                                       start=(ic % 8 == 0), stop=(ic % 8 == 7))
                    return ins
                S.append(("t", trf, False))
                S.append(("a", lambda a: a.copy(stg[tr][:, :], tps[tr][:, :]),
                          False))
                S.append(("a", lambda a: a.dma_start(
                    out=dst_ap,
                    in_=stg[tr].rearrange("r (ic o) -> r ic o", o=128)), True))
                return S

            tracks = [C.fork("0"), C.fork("1")]
            for base in range(0, len(tiles), 2):
                pair = tiles[base:base + 2]
                lists = [tile_steps(base + j, s, d)
                         for j, (s, d) in enumerate(pair)]

                interleave(blk, tracks, lists)
            C.prev = tracks[0].join(tracks[1]).prev

        with ExitStack() as ph, nc.Block() as blk:
            C = CHAIN.bind(blk)
            xld = ph.enter_context(nc.sbuf_tensor("p2x", [128, H], F32))
            xnf = ph.enter_context(nc.sbuf_tensor("p2xn", [128, H], F32))
            xnb = ph.enter_context(nc.sbuf_tensor("p2xnb", [128, H], BF16))
            sqd = ph.enter_context(nc.sbuf_tensor("p2sqd", [128, H], F32))
            wqbf = ph.enter_context(nc.sbuf_tensor("p2wq", [128, 16 * 1536], BF16))
            xnfm = ph.enter_context(nc.sbuf_tensor("p2fm", [128, 16 * 512], BF16))
            kst = ph.enter_context(nc.sbuf_tensor("p2k", [128, 512], BF16))
            mcol = ph.enter_context(nc.sbuf_tensor("p2mc", [128, 1], F32))
            rstd = ph.enter_context(nc.sbuf_tensor("p2rs", [128, 1], F32))
            nmr = ph.enter_context(nc.sbuf_tensor("p2nmr", [128, 1], F32))
            ssd = ph.enter_context(nc.sbuf_tensor("p2ssd", [128, 1], F32))
            tpx = ph.enter_context(nc.psum_tensor("p2tp", [128, H], BF16))
            pmm = ph.enter_context(nc.psum_tensor("p2pm", [128, 512], F32))

            C.dma("g", lambda g: g.dma_start(
                out=wqbf.rearrange("r (ic o) -> r ic o", o=1536),
                in_=wqt_d.rearrange("(ic r) o -> r ic o", r=128)))

            wq3 = wqbf.rearrange("r (ic o) -> r ic o", o=1536)
            for gi in range(16):
                for tt in range(4):
                    i = gi * 4 + tt
                    C.dma("sy", lambda sy, i=i: sy.dma_start(
                        out=xld[:, :], in_=x_d[i * 128:(i + 1) * 128, :]))

                    Ev = C.edger("v")
                    Ea = C.edger("a")

                    def lnv(v):
                        Ev(v, v.tensor_reduce(mcol[:, :], xld[:, :], axis=AX.X,
                                              op=ALU.add))
                        return v.tensor_single_scalar(
                            mcol[:, :], mcol[:, :], -1.0 / 2048.0, op=ALU.mult)
                    C.v(lnv)

                    def lna(a):
                        Ea(a, a.activation(sqd[:, :], xld[:, :], AF.Square,
                                           bias=mcol[:, :],
                                           accum_out=ssd[:, :]))
                        Ea(a, a.activation(rstd[:, :], ssd[:, :], AF.Ln,
                                           scale=cst["i2048"][:, :],
                                           bias=cst["eps"][:, :]))
                        return a.activation(rstd[:, :], rstd[:, :], AF.Exp,
                                            scale=cst["nhalf"][:, :])
                    C.a(lna)
                    C.v(lambda v: v.tensor_tensor(nmr[:, :], mcol[:, :],
                                                  rstd[:, :], op=ALU.mult))
                    if i < 8:
                        C.a(lambda a: a.activation(xnf[:, :], xld[:, :],
                                                   AF.Identity, scale=rstd[:, :],
                                                   bias=nmr[:, :]))
                        C.a(lambda a: a.copy(xnb[:, :], xnf[:, :]))
                        C.dma("g", lambda g, i=i: g.dma_start(
                            out=xno_s[i * 128:(i + 1) * 128, :], in_=xnf[:, :]))
                    else:
                        C.a(lambda a: a.activation(xnb[:, :], xld[:, :],
                                                   AF.Identity, scale=rstd[:, :],
                                                   bias=nmr[:, :]))

                    def trx(t):
                        for ic in range(16):
                            ins = t.matmul(tpx[:, ic * 128:(ic + 1) * 128],
                                           xnb[:, ic * 128:(ic + 1) * 128],
                                           ident_bf[:, :], is_transpose=True,
                                           start=(ic % 8 == 0),
                                           stop=(ic % 8 == 7))
                        return ins
                    C.t(trx)

                    def evx(a, tt=tt):
                        dst = xnfm.rearrange("r (ic c) -> r ic c", c=512)[
                            :, :, tt * 128:(tt + 1) * 128]
                        return a.copy(dst, tpx.rearrange("r (ic c) -> r ic c",
                                                         c=128))
                    C.a(evx)

                b2p, g4 = gi // 4, gi % 4
                outs = [("k", hi) for hi in range(4)]
                outs += [("v", s4) for s4 in range(4)]
                if g4 < 2:
                    outs += [("q", hi) for hi in range(4)]
                for kind, idx in outs:
                    def mmf(t, kind=kind, idx=idx):
                        if kind in ("k", "q"):
                            col0 = idx * 128 + (512 if kind == "k" else 0)
                            for ic in range(16):
                                ins = t.matmul(pmm[:, :],
                                               wq3[:, ic, col0:col0 + 128],
                                               xnfm[:, ic * 512:(ic + 1) * 512],
                                               start=(ic == 0), stop=(ic == 15))
                        else:
                            for ic in range(16):
                                ins = t.matmul(
                                    pmm[:, :],
                                    xnfm[:, ic * 512 + idx * 128:
                                         ic * 512 + (idx + 1) * 128],
                                    wq3[:, ic, 1024:1536],
                                    start=(ic == 0), stop=(ic == 15))
                        return ins
                    C.t(mmf)
                    C.v(lambda v: v.tensor_copy(kst[:, :], pmm[:, :]))

                    def wrk(a, kind=kind, idx=idx, gi=gi, b2p=b2p, g4=g4):
                        if kind == "k":
                            return a.dma_start(
                                out=kf_s[idx][:, gi * 512:(gi + 1) * 512],
                                in_=kst[:, :])
                        if kind == "q":
                            return a.dma_start(
                                out=qf_s[idx][:, b2p * 1024 + g4 * 512:
                                              b2p * 1024 + (g4 + 1) * 512],
                                in_=kst[:, :])
                        return a.dma_start(
                            out=vt_s.rearrange("h t c -> t h c")[
                                gi * 512 + idx * 128:gi * 512 + (idx + 1) * 128],
                            in_=kst.rearrange("r (h c) -> r h c", c=128))
                    C.dma("a", wrk)

        with ExitStack() as ph, nc.Block() as blk:
            C = CHAIN.bind(blk)
            ksb = ph.enter_context(nc.sbuf_tensor("p3k", [128, 2048], BF16))
            vsb = ph.enter_context(nc.sbuf_tensor("p3v", [128, 16 * 128], BF16))
            qsb = ph.enter_context(nc.sbuf_tensor("p3q", [128, 1024], BF16))
            pbf = ph.enter_context(nc.sbuf_tensor("p3p", [128, 2048], BF16))
            p2b = ph.enter_context(nc.sbuf_tensor("p3p2", [128, 2048], BF16))
            pts = ph.enter_context(nc.sbuf_tensor("p3pt", [128, 16 * 512], BF16))
            ctx = ph.enter_context(nc.sbuf_tensor("p3ctx", [128, 16 * 1024], BF16))
            nmx = ph.enter_context(nc.sbuf_tensor("p3nm", [128, 1], F32))
            ssum = ph.enter_context(nc.sbuf_tensor("p3ss", [128, 1], F32))
            rec = ph.enter_context(nc.sbuf_tensor("p3rc", [128, 1], F32))
            ostg = ph.enter_context(nc.sbuf_tensor("p3o", [128, 512], F32))
            wpbf = ph.enter_context(nc.sbuf_tensor("p3wp", [128, 16 * 2048], BF16))
            psc = ph.enter_context(nc.psum_tensor("p3sc", [128, 2048], F32))
            ppt = ph.enter_context(nc.psum_tensor("p3ppt", [128, 2048], BF16))
            pcx = ph.enter_context(nc.psum_tensor("p3cx", [128, 512], F32))

            C.dma("g", lambda g: g.dma_start(
                out=wpbf.rearrange("r (ic o) -> r ic o", o=2048),
                in_=wpt_d.rearrange("(ic r) o -> r ic o", r=128)))
            wp3 = wpbf.rearrange("r (ic o) -> r ic o", o=2048)

            for np_ in range(16):
                bqp, hi = np_ // 4, np_ % 4
                C.dma("sy", lambda sy, hi=hi, bqp=bqp: sy.dma_start(
                    out=ksb[:, :], in_=kf_s[hi][:, bqp * 2048:(bqp + 1) * 2048]))
                C.dma("sy", lambda sy, hi=hi, bqp=bqp: sy.dma_start(
                    out=vsb.rearrange("r (j c) -> r j c", c=128),
                    in_=vt_s[hi].rearrange("(j r) c -> r j c", r=128)
                    [:, bqp * 16:(bqp + 1) * 16, :]))
                C.dma("sy", lambda sy, hi=hi, bqp=bqp: sy.dma_start(
                    out=qsb[:, :], in_=qf_s[hi][:, bqp * 1024:(bqp + 1) * 1024]))

                for qt in range(8):
                    def qk(t, qt=qt):
                        for j in range(4):
                            ins = t.matmul(psc[:, j * 512:(j + 1) * 512],
                                           qsb[:, qt * 128:(qt + 1) * 128],
                                           ksb[:, j * 512:(j + 1) * 512],
                                           start=True, stop=True)
                        return ins
                    C.t(qk)

                    def nmax(v):
                        C.edger("v")(v, v.tensor_reduce(
                            nmx[:, :], psc[:, :], axis=AX.X,
                            op=ALU.max, negate=True))
                        return v.tensor_single_scalar(
                            nmx[:, :], nmx[:, :], 1.0 / math.sqrt(HN),
                            op=ALU.mult)
                    C.v(nmax)
                    C.a(lambda a: a.activation(pbf[:, :], psc[:, :], AF.Exp,
                                               scale=cst["anorm"][:, :],
                                               bias=nmx[:, :],
                                               accum_out=ssum[:, :]))
                    C.v(lambda v: v.reciprocal(rec[:, :], ssum[:, :]))
                    C.v(lambda v: v.tensor_single_scalar(p2b[:, :], pbf[:, :],
                                                         rec[:, :], op=ALU.mult))

                    def trp(t):
                        for j in range(16):
                            ins = t.matmul(ppt[:, j * 128:(j + 1) * 128],
                                           p2b[:, j * 128:(j + 1) * 128],
                                           ident_bf[:, :], is_transpose=True,
                                           start=(j % 8 == 0), stop=(j % 8 == 7))
                        return ins
                    C.t(trp)

                    def evp(v, qt=qt):
                        dst = pts.rearrange("r (j c) -> r j c", c=512)[
                            :, :, (qt % 4) * 128:(qt % 4 + 1) * 128]
                        return v.tensor_copy(dst, ppt.rearrange(
                            "r (j c) -> r j c", c=128))
                    C.v(evp)

                    if qt % 4 == 3:
                        def av(t):
                            for j in range(16):
                                ins = t.matmul(pcx[:, :],
                                               vsb[:, j * 128:(j + 1) * 128],
                                               pts[:, j * 512:(j + 1) * 512],
                                               start=(j == 0), stop=(j == 15))
                            return ins
                        C.t(av)

                        def evc(a, qt=qt, np_=np_):
                            dst = ctx[:, np_ * 1024 + (qt // 4) * 512:
                                      np_ * 1024 + (qt // 4 + 1) * 512]
                            return a.copy(dst, pcx[:, :])
                        C.a(evc)

            for qt in range(8):
                for dc in range(4):
                    def wpj(t, qt=qt, dc=dc):
                        for n2 in range(16):
                            t.matmul(pcx[:, :],
                                     ctx[:, n2 * 1024 + qt * 128:
                                         n2 * 1024 + (qt + 1) * 128],
                                     wp3[:, n2, dc * 512:(dc + 1) * 512],
                                     start=(n2 == 0), stop=False)
                        return t.matmul(pcx[:, :], ones1[0:1, 0:128],
                                        bprj[0:1, dc * 512:(dc + 1) * 512],
                                        start=False, stop=True)
                    C.t(wpj)
                    C.a(lambda a: a.copy(ostg[:, :], pcx[:, :]))
                    C.dma("g", lambda g, qt=qt, dc=dc: g.dma_start(
                        out=out_d[qt * 128:(qt + 1) * 128,
                                  dc * 512:(dc + 1) * 512],
                        in_=ostg[:, :]))

        ph03.close()
        mker = sb("mker", [128, 8 * E], F32)
        ph45 = st.enter_context(ExitStack())
        xnbfm = ph45.enter_context(nc.sbuf_tensor(
            "xnbfm", [128, 8 * H], BF16))
        with ExitStack() as ph, nc.Block() as blk:
            C = CHAIN.bind(blk)
            xot = ph.enter_context(nc.sbuf_tensor("p4x", [128, H], F32))
            gwsb = ph.enter_context(nc.sbuf_tensor("p4gw", [128, 16 * E], F32))
            xn32 = ph.enter_context(nc.sbuf_tensor("p4fm", [128, 2048], F32))
            sc8 = ph.enter_context(nc.sbuf_tensor("p4sc", [128, E], F32))
            bb8 = ph.enter_context(nc.sbuf_tensor("p4bb", [128, E], F32))
            t8 = ph.enter_context(nc.sbuf_tensor("p4t8", [128, E], F32))
            m1 = ph.enter_context(nc.sbuf_tensor("p4m1", [128, E], F32))
            m2 = ph.enter_context(nc.sbuf_tensor("p4m2", [128, E], F32))
            mx1 = ph.enter_context(nc.sbuf_tensor("p4mx", [128, 1], F32))
            sv1 = ph.enter_context(nc.sbuf_tensor("p4sv1", [128, 1], F32))
            sv2 = ph.enter_context(nc.sbuf_tensor("p4sv2", [128, 1], F32))
            den = ph.enter_context(nc.sbuf_tensor("p4den", [128, 1], F32))
            tpf = ph.enter_context(nc.psum_tensor("p4tp", [128, 2048], F32))
            pg = ph.enter_context(nc.psum_tensor("p4pg", [128, E], F32))

            C.dma("sy", lambda sy: sy.dma_start(
                out=gwsb.rearrange("r (ic e) -> r ic e", e=E),
                in_=gwt_d.rearrange("(ic r) e -> r ic e", r=128)))
            gw3 = gwsb.rearrange("r (ic e) -> r ic e", e=E)

            for tt in range(8):
                C.dma("sy", lambda sy, tt=tt: sy.dma_start(
                    out=xot[:, :], in_=xno_s[tt * 128:(tt + 1) * 128, :]))

                def trf(t):
                    for ic in range(16):
                        ins = t.matmul(tpf[:, ic * 128:(ic + 1) * 128],
                                       xot[:, ic * 128:(ic + 1) * 128],
                                       ident_f[:, :], is_transpose=True,
                                       start=(ic % 4 == 0), stop=(ic % 4 == 3))
                    return ins
                C.t(trf)

                def evf(a, tt=tt):
                    a.copy(xn32[:, :], tpf[:, :])
                    return a.copy(xnbfm[:, tt * H:(tt + 1) * H], tpf[:, :])
                C.a(evf)

                def gmm(t):
                    for ic in range(16):
                        ins = t.matmul(pg[:, :], xn32[:, ic * 128:(ic + 1) * 128],
                                       gw3[:, ic, :],
                                       start=(ic == 0), stop=(ic == 15))
                    return ins
                C.t(gmm)
                C.a(lambda a: a.activation(sc8[:, :], pg[:, :], AF.Sigmoid))

                def top2(v, tt=tt):
                    BIG = 1.0e4
                    Ev = C.edger("v")
                    Mt = mker[:, tt * E:(tt + 1) * E]
                    Ev(v, v.tensor_tensor(bb8[:, :], sc8[:, :], ebias[:, :],
                                          op=ALU.add))
                    Ev(v, v.tensor_reduce(mx1[:, :], bb8[:, :], axis=AX.X,
                                          op=ALU.max))
                    Ev(v, v.tensor_single_scalar(m1[:, :], bb8[:, :], mx1[:, :],
                                                 op=ALU.is_ge))
                    Ev(v, v.scalar_tensor_tensor(t8[:, :], m1[:, :], -BIG,
                                                 iota_b[:, :], op0=ALU.mult,
                                                 op1=ALU.add))
                    Ev(v, v.tensor_reduce(mx1[:, :], t8[:, :], axis=AX.X,
                                          op=ALU.min))
                    Ev(v, v.tensor_single_scalar(m1[:, :], iota_e[:, :],
                                                 mx1[:, :], op=ALU.is_equal))
                    Ev(v, v.tensor_tensor(t8[:, :], m1[:, :], sc8[:, :],
                                          op=ALU.mult))
                    Ev(v, v.tensor_reduce(sv1[:, :], t8[:, :], axis=AX.X,
                                          op=ALU.add))
                    Ev(v, v.scalar_tensor_tensor(t8[:, :], m1[:, :], -BIG,
                                                 bb8[:, :], op0=ALU.mult,
                                                 op1=ALU.add))
                    Ev(v, v.tensor_reduce(mx1[:, :], t8[:, :], axis=AX.X,
                                          op=ALU.max))
                    Ev(v, v.tensor_single_scalar(m2[:, :], t8[:, :], mx1[:, :],
                                                 op=ALU.is_ge))
                    Ev(v, v.scalar_tensor_tensor(t8[:, :], m2[:, :], -BIG,
                                                 iota_b[:, :], op0=ALU.mult,
                                                 op1=ALU.add))
                    Ev(v, v.tensor_reduce(mx1[:, :], t8[:, :], axis=AX.X,
                                          op=ALU.min))
                    Ev(v, v.tensor_single_scalar(m2[:, :], iota_e[:, :],
                                                 mx1[:, :], op=ALU.is_equal))
                    Ev(v, v.tensor_tensor(t8[:, :], m2[:, :], sc8[:, :],
                                          op=ALU.mult))
                    Ev(v, v.tensor_reduce(sv2[:, :], t8[:, :], axis=AX.X,
                                          op=ALU.add))
                    Ev(v, v.tensor_tensor(den[:, :], sv1[:, :], sv2[:, :],
                                          op=ALU.add))
                    Ev(v, v.tensor_single_scalar(den[:, :], den[:, :], 1e-20,
                                                 op=ALU.add))
                    Ev(v, v.reciprocal(den[:, :], den[:, :]))
                    Ev(v, v.tensor_single_scalar(sv1[:, :], sv1[:, :],
                                                 den[:, :], op=ALU.mult))
                    Ev(v, v.tensor_single_scalar(sv2[:, :], sv2[:, :],
                                                 den[:, :], op=ALU.mult))
                    Ev(v, v.tensor_single_scalar(m1[:, :], m1[:, :], sv1[:, :],
                                                 op=ALU.mult))
                    Ev(v, v.tensor_single_scalar(m2[:, :], m2[:, :], sv2[:, :],
                                                 op=ALU.mult))
                    return v.tensor_tensor(Mt, m1[:, :], m2[:, :], op=ALU.add)
                C.v(top2)

            if debug:
                C.dma("a", lambda a: a.dma_start(out=dbg["d_m"], in_=mker[:, :]))
                C.dma("a", lambda a: a.dma_start(out=dbg["d_sc"], in_=sc8[:, :]))

        with ExitStack() as ph, nc.Block() as blk:
            C = CHAIN.bind(blk)
            w1c = ph.enter_context(nc.sbuf_tensor("p5w1", [128, 16 * 512], BF16))
            w3c = ph.enter_context(nc.sbuf_tensor("p5w3", [128, 16 * 512], BF16))
            w2c = ph.enter_context(nc.sbuf_tensor("p5w2", [128, 16 * 512], BF16))
            s1f = ph.enter_context(nc.sbuf_tensor("p5s1", [128, 512], F32))
            h3f = ph.enter_context(nc.sbuf_tensor("p5h3", [128, 512], F32))
            hhb = ph.enter_context(nc.sbuf_tensor("p5hh", [128, 512], BF16))
            hhf = ph.enter_context(nc.sbuf_tensor("p5hf", [128, 8 * 2048], BF16))
            ostg = ph.enter_context(nc.sbuf_tensor("p5o", [128, 512], F32))
            ph1 = ph.enter_context(nc.psum_tensor("p5p1", [128, 512], F32))
            ph3 = ph.enter_context(nc.psum_tensor("p5p3", [128, 512], F32))
            ptp = ph.enter_context(nc.psum_tensor("p5tp", [128, 512], BF16))
            pyo = ph.enter_context(nc.psum_tensor("p5py", [128, 512], F32))

            for hc in range(4):
                C.dma("g", lambda sy, hc=hc: sy.dma_start(
                    out=w1c.rearrange("r (ic o) -> r ic o", o=512),
                    in_=sw1t_d.rearrange("(ic r) o -> r ic o", r=128)
                    [:, :, hc * 512:(hc + 1) * 512]))
                C.dma("g", lambda sy, hc=hc: sy.dma_start(
                    out=w3c.rearrange("r (ic o) -> r ic o", o=512),
                    in_=sw3t_d.rearrange("(ic r) o -> r ic o", r=128)
                    [:, :, hc * 512:(hc + 1) * 512]))
                for tt in range(8):
                    def mm13(t, tt=tt):
                        for wch, pp in ((w1c, ph1), (w3c, ph3)):
                            w3_ = wch.rearrange("r (ic o) -> r ic o", o=512)
                            for ic in range(16):
                                ins = t.matmul(
                                    pp[:, :],
                                    xnbfm[:, tt * H + ic * 128:
                                          tt * H + (ic + 1) * 128],
                                    w3_[:, ic, :],
                                    start=(ic == 0), stop=(ic == 15))
                        return ins
                    C.t(mm13)
                    C.a(lambda a: a.activation(s1f[:, :], ph1[:, :], AF.Silu))
                    C.v(lambda v: v.tensor_copy(h3f[:, :], ph3[:, :]))
                    C.g(lambda g: g.tensor_tensor(hhb[:, :], s1f[:, :],
                                                  h3f[:, :], op=ALU.mult))

                    def trh(t):
                        for j in range(4):
                            ins = t.matmul(ptp[:, j * 128:(j + 1) * 128],
                                           hhb[:, j * 128:(j + 1) * 128],
                                           ident_bf[:, :], is_transpose=True,
                                           start=(j == 0), stop=(j == 3))
                        return ins
                    C.t(trh)
                    C.v(lambda v, hc=hc, tt=tt: v.tensor_copy(
                        hhf[:, tt * 2048 + hc * 512:tt * 2048 + (hc + 1) * 512],
                        ptp[:, :]))

            for dc in range(4):
                C.dma("g", lambda sy, dc=dc: sy.dma_start(
                    out=w2c.rearrange("r (ic o) -> r ic o", o=512),
                    in_=sw2t_d.rearrange("(ic r) o -> r ic o", r=128)
                    [:, :, dc * 512:(dc + 1) * 512]))
                for tt in range(8):
                    def mm2(t, tt=tt):
                        w3_ = w2c.rearrange("r (ic o) -> r ic o", o=512)
                        for j in range(16):
                            ins = t.matmul(
                                pyo[:, :],
                                hhf[:, tt * 2048 + j * 128:
                                    tt * 2048 + (j + 1) * 128],
                                w3_[:, j, :],
                                start=(j == 0), stop=(j == 15))
                        return ins
                    C.t(mm2)
                    C.a(lambda a: a.copy(ostg[:, :], pyo[:, :]))
                    C.dma("g", lambda g, dc=dc, tt=tt: g.dma_start(
                        out=out_d[tt * 128:(tt + 1) * 128,
                                  dc * 512:(dc + 1) * 512],
                        in_=ostg[:, :], accum_op=ALU.add))

        ph45.close()

        with ExitStack() as ph, nc.Block() as blk:
            C = CHAIN.bind(blk)

            def pb(name, shape, dt=F32):
                return ph.enter_context(nc.sbuf_tensor(name, shape, dt))
            qbx = [qdq_bufs(f"p6q{i}", H, 1, pb) for i in range(2)]
            qbh = []
            for i in range(2):
                d = dict(qbx[i])
                d["nfree"], d["nb"] = 1024, 32
                qbh.append(d)
            xqf = pb("p6xqf", [128, 8 * H], BF16)
            w13c = [pb(f"p6w13{i}", [128, 16 * 512], BF16) for i in range(2)]

            s1f = [pb(f"p6s1{i}", [128, 512]) for i in range(2)]
            h3f = [pb(f"p6h3{i}", [128, 512]) for i in range(2)]
            hf = [pb(f"p6h{i}", [128, 1024]) for i in range(2)]
            hqb = [pb(f"p6hq{i}", [128, 1024], BF16) for i in range(2)]
            hqf = pb("p6hqf", [128, 8 * H], BF16)
            ostg = s1f

            for e in range(E):
                with ExitStack() as pp:
                    tpb = [pp.enter_context(nc.psum_tensor(
                        f"p6tp{e}_{i}", [128, H], BF16)) for i in range(2)]
                    xot = [pp.enter_context(nc.sbuf_tensor(
                        f"p6x{e}_{i}", [128, H], F32)) for i in range(2)]
                    xqb = [hqf[:, (6 + i) * H:(7 + i) * H] for i in range(2)]
                    tracks = [C.fork("0"), C.fork("1")]

                    def xsteps(tt, e=e):
                        tr = tt % 2
                        S = [("sy", lambda sy, tt=tt: sy.dma_start(
                            out=xot[tr][:, :],
                            in_=xno_s[tt * 128:(tt + 1) * 128, :]), True)]
                        S += qdq_steps([(xot[tr][:, :],
                                         mker[:, tt * E + e:tt * E + e + 1],
                                         xqb[tr])], qbx[tr], 0, H)

                        def trf(t, tr=tr):
                            for ic in range(16):
                                ins = t.matmul(
                                    tpb[tr][:, ic * 128:(ic + 1) * 128],
                                    xqb[tr][:, ic * 128:(ic + 1) * 128],
                                    ident_bf[:, :], is_transpose=True,
                                    start=(ic % 8 == 0), stop=(ic % 8 == 7))
                            return ins
                        S.append(("t", trf, False))
                        S.append(("a", lambda a, tt=tt, tr=tr: a.copy(
                            xqf[:, tt * H:(tt + 1) * H], tpb[tr][:, :]), False))
                        return S

                    for q in range(4):
                        lists = [xsteps(2 * q), xsteps(2 * q + 1)]
                        interleave(blk, tracks, lists)
                    C.prev = tracks[0].join(tracks[1]).prev

                with ExitStack() as pp:
                    ph1 = [[pp.enter_context(nc.psum_tensor(
                        f"p6p1{e}_{i}_{w}", [128, 512], F32))
                        for w in range(2)] for i in range(2)]
                    ptp2 = [pp.enter_context(nc.psum_tensor(
                        f"p6t2{e}_{i}", [128, 1024], BF16)) for i in range(2)]
                    for oc in range(4):
                        C.dma("sy", lambda sy, e=e, oc=oc: sy.dma_start(
                            out=w13c[0].rearrange("r (ic o) -> r ic o", o=512),
                            in_=w13t_s[e].rearrange("(ic r) o -> r ic o", r=128)
                            [:, :, oc * 512:(oc + 1) * 512]))
                        C.dma("sy", lambda sy, e=e, oc=oc: sy.dma_start(
                            out=w13c[1].rearrange("r (ic o) -> r ic o", o=512),
                            in_=w13t_s[e].rearrange("(ic r) o -> r ic o", r=128)
                            [:, :, 2048 + oc * 512:2048 + (oc + 1) * 512]))
                        tracks = [C.fork("0"), C.fork("1")]

                        def hsteps(t0, oc=oc):
                            tr = (t0 // 2) % 2
                            S = []
                            for half, tt in enumerate((t0, t0 + 1)):
                                def mm1(t, tt=tt, tr=tr):
                                    for w in range(2):
                                        w3_ = w13c[w].rearrange(
                                            "r (ic o) -> r ic o", o=512)
                                        for ic in range(16):
                                            ins = t.matmul(
                                                ph1[tr][w][:, :],
                                                xqf[:, tt * H + ic * 128:
                                                    tt * H + (ic + 1) * 128],
                                                w3_[:, ic, :],
                                                start=(ic == 0),
                                                stop=(ic == 15))
                                    return ins
                                S.append(("t", mm1, False))
                                S.append(("a", lambda a, tr=tr: a.activation(
                                    s1f[tr][:, :], ph1[tr][0][:, :], AF.Silu),
                                    False))
                                S.append(("v", lambda v, tr=tr: v.tensor_copy(
                                    h3f[tr][:, :], ph1[tr][1][:, :]), False))
                                S.append(("g", lambda g, tr=tr, half=half:
                                          g.tensor_tensor(
                                              hf[tr][:, half * 512:
                                                     (half + 1) * 512],
                                              s1f[tr][:, :], h3f[tr][:, :],
                                              op=ALU.mult), False))
                            S += qdq_steps([(hf[tr][:, :], None,
                                             hqb[tr][:, :])], qbh[tr], 0, 1024)

                            def trh(t, tr=tr):
                                for j in range(8):
                                    ins = t.matmul(
                                        ptp2[tr][:, j * 128:(j + 1) * 128],
                                        hqb[tr][:, j * 128:(j + 1) * 128],
                                        ident_bf[:, :], is_transpose=True,
                                        start=(j == 0), stop=(j == 7))
                                return ins
                            S.append(("t", trh, False))
                            S.append(("v", lambda v, t0=t0, oc=oc, tr=tr:
                                      v.tensor_copy(
                                          hqf.rearrange("r (t c) -> r t c",
                                                        c=H)
                                          [:, t0:t0 + 2,
                                           oc * 512:(oc + 1) * 512],
                                          ptp2[tr].rearrange(
                                              "r (t c) -> r t c", c=512)),
                                      False))
                            return S

                        for q in range(2):
                            lists = [hsteps(4 * q), hsteps(4 * q + 2)]
                            interleave(blk, tracks, lists)
                        C.prev = tracks[0].join(tracks[1]).prev

                with ExitStack() as pp:
                    pyo = [pp.enter_context(nc.psum_tensor(
                        f"p6py{e}_{i}", [128, 512], F32)) for i in range(2)]
                    for dc in range(4):
                        w2cb = w13c[0]
                        C.dma("sy", lambda sy, e=e, dc=dc: sy.dma_start(
                            out=w2cb.rearrange("r (ic o) -> r ic o", o=512),
                            in_=w2t_s[e].rearrange("(ic r) o -> r ic o", r=128)
                            [:, :, dc * 512:(dc + 1) * 512]))
                        tracks = [C.fork("0"), C.fork("1")]

                        def ysteps(tt, dc=dc):
                            tr = tt % 2
                            S = []

                            def mm2(t, tt=tt, tr=tr):
                                w3_ = w2cb.rearrange("r (ic o) -> r ic o", o=512)
                                for j in range(16):
                                    ins = t.matmul(
                                        pyo[tr][:, :],
                                        hqf[:, tt * H + j * 128:
                                            tt * H + (j + 1) * 128],
                                        w3_[:, j, :],
                                        start=(j == 0), stop=(j == 15))
                                return ins
                            S.append(("t", mm2, False))
                            S.append(("a", lambda a, tr=tr: a.copy(
                                ostg[tr][:, :], pyo[tr][:, :]), False))
                            S.append(("g", lambda g, tt=tt, dc=dc, tr=tr:
                                      g.dma_start(
                                          out=out_d[tt * 128:(tt + 1) * 128,
                                                    dc * 512:(dc + 1) * 512],
                                          in_=ostg[tr][:, :],
                                          accum_op=ALU.add), True))
                            return S

                        for q in range(4):
                            lists = [ysteps(2 * q), ysteps(2 * q + 1)]
                            interleave(blk, tracks, lists)
                        C.prev = tracks[0].join(tracks[1]).prev

        with ExitStack() as ph, nc.Block() as blk:
            C = CHAIN.bind(blk)
            xrt = ph.enter_context(nc.sbuf_tensor("p7x", [128, H], F32))
            for tt in range(8):
                C.dma("sy", lambda sy, tt=tt: sy.dma_start(
                    out=xrt[:, :], in_=x_d[tt * 128:(tt + 1) * 128, :]))
                C.dma("g", lambda g, tt=tt: g.dma_start(
                    out=out_d[tt * 128:(tt + 1) * 128, :], in_=xrt[:, :],
                    accum_op=ALU.add))
            if debug:
                C.dma("a", lambda a: a.dma_start(out=dbg["d_xno"], in_=xno_s))
                C.dma("a", lambda a: a.dma_start(out=dbg["d_kf"], in_=kf_s[0]))
                C.dma("a", lambda a: a.dma_start(out=dbg["d_qf"], in_=qf_s[0]))
                C.dma("a", lambda a: a.dma_start(out=dbg["d_vt"], in_=vt_s[0]))
                C.dma("a", lambda a: a.dma_start(out=dbg["d_w13t"],
                                                 in_=w13t_s[0]))
                C.dma("a", lambda a: a.dma_start(out=dbg["d_w2t"],
                                                 in_=w2t_s[0]))
            prev = list(C.prev)

            def fin(g):
                for ps, pv in prev:
                    g.wait_ge(ps.h, pv)
            blk.gpsimd(fin)

    return nc


_NC_CACHE = {}


def kernel(L0_attn_query, L0_attn_key, L0_attn_value, L0_attn_Wqkv,
           L0_attn_Wproj, L0_attn_bproj, L0_ffn_gate_w, L0_ffn_w13, L0_ffn_w2,
           L0_ffn_sw1, L0_ffn_sw2, L0_ffn_sw3, L0_ffn_expert_bias,
           _debug=False):
    x = np.ascontiguousarray(np.asarray(L0_attn_query, dtype=np.float32))
    wqkv = np.asarray(L0_attn_Wqkv, dtype=np.float32)
    wproj = np.asarray(L0_attn_Wproj, dtype=np.float32)
    bproj = np.asarray(L0_attn_bproj, dtype=np.float32).reshape(1, H)
    gate = np.asarray(L0_ffn_gate_w, dtype=np.float32)
    w13 = np.ascontiguousarray(np.asarray(L0_ffn_w13, dtype=np.float32))
    w2 = np.ascontiguousarray(np.asarray(L0_ffn_w2, dtype=np.float32))
    sw1 = np.asarray(L0_ffn_sw1, dtype=np.float32)
    sw2 = np.asarray(L0_ffn_sw2, dtype=np.float32)
    sw3 = np.asarray(L0_ffn_sw3, dtype=np.float32)
    eb = np.asarray(L0_ffn_expert_bias, dtype=np.float32)

    wqkv_t = np.ascontiguousarray(wqkv.T)
    wproj_t = np.ascontiguousarray(wproj.T)
    gate_t = np.ascontiguousarray(gate.T)
    sw1_t = np.ascontiguousarray(sw1.T)
    sw3_t = np.ascontiguousarray(sw3.T)
    sw2_t = np.ascontiguousarray(sw2.T)
    ebias_r = np.ascontiguousarray(np.broadcast_to(eb[None, :], (128, E)))

    in_maps = []
    for c in range(NCORE):
        b2, half = c // 2, c % 2
        bidx = (np.arange(B) + b2) % B
        sidx = (np.arange(S) + half * 1024) % S
        x_c = x[bidx][:, sidx, :].reshape(T, H)
        heads = [b2 + 4 * hi for hi in range(4)]
        cols = []
        for part in range(3):
            for h in heads:
                cols.extend(range(h * 384 + part * 128,
                                  h * 384 + part * 128 + 128))
        wqt_c = np.ascontiguousarray(wqkv_t[:, cols])
        rows = []
        for npb in range(16):
            bqp, hi = npb // 4, npb % 4
            n = 4 * ((bqp + b2) % 4) + hi
            rows.extend(range(n * 128, (n + 1) * 128))
        wpt_c = np.ascontiguousarray(wproj_t[rows, :])
        in_maps.append({
            "x": np.ascontiguousarray(x_c), "wqt": wqt_c, "wpt": wpt_c,
            "bproj": np.ascontiguousarray(bproj), "gwt": gate_t,
            "ebias": ebias_r, "sw1t": sw1_t, "sw3t": sw3_t, "sw2t": sw2_t,
            "w13": w13, "w2": w2,
        })

    key = "dbg" if _debug else "nc"
    if key not in _NC_CACHE:
        _NC_CACHE[key] = build_program(debug=_debug)
    nc = _NC_CACHE[key]
    res = run_bass_kernel_spmd(nc, in_maps, core_ids=list(range(NCORE)))

    out = np.empty((B, S, H), dtype=np.float32)
    for c in range(NCORE):
        b2, half = c // 2, c % 2
        out[b2, half * 1024:(half + 1) * 1024, :] = np.asarray(
            res.results[c]["out"])
    if _debug:
        return out, res
    return out


# version 17
# speedup vs baseline: 1.1671x; 1.0412x over previous
import math
import numpy as np
from contextlib import ExitStack

import concourse.bass as bass
import concourse.mybir as mybir
from concourse.bass_utils import run_bass_kernel_spmd

F32 = mybir.dt.float32
BF16 = mybir.dt.bfloat16
I32 = mybir.dt.int32
ALU = mybir.AluOpType
AF = mybir.ActivationFunctionType
AX = mybir.AxisListType

LN2 = math.log(2.0)
B, S, H = 4, 2048, 2048
NH, HN = 16, 128
E, HID = 8, 2048
T = B * S
TOWN = 1024
NCORE = 8
I2L2 = 1.0 / (2.0 * LN2)
MAGIC = 12582912.0


class Sem:
    def __init__(self, st, nc, name):
        self.h = st.enter_context(nc.semaphore(name))
        self.n = 0


class Chain:

    def __init__(self, sems, sfx="j"):
        self.blk = None
        self.s = sems
        self.sfx = sfx
        self.prev = []

    def bind(self, blk):
        self.blk = blk
        return self

    def fork(self, sfx=None):
        c = Chain(self.s, sfx if sfx is not None else self.sfx)
        c.blk = self.blk
        c.prev = list(self.prev)
        return c

    def join(self, other):
        seen = {}
        for s, v in self.prev + other.prev:
            seen[id(s)] = (s, max(v, seen.get(id(s), (s, -1))[1]))
        self.prev = [sv for sv in seen.values()]
        return self

    def _wrap(self, kind, fn, dma=False):
        sem = self.s[kind + "d" + self.sfx] if dma else self.s[kind]
        prev = list(self.prev)

        def body(eng):
            for ps, pv in prev:
                eng.wait_ge(ps.h, pv)
            ins = fn(eng)
            assert ins is not None, "chain step must return its last instruction"
            ins.then_inc(sem.h, 16 if dma else 1)
            sem.n += 16 if dma else 1
        getattr(self.blk, {"v": "vector", "a": "scalar", "g": "gpsimd",
                           "t": "tensor", "sy": "sync"}[kind])(body)
        self.prev = [(sem, sem.n)]

    def v(self, fn):
        self._wrap("v", fn)

    def a(self, fn):
        self._wrap("a", fn)

    def g(self, fn):
        self._wrap("g", fn)

    def t(self, fn):
        self._wrap("t", fn)

    def dma(self, kind, fn):
        self._wrap(kind, fn, dma=True)

    def edger(self, kind):
        sem = self.s[kind + "e"]

        def E(eng, ins):
            ins.then_inc(sem.h, 1)
            sem.n += 1
            eng.wait_ge(sem.h, sem.n)
        return E

    def sched(self, kind, fn, dma=False):
        sem = self.s[kind + "d" + self.sfx] if dma else self.s[kind]
        entry = (kind, fn, dma, list(self.prev), sem, 16 if dma else 1)
        sem.n += entry[5]
        self.prev = [(sem, sem.n)]
        return entry

    def run(self, steps):
        for kind, fn, dma in steps:
            self._wrap(kind, fn, dma=dma)


KINDMAP = {"v": "vector", "a": "scalar", "g": "gpsimd", "t": "tensor",
           "sy": "sync"}


def emit_plan(blk, plan):
    kinds = []
    for e in plan:
        if e[0] not in kinds:
            kinds.append(e[0])
    for k in kinds:
        group = [e for e in plan if e[0] == k]

        def body(eng, group=group):
            for kind, fn, dma, prevs, sem, inc in group:
                for ps, pv in prevs:
                    eng.wait_ge(ps.h, pv)
                ins = fn(eng)
                assert ins is not None
                ins.then_inc(sem.h, inc)
        getattr(blk, KINDMAP[k])(body)


def interleave(blk, chains, steplists, stagger=None):
    if stagger:
        steplists = [[None] * s + list(sl)
                     for s, sl in zip(stagger, steplists)]
    plan = []
    n = max(len(s) for s in steplists)
    for i in range(n):
        for c, steps in zip(chains, steplists):
            if i < len(steps) and steps[i] is not None:
                kind, fn, dma = steps[i]
                plan.append(c.sched(kind, fn, dma=dma))
    emit_plan(blk, plan)


def build_program(debug=False):
    nc = bass.Bass("TRN2", target_bir_lowering=False, debug=False)

    x_d = nc.dram_tensor("x", [T, H], F32, kind="ExternalInput").ap()
    wqt_d = nc.dram_tensor("wqt", [H, 1536], F32, kind="ExternalInput").ap()
    wpt_d = nc.dram_tensor("wpt", [H, H], F32, kind="ExternalInput").ap()
    bproj_d = nc.dram_tensor("bproj", [1, H], F32, kind="ExternalInput").ap()
    gwt_d = nc.dram_tensor("gwt", [H, E], F32, kind="ExternalInput").ap()
    ebias_d = nc.dram_tensor("ebias", [128, E], F32, kind="ExternalInput").ap()
    sw1t_d = nc.dram_tensor("sw1t", [H, HID], F32, kind="ExternalInput").ap()
    sw3t_d = nc.dram_tensor("sw3t", [H, HID], F32, kind="ExternalInput").ap()
    sw2t_d = nc.dram_tensor("sw2t", [HID, H], F32, kind="ExternalInput").ap()
    w13_d = nc.dram_tensor("w13", [E, 2 * HID, H], F32, kind="ExternalInput").ap()
    w2_d = nc.dram_tensor("w2", [E, H, HID], F32, kind="ExternalInput").ap()
    out_d = nc.dram_tensor("out", [TOWN, H], F32, kind="ExternalOutput").ap()

    w13t_s = nc.dram_tensor("s_w13t", [E, H, 2 * HID], BF16, kind="Internal").ap()
    w2t_s = nc.dram_tensor("s_w2t", [E, HID, H], BF16, kind="Internal").ap()
    kf_s = nc.dram_tensor("s_kf", [4, HN, T], BF16, kind="Internal").ap()
    vt_s = nc.dram_tensor("s_vt", [4, T, HN], BF16, kind="Internal").ap()
    qf_s = nc.dram_tensor("s_qf", [4, HN, 4096], BF16, kind="Internal").ap()
    xno_s = nc.dram_tensor("s_xno", [TOWN, H], F32, kind="Internal").ap()
    dbg = {}
    if debug:
        for nm, shp, dt in [("d_xno", [TOWN, H], F32),
                            ("d_m", [128, 8 * E], F32),
                            ("d_sc", [128, E], F32),
                            ("d_kf", [HN, T], BF16),
                            ("d_qf", [HN, 4096], BF16),
                            ("d_vt", [T, HN], BF16),
                            ("d_w13t", [H, 2 * HID], BF16),
                            ("d_w2t", [HID, H], BF16)]:
            dbg[nm] = nc.dram_tensor(nm, shp, dt, kind="ExternalOutput").ap()

    st = ExitStack()
    with st:
        def sb(name, shape, dt=F32):
            return st.enter_context(nc.sbuf_tensor(name, shape, dt))

        sems = {k: Sem(st, nc, "s_" + k)
                for k in ["v", "a", "g", "t", "sy", "ve", "ae", "ge"]}
        for q in ["vd", "ad", "gd", "syd"]:
            for sfx in ["j", "0", "1"]:
                sems[q + sfx] = Sem(st, nc, "s_" + q + sfx)
        CHAIN = Chain(sems)

        ph03 = st.enter_context(ExitStack())
        ident_bf = sb("ident_bf", [128, 128], BF16)
        ident_f = sb("ident_f", [128, 128], F32)
        iota_t = sb("iota_t", [128, 128], F32)
        iota_e = sb("iota_e", [128, E], F32)
        iota_b = sb("iota_b", [128, E], F32)
        ebias = sb("ebias_t", [128, E], F32)
        cst = {}
        for nm, val in [("ln2", LN2), ("nln2", -LN2), ("n9ln2", 9 * LN2),
                        ("i2048", 1.0 / 2048.0), ("eps", 1e-5),
                        ("nhalf", -0.5), ("anorm", 1.0 / math.sqrt(HN)),
                        ("i2l2", I2L2), ("tiny", 2.0 ** -100)]:
            cst[nm] = sb("c_" + nm, [128, 1], F32)
        ones1 = ph03.enter_context(nc.sbuf_tensor("ones1", [1, 128], F32))
        bprj = ph03.enter_context(nc.sbuf_tensor("bprj", [1, H], F32))

        LNEXP_SET = 6

        def preload_lnexp(a):
            return a.add_instruction(mybir.InstLoadActFuncSet(
                name=nc.get_next_instruction_name(),
                act_func_set_id=LNEXP_SET, ins=[], outs=[]))

        with nc.Block() as blk:
            C = CHAIN.bind(blk)
            C.g(lambda g: (g.iota(iota_t[:, :], [[1, 128]], channel_multiplier=-1,
                                  allow_small_or_imprecise_dtypes=True),
                           g.iota(iota_e[:, :], [[1, E]], channel_multiplier=0,
                                  allow_small_or_imprecise_dtypes=True))[-1])

            Ev0 = C.edger("v")

            def p0v(v):
                for nm, val in [("ln2", LN2), ("nln2", -LN2), ("n9ln2", 9 * LN2),
                                ("i2048", 1.0 / 2048.0), ("eps", 1e-5),
                                ("nhalf", -0.5), ("anorm", 1.0 / math.sqrt(HN)),
                                ("i2l2", I2L2), ("tiny", 2.0 ** -100)]:
                    v.memset(cst[nm][:, :], val)
                v.memset(ones1[:, :], 1.0)
                Ev0(v, v.tensor_single_scalar(ident_f[:, :], iota_t[:, :], 0.0,
                                              op=ALU.is_equal))
                v.tensor_single_scalar(iota_b[:, :], iota_e[:, :], 1.0e4,
                                       op=ALU.add)
                return v.tensor_copy(ident_bf[:, :], ident_f[:, :])
            C.v(p0v)
            C.dma("sy", lambda sy: sy.dma_start(out=bprj[:, :], in_=bproj_d))
            C.dma("sy", lambda sy: sy.dma_start(out=ebias[:, :], in_=ebias_d))



        def qdq_bufs(prefix, nfree, nbatch, alloc):
            nb = nfree // 32
            d = {"nfree": nfree, "nb": nb, "nbatch": nbatch}
            for nm in ["amx", "e0i", "bex", "aex", "Bi", "Ai", "tc", "cl",
                       "ebf", "jf", "inv"]:
                dt = I32 if nm in ("e0i", "Bi", "Ai") else F32
                d[nm] = alloc(prefix + nm, [128, nb * nbatch], dt)
            d["S1i"] = d["Bi"]
            d["S2i"] = d["Ai"]
            d["j1"] = d["bex"]
            d["j2"] = d["aex"]
            d["ebm"] = d["tc"]
            for nm, dt in [("u", F32), ("ei", I32), ("iv", I32),
                           ("ff", F32)]:
                d[nm] = alloc(prefix + nm, [128, nfree * nbatch], dt)
            d["es"] = d["u"]
            d["ri"] = d["iv"]
            return d

        def qdq_steps(items, bufs, c0, w):
            nfree, NB = bufs["nfree"], bufs["nb"]
            nb = w // 32
            b0 = c0 // 32
            EV = CHAIN.edger("v")
            EA = CHAIN.edger("a")
            EG = CHAIN.edger("g")

            def sm(name, k):
                return bufs[name][:, k * NB + b0:k * NB + b0 + nb]

            def fl(name, k):
                return bufs[name][:, k * nfree + c0:k * nfree + c0 + w]

            def xs(X):
                return X[:, c0:c0 + w]

            ks = list(range(len(items)))
            S = []

            def s1(v):
                for k in ks:
                    X, m, _ = items[k]
                    X3 = xs(X).rearrange("p (a b) -> p a b", b=32)
                    EV(v, v.tensor_reduce(sm("amx", k).unsqueeze(2), X3,
                                          axis=AX.X, op=ALU.max,
                                          apply_absolute_value=True))
                    if m is not None:
                        EV(v, v.tensor_single_scalar(sm("amx", k), sm("amx", k),
                                                     m, op=ALU.mult))
                    ins = v.tensor_single_scalar(sm("amx", k), sm("amx", k),
                                                 2.0 ** -40, op=ALU.max)
                return ins
            S.append(("v", s1, False))

            def s2(a):
                for k in ks:
                    EA(a, a.activation(sm("tc", k), sm("amx", k), AF.Square))
                    EA(a, a.activation(sm("tc", k), sm("tc", k), AF.Ln))
                    ins = a.activation(sm("tc", k), sm("tc", k), AF.Identity,
                                       scale=cst["i2l2"][:, :],
                                       bias=cst["nhalf"][:, :])
                return ins
            S.append(("a", s2, False))

            def s3(v):
                for k in ks:
                    EV(v, v.tensor_single_scalar(sm("tc", k), sm("tc", k),
                                                 -200.0, op=ALU.max))
                    EV(v, v.tensor_scalar(sm("e0i", k), sm("tc", k), MAGIC,
                                          -MAGIC, op0=ALU.add, op1=ALU.add))
                    EV(v, v.tensor_scalar(sm("bex", k), sm("e0i", k), 0.0, 14.0,
                                          op0=ALU.max, op1=ALU.min))
                    EV(v, v.tensor_tensor(sm("aex", k), sm("bex", k),
                                          sm("e0i", k), op=ALU.subtract))
                    ins = v.tensor_single_scalar(sm("aex", k), sm("aex", k),
                                                 14.0, op=ALU.min)
                return ins
            S.append(("v", s3, False))

            def s4(a):
                for k in ks:
                    EA(a, a.activation(sm("cl", k), sm("bex", k), AF.Exp,
                                       scale=cst["ln2"][:, :]))
                    ins = a.activation(sm("jf", k), sm("aex", k), AF.Exp,
                                       scale=cst["ln2"][:, :])
                return ins
            S.append(("a", s4, False))

            def s5(v):
                for k in ks:
                    X, m, _ = items[k]
                    EV(v, v.tensor_scalar(sm("Bi", k), sm("cl", k), MAGIC,
                                          -MAGIC, op0=ALU.add, op1=ALU.add))
                    EV(v, v.tensor_scalar(sm("Ai", k), sm("jf", k), MAGIC,
                                          -MAGIC, op0=ALU.add, op1=ALU.add))
                    EV(v, v.tensor_tensor(sm("tc", k), sm("amx", k),
                                          sm("Ai", k), op=ALU.mult))
                    EV(v, v.tensor_tensor(sm("cl", k), sm("tc", k), sm("Bi", k),
                                          op=ALU.is_lt))
                    EV(v, v.scalar_tensor_tensor(sm("jf", k), sm("tc", k), 0.5,
                                                 sm("Bi", k), op0=ALU.mult,
                                                 op1=ALU.is_ge))
                    EV(v, v.tensor_tensor(sm("ebf", k), sm("e0i", k),
                                          sm("cl", k), op=ALU.subtract))
                    EV(v, v.tensor_tensor(sm("ebf", k), sm("ebf", k),
                                          sm("jf", k), op=ALU.add))
                    EV(v, v.tensor_scalar(sm("jf", k), sm("ebf", k), -1.0, 8.0,
                                          op0=ALU.mult, op1=ALU.add))
                    EV(v, v.tensor_scalar(sm("jf", k), sm("jf", k), 0.0, 28.0,
                                          op0=ALU.max, op1=ALU.min))
                    EV(v, v.tensor_single_scalar(sm("j1", k), sm("jf", k), 14.0,
                                                 op=ALU.min))
                    ins = v.tensor_tensor(sm("j2", k), sm("jf", k), sm("j1", k),
                                          op=ALU.subtract)
                return ins
            S.append(("v", s5, False))

            def s6(a):
                for k in ks:
                    EA(a, a.activation(sm("cl", k), sm("j1", k), AF.Exp,
                                       scale=cst["ln2"][:, :]))
                    ins = a.activation(sm("jf", k), sm("j2", k), AF.Exp,
                                       scale=cst["ln2"][:, :])
                return ins
            S.append(("a", s6, False))

            def s7(v):
                for k in ks:
                    X, m, _ = items[k]
                    EV(v, v.tensor_scalar(sm("S1i", k), sm("cl", k), MAGIC,
                                          -MAGIC, op0=ALU.add, op1=ALU.add))
                    EV(v, v.tensor_scalar(sm("S2i", k), sm("jf", k), MAGIC,
                                          -MAGIC, op0=ALU.add, op1=ALU.add))
                    EV(v, v.tensor_tensor(sm("inv", k), sm("S1i", k),
                                          sm("S2i", k), op=ALU.mult))
                    if m is not None:
                        EV(v, v.tensor_single_scalar(sm("inv", k), sm("inv", k),
                                                     m, op=ALU.mult))
                    ins = v.tensor_single_scalar(sm("ebm", k), sm("ebf", k),
                                                 -11.0, op=ALU.add)
                return ins
            S.append(("v", s7, False))

            def s8(v):
                for k in ks:
                    X, m, _ = items[k]
                    uT = fl("u", k).rearrange("p (a b) -> p b a", b=32)
                    XT = xs(X).rearrange("p (a b) -> p b a", b=32)
                    invB = sm("inv", k).unsqueeze(1).broadcast_to([128, 32, nb])
                    ins = v.tensor_tensor(uT, XT, invB, op=ALU.mult)
                return ins
            S.append(("v", s8, False))

            def s9(g):
                for k in ks:
                    ins = g.tensor_scalar(fl("u", k), fl("u", k), 448.0, -448.0,
                                          op0=ALU.min, op1=ALU.max)
                return ins
            S.append(("g", s9, False))

            def s10(a):
                for k in ks:
                    EA(a, a.activation(fl("ff", k), fl("u", k), AF.Square))
                    ins = a.activation(fl("ff", k), fl("ff", k), AF.Ln,
                                       bias=cst["tiny"][:, :])
                return ins
            S.append(("a", s10, False))

            def s11(g):
                for k in ks:
                    ins = g.tensor_scalar(fl("ff", k), fl("ff", k), I2L2, -0.5,
                                          op0=ALU.mult, op1=ALU.add)
                return ins
            S.append(("g", s11, False))

            def s12(v):
                for k in ks:
                    EV(v, v.tensor_scalar(fl("ff", k), fl("ff", k), -6.499,
                                          MAGIC, op0=ALU.max, op1=ALU.add))
                    ins = v.tensor_single_scalar(fl("ei", k), fl("ff", k),
                                                 -MAGIC, op=ALU.add)
                return ins
            S.append(("v", s12, False))

            def s13(a):
                for k in ks:
                    ins = a.activation(fl("ff", k), fl("ei", k), AF.Exp,
                                       scale=cst["nln2"][:, :],
                                       bias=cst["n9ln2"][:, :])
                return ins
            S.append(("a", s13, False))

            def s14(v):
                for k in ks:
                    EV(v, v.tensor_scalar(fl("iv", k), fl("ff", k), MAGIC,
                                          -MAGIC, op0=ALU.add, op1=ALU.add))
                    EV(v, v.scalar_tensor_tensor(fl("ff", k), fl("iv", k),
                                                 2.0 ** -6, fl("u", k),
                                                 op0=ALU.mult, op1=ALU.mult))
                    EV(v, v.tensor_scalar(fl("ri", k), fl("ff", k), MAGIC,
                                          -MAGIC, op0=ALU.add, op1=ALU.add))
                    esT = fl("es", k).rearrange("p (a b) -> p b a", b=32)
                    eiT = fl("ei", k).rearrange("p (a b) -> p b a", b=32)
                    ebmB = sm("ebm", k).unsqueeze(1).broadcast_to([128, 32, nb])
                    ins = v.tensor_tensor(esT, eiT, ebmB, op=ALU.add)
                return ins
            S.append(("v", s14, False))

            def s16(a):
                for k in ks:
                    ins = a.activation(fl("es", k), fl("es", k), AF.Exp,
                                       scale=cst["ln2"][:, :])
                return ins
            S.append(("a", s16, False))

            def s17(v):
                for k in ks:
                    X, m, out_bf = items[k]
                    ins = v.tensor_tensor(out_bf[:, c0:c0 + w], fl("ri", k),
                                          fl("es", k), op=ALU.mult)
                return ins
            S.append(("v", s17, False))
            return S

        def emit_qdq(C, items, bufs):
            C.run(qdq_steps(items, bufs, 0, bufs["nfree"]))

        with ExitStack() as ph, nc.Block() as blk:
            C = CHAIN.bind(blk)

            def p1alloc(name, shape, dt=F32):
                return ph.enter_context(nc.sbuf_tensor(name, shape, dt))
            wld = [p1alloc(f"p1w{i}", [128, H]) for i in range(4)]
            dq = [p1alloc(f"p1d{i}", [128, H], BF16) for i in range(4)]
            stg = [p1alloc(f"p1s{i}", [128, H], BF16) for i in range(2)]
            qb = [qdq_bufs(f"p1q{i}", H, 2, p1alloc) for i in range(2)]
            tps = [ph.enter_context(nc.psum_tensor(f"p1tp{i}", [128, H], BF16))
                   for i in range(2)]

            tiles = []
            for e in range(E):
                for ot in range(2 * HID // 128):
                    tiles.append((w13_d[e, ot * 128:(ot + 1) * 128, :],
                                  w13t_s[e].rearrange("(ic r) o -> r ic o", r=128)
                                  [:, :, ot * 128:(ot + 1) * 128]))
                for ot in range(H // 128):
                    tiles.append((w2_d[e, ot * 128:(ot + 1) * 128, :],
                                  w2t_s[e].rearrange("(ic r) o -> r ic o", r=128)
                                  [:, :, ot * 128:(ot + 1) * 128]))

            def tile_steps(i, src_ap, dst_ap):
                tr = i % 2
                S = [("sy", lambda sy: sy.dma_start(out=wld[tr][:, :],
                                                    in_=src_ap), True)]
                S += qdq_steps([(wld[tr][:, :], None, dq[tr][:, :])], qb[tr],
                               0, H)

                def trf(t):
                    for ic in range(16):
                        ins = t.matmul(tps[tr][:, ic * 128:(ic + 1) * 128],
                                       dq[tr][:, ic * 128:(ic + 1) * 128],
                                       ident_bf[:, :], is_transpose=True,
                                       start=(ic % 8 == 0), stop=(ic % 8 == 7))
                    return ins
                S.append(("t", trf, False))
                S.append(("a", lambda a: a.copy(stg[tr][:, :], tps[tr][:, :]),
                          False))
                S.append(("a", lambda a: a.dma_start(
                    out=dst_ap,
                    in_=stg[tr].rearrange("r (ic o) -> r ic o", o=128)), True))
                return S

            tracks = [C.fork("0"), C.fork("1")]
            for base in range(0, len(tiles), 2):
                pair = tiles[base:base + 2]
                lists = [tile_steps(base + j, s, d)
                         for j, (s, d) in enumerate(pair)]

                interleave(blk, tracks, lists)
            C.prev = tracks[0].join(tracks[1]).prev

        with ExitStack() as ph, nc.Block() as blk:
            C = CHAIN.bind(blk)

            def p2a(name, shape, dt=F32):
                return ph.enter_context(nc.sbuf_tensor(name, shape, dt))
            xld = [p2a(f"p2x{i}", [128, H]) for i in range(2)]
            xnf = [p2a(f"p2xn{i}", [128, H]) for i in range(2)]
            xnb = [p2a(f"p2xnb{i}", [128, H], BF16) for i in range(2)]
            sqd = [p2a(f"p2sqd{i}", [128, H]) for i in range(2)]
            kst = [p2a(f"p2k{i}", [128, 512], BF16) for i in range(2)]
            mcol = [p2a(f"p2mc{i}", [128, 1]) for i in range(2)]
            rstd = [p2a(f"p2rs{i}", [128, 1]) for i in range(2)]
            nmr = [p2a(f"p2nmr{i}", [128, 1]) for i in range(2)]
            ssd = [p2a(f"p2ssd{i}", [128, 1]) for i in range(2)]
            wqbf = p2a("p2wq", [128, 16 * 1536], BF16)
            xnfm = p2a("p2fm", [128, 16 * 512], BF16)
            tpx = [ph.enter_context(nc.psum_tensor(f"p2tp{i}", [128, H], BF16))
                   for i in range(2)]
            pmm = [ph.enter_context(nc.psum_tensor(f"p2pm{i}", [128, 512],
                                                   F32)) for i in range(2)]

            C.dma("g", lambda g: g.dma_start(
                out=wqbf.rearrange("r (ic o) -> r ic o", o=1536),
                in_=wqt_d.rearrange("(ic r) o -> r ic o", r=128)))
            wq3 = wqbf.rearrange("r (ic o) -> r ic o", o=1536)
            EV2 = CHAIN.edger("v")
            EA2 = CHAIN.edger("a")

            def ln_steps(i):
                tr = i % 2
                tt = i % 4
                S = [("sy", lambda sy: sy.dma_start(
                    out=xld[tr][:, :], in_=x_d[i * 128:(i + 1) * 128, :]),
                    True)]

                def lnv(v):
                    EV2(v, v.tensor_reduce(mcol[tr][:, :], xld[tr][:, :],
                                           axis=AX.X, op=ALU.add))
                    return v.tensor_single_scalar(
                        mcol[tr][:, :], mcol[tr][:, :], -1.0 / 2048.0,
                        op=ALU.mult)
                S.append(("v", lnv, False))

                def lna(a):
                    EA2(a, a.activation(sqd[tr][:, :], xld[tr][:, :],
                                        AF.Square, bias=mcol[tr][:, :],
                                        accum_out=ssd[tr][:, :]))
                    EA2(a, a.activation(rstd[tr][:, :], ssd[tr][:, :], AF.Ln,
                                        scale=cst["i2048"][:, :],
                                        bias=cst["eps"][:, :]))
                    return a.activation(rstd[tr][:, :], rstd[tr][:, :], AF.Exp,
                                        scale=cst["nhalf"][:, :])
                S.append(("a", lna, False))
                S.append(("v", lambda v: v.tensor_tensor(
                    nmr[tr][:, :], mcol[tr][:, :], rstd[tr][:, :],
                    op=ALU.mult), False))
                if i < 8:
                    def xa(a):
                        EA2(a, a.activation(xnf[tr][:, :], xld[tr][:, :],
                                            AF.Identity, scale=rstd[tr][:, :],
                                            bias=nmr[tr][:, :]))
                        return a.copy(xnb[tr][:, :], xnf[tr][:, :])
                    S.append(("a", xa, False))
                    S.append(("g", lambda g: g.dma_start(
                        out=xno_s[i * 128:(i + 1) * 128, :],
                        in_=xnf[tr][:, :]), True))
                else:
                    S.append(("a", lambda a: a.activation(
                        xnb[tr][:, :], xld[tr][:, :], AF.Identity,
                        scale=rstd[tr][:, :], bias=nmr[tr][:, :]), False))

                def trx(t):
                    for ic in range(16):
                        ins = t.matmul(tpx[tr][:, ic * 128:(ic + 1) * 128],
                                       xnb[tr][:, ic * 128:(ic + 1) * 128],
                                       ident_bf[:, :], is_transpose=True,
                                       start=(ic % 8 == 0), stop=(ic % 8 == 7))
                    return ins
                S.append(("t", trx, False))

                def evx(a):
                    dst = xnfm.rearrange("r (ic c) -> r ic c", c=512)[
                        :, :, tt * 128:(tt + 1) * 128]
                    return a.copy(dst, tpx[tr].rearrange(
                        "r (ic c) -> r ic c", c=128))
                S.append(("a", evx, False))
                return S

            def out_steps(kind, idx, tr, gi, b2p, g4):
                S = []

                def mmf(t):
                    if kind in ("k", "q"):
                        col0 = idx * 128 + (512 if kind == "k" else 0)
                        for ic in range(16):
                            ins = t.matmul(pmm[tr][:, :],
                                           wq3[:, ic, col0:col0 + 128],
                                           xnfm[:, ic * 512:(ic + 1) * 512],
                                           start=(ic == 0), stop=(ic == 15))
                    else:
                        for ic in range(16):
                            ins = t.matmul(
                                pmm[tr][:, :],
                                xnfm[:, ic * 512 + idx * 128:
                                     ic * 512 + (idx + 1) * 128],
                                wq3[:, ic, 1024:1536],
                                start=(ic == 0), stop=(ic == 15))
                    return ins
                S.append(("t", mmf, False))
                S.append(("v", lambda v: v.tensor_copy(kst[tr][:, :],
                                                       pmm[tr][:, :]), False))

                def wrk(a):
                    if kind == "k":
                        return a.dma_start(
                            out=kf_s[idx][:, gi * 512:(gi + 1) * 512],
                            in_=kst[tr][:, :])
                    if kind == "q":
                        return a.dma_start(
                            out=qf_s[idx][:, b2p * 1024 + g4 * 512:
                                          b2p * 1024 + (g4 + 1) * 512],
                            in_=kst[tr][:, :])
                    return a.dma_start(
                        out=vt_s.rearrange("h t c -> t h c")[
                            gi * 512 + idx * 128:gi * 512 + (idx + 1) * 128],
                        in_=kst[tr].rearrange("r (h c) -> r h c", c=128))
                S.append(("a", wrk, True))
                return S

            for gi in range(16):
                tracks = [C.fork("0"), C.fork("1")]
                for q in range(2):
                    lists = [ln_steps(gi * 4 + 2 * q),
                             ln_steps(gi * 4 + 2 * q + 1)]
                    interleave(blk, tracks, lists, stagger=[0, 3])
                C.prev = tracks[0].join(tracks[1]).prev

                b2p, g4 = gi // 4, gi % 4
                outs = [("k", hi) for hi in range(4)]
                outs += [("v", s4) for s4 in range(4)]
                if g4 < 2:
                    outs += [("q", hi) for hi in range(4)]
                tracks = [C.fork("0"), C.fork("1")]
                for q in range(0, len(outs), 2):
                    lists = [out_steps(*outs[q], 0, gi, b2p, g4)]
                    if q + 1 < len(outs):
                        lists.append(out_steps(*outs[q + 1], 1, gi, b2p, g4))
                    interleave(blk, tracks[:len(lists)], lists,
                               stagger=[0, 1][:len(lists)])
                C.prev = tracks[0].join(tracks[1]).prev

        with ExitStack() as ph, nc.Block() as blk:
            C = CHAIN.bind(blk)
            ksb = ph.enter_context(nc.sbuf_tensor("p3k", [128, 2048], BF16))
            vsb = ph.enter_context(nc.sbuf_tensor("p3v", [128, 16 * 128], BF16))
            qsb = ph.enter_context(nc.sbuf_tensor("p3q", [128, 1024], BF16))
            pbf = ph.enter_context(nc.sbuf_tensor("p3p", [128, 2048], BF16))
            p2b = ph.enter_context(nc.sbuf_tensor("p3p2", [128, 2048], BF16))
            pts = ph.enter_context(nc.sbuf_tensor("p3pt", [128, 16 * 512], BF16))
            ctx = ph.enter_context(nc.sbuf_tensor("p3ctx", [128, 16 * 1024], BF16))
            nmx = ph.enter_context(nc.sbuf_tensor("p3nm", [128, 1], F32))
            ssum = ph.enter_context(nc.sbuf_tensor("p3ss", [128, 1], F32))
            rec = ph.enter_context(nc.sbuf_tensor("p3rc", [128, 1], F32))
            ostg = ph.enter_context(nc.sbuf_tensor("p3o", [128, 512], F32))
            wpbf = ph.enter_context(nc.sbuf_tensor("p3wp", [128, 16 * 2048], BF16))
            psc = ph.enter_context(nc.psum_tensor("p3sc", [128, 2048], F32))
            ppt = ph.enter_context(nc.psum_tensor("p3ppt", [128, 2048], BF16))
            pcx = ph.enter_context(nc.psum_tensor("p3cx", [128, 512], F32))

            C.dma("g", lambda g: g.dma_start(
                out=wpbf.rearrange("r (ic o) -> r ic o", o=2048),
                in_=wpt_d.rearrange("(ic r) o -> r ic o", r=128)))
            wp3 = wpbf.rearrange("r (ic o) -> r ic o", o=2048)

            for np_ in range(16):
                bqp, hi = np_ // 4, np_ % 4
                C.dma("sy", lambda sy, hi=hi, bqp=bqp: sy.dma_start(
                    out=ksb[:, :], in_=kf_s[hi][:, bqp * 2048:(bqp + 1) * 2048]))
                C.dma("sy", lambda sy, hi=hi, bqp=bqp: sy.dma_start(
                    out=vsb.rearrange("r (j c) -> r j c", c=128),
                    in_=vt_s[hi].rearrange("(j r) c -> r j c", r=128)
                    [:, bqp * 16:(bqp + 1) * 16, :]))
                C.dma("sy", lambda sy, hi=hi, bqp=bqp: sy.dma_start(
                    out=qsb[:, :], in_=qf_s[hi][:, bqp * 1024:(bqp + 1) * 1024]))

                for qt in range(8):
                    def qk(t, qt=qt):
                        for j in range(4):
                            ins = t.matmul(psc[:, j * 512:(j + 1) * 512],
                                           qsb[:, qt * 128:(qt + 1) * 128],
                                           ksb[:, j * 512:(j + 1) * 512],
                                           start=True, stop=True)
                        return ins
                    C.t(qk)

                    def nmax(v):
                        C.edger("v")(v, v.tensor_reduce(
                            nmx[:, :], psc[:, :], axis=AX.X,
                            op=ALU.max, negate=True))
                        return v.tensor_single_scalar(
                            nmx[:, :], nmx[:, :], 1.0 / math.sqrt(HN),
                            op=ALU.mult)
                    C.v(nmax)
                    C.a(lambda a: a.activation(pbf[:, :], psc[:, :], AF.Exp,
                                               scale=cst["anorm"][:, :],
                                               bias=nmx[:, :],
                                               accum_out=ssum[:, :]))
                    C.v(lambda v: v.reciprocal(rec[:, :], ssum[:, :]))
                    C.v(lambda v: v.tensor_single_scalar(p2b[:, :], pbf[:, :],
                                                         rec[:, :], op=ALU.mult))

                    def trp(t):
                        for j in range(16):
                            ins = t.matmul(ppt[:, j * 128:(j + 1) * 128],
                                           p2b[:, j * 128:(j + 1) * 128],
                                           ident_bf[:, :], is_transpose=True,
                                           start=(j % 8 == 0), stop=(j % 8 == 7))
                        return ins
                    C.t(trp)

                    def evp(v, qt=qt):
                        dst = pts.rearrange("r (j c) -> r j c", c=512)[
                            :, :, (qt % 4) * 128:(qt % 4 + 1) * 128]
                        return v.tensor_copy(dst, ppt.rearrange(
                            "r (j c) -> r j c", c=128))
                    C.v(evp)

                    if qt % 4 == 3:
                        def av(t):
                            for j in range(16):
                                ins = t.matmul(pcx[:, :],
                                               vsb[:, j * 128:(j + 1) * 128],
                                               pts[:, j * 512:(j + 1) * 512],
                                               start=(j == 0), stop=(j == 15))
                            return ins
                        C.t(av)

                        def evc(a, qt=qt, np_=np_):
                            dst = ctx[:, np_ * 1024 + (qt // 4) * 512:
                                      np_ * 1024 + (qt // 4 + 1) * 512]
                            return a.copy(dst, pcx[:, :])
                        C.a(evc)

            for qt in range(8):
                for dc in range(4):
                    def wpj(t, qt=qt, dc=dc):
                        for n2 in range(16):
                            t.matmul(pcx[:, :],
                                     ctx[:, n2 * 1024 + qt * 128:
                                         n2 * 1024 + (qt + 1) * 128],
                                     wp3[:, n2, dc * 512:(dc + 1) * 512],
                                     start=(n2 == 0), stop=False)
                        return t.matmul(pcx[:, :], ones1[0:1, 0:128],
                                        bprj[0:1, dc * 512:(dc + 1) * 512],
                                        start=False, stop=True)
                    C.t(wpj)
                    C.a(lambda a: a.copy(ostg[:, :], pcx[:, :]))
                    C.dma("g", lambda g, qt=qt, dc=dc: g.dma_start(
                        out=out_d[qt * 128:(qt + 1) * 128,
                                  dc * 512:(dc + 1) * 512],
                        in_=ostg[:, :]))

        ph03.close()
        mker = sb("mker", [128, 8 * E], F32)
        ph45 = st.enter_context(ExitStack())
        xnbfm = ph45.enter_context(nc.sbuf_tensor(
            "xnbfm", [128, 8 * H], BF16))
        with ExitStack() as ph, nc.Block() as blk:
            C = CHAIN.bind(blk)
            xot = ph.enter_context(nc.sbuf_tensor("p4x", [128, H], F32))
            gwsb = ph.enter_context(nc.sbuf_tensor("p4gw", [128, 16 * E], F32))
            xn32 = ph.enter_context(nc.sbuf_tensor("p4fm", [128, 2048], F32))
            sc8 = ph.enter_context(nc.sbuf_tensor("p4sc", [128, E], F32))
            bb8 = ph.enter_context(nc.sbuf_tensor("p4bb", [128, E], F32))
            t8 = ph.enter_context(nc.sbuf_tensor("p4t8", [128, E], F32))
            m1 = ph.enter_context(nc.sbuf_tensor("p4m1", [128, E], F32))
            m2 = ph.enter_context(nc.sbuf_tensor("p4m2", [128, E], F32))
            mx1 = ph.enter_context(nc.sbuf_tensor("p4mx", [128, 1], F32))
            sv1 = ph.enter_context(nc.sbuf_tensor("p4sv1", [128, 1], F32))
            sv2 = ph.enter_context(nc.sbuf_tensor("p4sv2", [128, 1], F32))
            den = ph.enter_context(nc.sbuf_tensor("p4den", [128, 1], F32))
            tpf = ph.enter_context(nc.psum_tensor("p4tp", [128, 2048], F32))
            pg = ph.enter_context(nc.psum_tensor("p4pg", [128, E], F32))

            C.dma("sy", lambda sy: sy.dma_start(
                out=gwsb.rearrange("r (ic e) -> r ic e", e=E),
                in_=gwt_d.rearrange("(ic r) e -> r ic e", r=128)))
            gw3 = gwsb.rearrange("r (ic e) -> r ic e", e=E)

            for tt in range(8):
                C.dma("sy", lambda sy, tt=tt: sy.dma_start(
                    out=xot[:, :], in_=xno_s[tt * 128:(tt + 1) * 128, :]))

                def trf(t):
                    for ic in range(16):
                        ins = t.matmul(tpf[:, ic * 128:(ic + 1) * 128],
                                       xot[:, ic * 128:(ic + 1) * 128],
                                       ident_f[:, :], is_transpose=True,
                                       start=(ic % 4 == 0), stop=(ic % 4 == 3))
                    return ins
                C.t(trf)

                def evf(a, tt=tt):
                    a.copy(xn32[:, :], tpf[:, :])
                    return a.copy(xnbfm[:, tt * H:(tt + 1) * H], tpf[:, :])
                C.a(evf)

                def gmm(t):
                    for ic in range(16):
                        ins = t.matmul(pg[:, :], xn32[:, ic * 128:(ic + 1) * 128],
                                       gw3[:, ic, :],
                                       start=(ic == 0), stop=(ic == 15))
                    return ins
                C.t(gmm)
                C.a(lambda a: a.activation(sc8[:, :], pg[:, :], AF.Sigmoid))

                def top2(v, tt=tt):
                    BIG = 1.0e4
                    Ev = C.edger("v")
                    Mt = mker[:, tt * E:(tt + 1) * E]
                    Ev(v, v.tensor_tensor(bb8[:, :], sc8[:, :], ebias[:, :],
                                          op=ALU.add))
                    Ev(v, v.tensor_reduce(mx1[:, :], bb8[:, :], axis=AX.X,
                                          op=ALU.max))
                    Ev(v, v.tensor_single_scalar(m1[:, :], bb8[:, :], mx1[:, :],
                                                 op=ALU.is_ge))
                    Ev(v, v.scalar_tensor_tensor(t8[:, :], m1[:, :], -BIG,
                                                 iota_b[:, :], op0=ALU.mult,
                                                 op1=ALU.add))
                    Ev(v, v.tensor_reduce(mx1[:, :], t8[:, :], axis=AX.X,
                                          op=ALU.min))
                    Ev(v, v.tensor_single_scalar(m1[:, :], iota_e[:, :],
                                                 mx1[:, :], op=ALU.is_equal))
                    Ev(v, v.tensor_tensor(t8[:, :], m1[:, :], sc8[:, :],
                                          op=ALU.mult))
                    Ev(v, v.tensor_reduce(sv1[:, :], t8[:, :], axis=AX.X,
                                          op=ALU.add))
                    Ev(v, v.scalar_tensor_tensor(t8[:, :], m1[:, :], -BIG,
                                                 bb8[:, :], op0=ALU.mult,
                                                 op1=ALU.add))
                    Ev(v, v.tensor_reduce(mx1[:, :], t8[:, :], axis=AX.X,
                                          op=ALU.max))
                    Ev(v, v.tensor_single_scalar(m2[:, :], t8[:, :], mx1[:, :],
                                                 op=ALU.is_ge))
                    Ev(v, v.scalar_tensor_tensor(t8[:, :], m2[:, :], -BIG,
                                                 iota_b[:, :], op0=ALU.mult,
                                                 op1=ALU.add))
                    Ev(v, v.tensor_reduce(mx1[:, :], t8[:, :], axis=AX.X,
                                          op=ALU.min))
                    Ev(v, v.tensor_single_scalar(m2[:, :], iota_e[:, :],
                                                 mx1[:, :], op=ALU.is_equal))
                    Ev(v, v.tensor_tensor(t8[:, :], m2[:, :], sc8[:, :],
                                          op=ALU.mult))
                    Ev(v, v.tensor_reduce(sv2[:, :], t8[:, :], axis=AX.X,
                                          op=ALU.add))
                    Ev(v, v.tensor_tensor(den[:, :], sv1[:, :], sv2[:, :],
                                          op=ALU.add))
                    Ev(v, v.tensor_single_scalar(den[:, :], den[:, :], 1e-20,
                                                 op=ALU.add))
                    Ev(v, v.reciprocal(den[:, :], den[:, :]))
                    Ev(v, v.tensor_single_scalar(sv1[:, :], sv1[:, :],
                                                 den[:, :], op=ALU.mult))
                    Ev(v, v.tensor_single_scalar(sv2[:, :], sv2[:, :],
                                                 den[:, :], op=ALU.mult))
                    Ev(v, v.tensor_single_scalar(m1[:, :], m1[:, :], sv1[:, :],
                                                 op=ALU.mult))
                    Ev(v, v.tensor_single_scalar(m2[:, :], m2[:, :], sv2[:, :],
                                                 op=ALU.mult))
                    return v.tensor_tensor(Mt, m1[:, :], m2[:, :], op=ALU.add)
                C.v(top2)

            if debug:
                C.dma("a", lambda a: a.dma_start(out=dbg["d_m"], in_=mker[:, :]))
                C.dma("a", lambda a: a.dma_start(out=dbg["d_sc"], in_=sc8[:, :]))

        with ExitStack() as ph, nc.Block() as blk:
            C = CHAIN.bind(blk)
            w1c = ph.enter_context(nc.sbuf_tensor("p5w1", [128, 16 * 512], BF16))
            w3c = ph.enter_context(nc.sbuf_tensor("p5w3", [128, 16 * 512], BF16))
            w2c = ph.enter_context(nc.sbuf_tensor("p5w2", [128, 16 * 512], BF16))
            s1f = ph.enter_context(nc.sbuf_tensor("p5s1", [128, 512], F32))
            h3f = ph.enter_context(nc.sbuf_tensor("p5h3", [128, 512], F32))
            hhb = ph.enter_context(nc.sbuf_tensor("p5hh", [128, 512], BF16))
            hhf = ph.enter_context(nc.sbuf_tensor("p5hf", [128, 8 * 2048], BF16))
            ostg = ph.enter_context(nc.sbuf_tensor("p5o", [128, 512], F32))
            ph1 = ph.enter_context(nc.psum_tensor("p5p1", [128, 512], F32))
            ph3 = ph.enter_context(nc.psum_tensor("p5p3", [128, 512], F32))
            ptp = ph.enter_context(nc.psum_tensor("p5tp", [128, 512], BF16))
            pyo = ph.enter_context(nc.psum_tensor("p5py", [128, 512], F32))

            for hc in range(4):
                C.dma("g", lambda sy, hc=hc: sy.dma_start(
                    out=w1c.rearrange("r (ic o) -> r ic o", o=512),
                    in_=sw1t_d.rearrange("(ic r) o -> r ic o", r=128)
                    [:, :, hc * 512:(hc + 1) * 512]))
                C.dma("g", lambda sy, hc=hc: sy.dma_start(
                    out=w3c.rearrange("r (ic o) -> r ic o", o=512),
                    in_=sw3t_d.rearrange("(ic r) o -> r ic o", r=128)
                    [:, :, hc * 512:(hc + 1) * 512]))
                for tt in range(8):
                    def mm13(t, tt=tt):
                        for wch, pp in ((w1c, ph1), (w3c, ph3)):
                            w3_ = wch.rearrange("r (ic o) -> r ic o", o=512)
                            for ic in range(16):
                                ins = t.matmul(
                                    pp[:, :],
                                    xnbfm[:, tt * H + ic * 128:
                                          tt * H + (ic + 1) * 128],
                                    w3_[:, ic, :],
                                    start=(ic == 0), stop=(ic == 15))
                        return ins
                    C.t(mm13)
                    C.a(lambda a: a.activation(s1f[:, :], ph1[:, :], AF.Silu))
                    C.v(lambda v: v.tensor_copy(h3f[:, :], ph3[:, :]))
                    C.g(lambda g: g.tensor_tensor(hhb[:, :], s1f[:, :],
                                                  h3f[:, :], op=ALU.mult))

                    def trh(t):
                        for j in range(4):
                            ins = t.matmul(ptp[:, j * 128:(j + 1) * 128],
                                           hhb[:, j * 128:(j + 1) * 128],
                                           ident_bf[:, :], is_transpose=True,
                                           start=(j == 0), stop=(j == 3))
                        return ins
                    C.t(trh)
                    C.v(lambda v, hc=hc, tt=tt: v.tensor_copy(
                        hhf[:, tt * 2048 + hc * 512:tt * 2048 + (hc + 1) * 512],
                        ptp[:, :]))

            for dc in range(4):
                C.dma("g", lambda sy, dc=dc: sy.dma_start(
                    out=w2c.rearrange("r (ic o) -> r ic o", o=512),
                    in_=sw2t_d.rearrange("(ic r) o -> r ic o", r=128)
                    [:, :, dc * 512:(dc + 1) * 512]))
                for tt in range(8):
                    def mm2(t, tt=tt):
                        w3_ = w2c.rearrange("r (ic o) -> r ic o", o=512)
                        for j in range(16):
                            ins = t.matmul(
                                pyo[:, :],
                                hhf[:, tt * 2048 + j * 128:
                                    tt * 2048 + (j + 1) * 128],
                                w3_[:, j, :],
                                start=(j == 0), stop=(j == 15))
                        return ins
                    C.t(mm2)
                    C.a(lambda a: a.copy(ostg[:, :], pyo[:, :]))
                    C.dma("g", lambda g, dc=dc, tt=tt: g.dma_start(
                        out=out_d[tt * 128:(tt + 1) * 128,
                                  dc * 512:(dc + 1) * 512],
                        in_=ostg[:, :], accum_op=ALU.add))

        ph45.close()

        with ExitStack() as ph, nc.Block() as blk:
            C = CHAIN.bind(blk)

            def pb(name, shape, dt=F32):
                return ph.enter_context(nc.sbuf_tensor(name, shape, dt))
            qbx = [qdq_bufs(f"p6q{i}", H, 1, pb) for i in range(2)]
            qbh = []
            for i in range(2):
                d = dict(qbx[i])
                d["nfree"], d["nb"] = 1024, 32
                qbh.append(d)
            xqf = pb("p6xqf", [128, 8 * H], BF16)
            w13c = [pb(f"p6w13{i}", [128, 16 * 512], BF16) for i in range(2)]

            s1f = [pb(f"p6s1{i}", [128, 512]) for i in range(2)]
            h3f = [pb(f"p6h3{i}", [128, 512]) for i in range(2)]
            hf = [pb(f"p6h{i}", [128, 1024]) for i in range(2)]
            hqb = [pb(f"p6hq{i}", [128, 1024], BF16) for i in range(2)]
            hqf = pb("p6hqf", [128, 8 * H], BF16)
            ostg = s1f

            for e in range(E):
                with ExitStack() as pp:
                    tpb = [pp.enter_context(nc.psum_tensor(
                        f"p6tp{e}_{i}", [128, H], BF16)) for i in range(2)]
                    xot = [pp.enter_context(nc.sbuf_tensor(
                        f"p6x{e}_{i}", [128, H], F32)) for i in range(2)]
                    xqb = [hqf[:, (6 + i) * H:(7 + i) * H] for i in range(2)]
                    tracks = [C.fork("0"), C.fork("1")]

                    def xsteps(tt, e=e):
                        tr = tt % 2
                        S = [("sy", lambda sy, tt=tt: sy.dma_start(
                            out=xot[tr][:, :],
                            in_=xno_s[tt * 128:(tt + 1) * 128, :]), True)]
                        S += qdq_steps([(xot[tr][:, :],
                                         mker[:, tt * E + e:tt * E + e + 1],
                                         xqb[tr])], qbx[tr], 0, H)

                        def trf(t, tr=tr):
                            for ic in range(16):
                                ins = t.matmul(
                                    tpb[tr][:, ic * 128:(ic + 1) * 128],
                                    xqb[tr][:, ic * 128:(ic + 1) * 128],
                                    ident_bf[:, :], is_transpose=True,
                                    start=(ic % 8 == 0), stop=(ic % 8 == 7))
                            return ins
                        S.append(("t", trf, False))
                        S.append(("a", lambda a, tt=tt, tr=tr: a.copy(
                            xqf[:, tt * H:(tt + 1) * H], tpb[tr][:, :]), False))
                        return S

                    for q in range(4):
                        lists = [xsteps(2 * q), xsteps(2 * q + 1)]
                        interleave(blk, tracks, lists)
                    C.prev = tracks[0].join(tracks[1]).prev

                with ExitStack() as pp:
                    ph1 = [[pp.enter_context(nc.psum_tensor(
                        f"p6p1{e}_{i}_{w}", [128, 512], F32))
                        for w in range(2)] for i in range(2)]
                    ptp2 = [pp.enter_context(nc.psum_tensor(
                        f"p6t2{e}_{i}", [128, 1024], BF16)) for i in range(2)]
                    for oc in range(4):
                        C.dma("sy", lambda sy, e=e, oc=oc: sy.dma_start(
                            out=w13c[0].rearrange("r (ic o) -> r ic o", o=512),
                            in_=w13t_s[e].rearrange("(ic r) o -> r ic o", r=128)
                            [:, :, oc * 512:(oc + 1) * 512]))
                        C.dma("sy", lambda sy, e=e, oc=oc: sy.dma_start(
                            out=w13c[1].rearrange("r (ic o) -> r ic o", o=512),
                            in_=w13t_s[e].rearrange("(ic r) o -> r ic o", r=128)
                            [:, :, 2048 + oc * 512:2048 + (oc + 1) * 512]))
                        tracks = [C.fork("0"), C.fork("1")]

                        def hsteps(t0, oc=oc):
                            tr = (t0 // 2) % 2
                            S = []
                            for half, tt in enumerate((t0, t0 + 1)):
                                def mm1(t, tt=tt, tr=tr):
                                    for w in range(2):
                                        w3_ = w13c[w].rearrange(
                                            "r (ic o) -> r ic o", o=512)
                                        for ic in range(16):
                                            ins = t.matmul(
                                                ph1[tr][w][:, :],
                                                xqf[:, tt * H + ic * 128:
                                                    tt * H + (ic + 1) * 128],
                                                w3_[:, ic, :],
                                                start=(ic == 0),
                                                stop=(ic == 15))
                                    return ins
                                S.append(("t", mm1, False))
                                S.append(("a", lambda a, tr=tr: a.activation(
                                    s1f[tr][:, :], ph1[tr][0][:, :], AF.Silu),
                                    False))
                                S.append(("v", lambda v, tr=tr: v.tensor_copy(
                                    h3f[tr][:, :], ph1[tr][1][:, :]), False))
                                S.append(("g", lambda g, tr=tr, half=half:
                                          g.tensor_tensor(
                                              hf[tr][:, half * 512:
                                                     (half + 1) * 512],
                                              s1f[tr][:, :], h3f[tr][:, :],
                                              op=ALU.mult), False))
                            S += qdq_steps([(hf[tr][:, :], None,
                                             hqb[tr][:, :])], qbh[tr], 0, 1024)

                            def trh(t, tr=tr):
                                for j in range(8):
                                    ins = t.matmul(
                                        ptp2[tr][:, j * 128:(j + 1) * 128],
                                        hqb[tr][:, j * 128:(j + 1) * 128],
                                        ident_bf[:, :], is_transpose=True,
                                        start=(j == 0), stop=(j == 7))
                                return ins
                            S.append(("t", trh, False))
                            S.append(("v", lambda v, t0=t0, oc=oc, tr=tr:
                                      v.tensor_copy(
                                          hqf.rearrange("r (t c) -> r t c",
                                                        c=H)
                                          [:, t0:t0 + 2,
                                           oc * 512:(oc + 1) * 512],
                                          ptp2[tr].rearrange(
                                              "r (t c) -> r t c", c=512)),
                                      False))
                            return S

                        for q in range(2):
                            lists = [hsteps(4 * q), hsteps(4 * q + 2)]
                            interleave(blk, tracks, lists)
                        C.prev = tracks[0].join(tracks[1]).prev

                with ExitStack() as pp:
                    pyo = [pp.enter_context(nc.psum_tensor(
                        f"p6py{e}_{i}", [128, 512], F32)) for i in range(2)]
                    for dc in range(4):
                        w2cb = w13c[0]
                        C.dma("sy", lambda sy, e=e, dc=dc: sy.dma_start(
                            out=w2cb.rearrange("r (ic o) -> r ic o", o=512),
                            in_=w2t_s[e].rearrange("(ic r) o -> r ic o", r=128)
                            [:, :, dc * 512:(dc + 1) * 512]))
                        tracks = [C.fork("0"), C.fork("1")]

                        def ysteps(tt, dc=dc):
                            tr = tt % 2
                            S = []

                            def mm2(t, tt=tt, tr=tr):
                                w3_ = w2cb.rearrange("r (ic o) -> r ic o", o=512)
                                for j in range(16):
                                    ins = t.matmul(
                                        pyo[tr][:, :],
                                        hqf[:, tt * H + j * 128:
                                            tt * H + (j + 1) * 128],
                                        w3_[:, j, :],
                                        start=(j == 0), stop=(j == 15))
                                return ins
                            S.append(("t", mm2, False))
                            S.append(("a", lambda a, tr=tr: a.copy(
                                ostg[tr][:, :], pyo[tr][:, :]), False))
                            S.append(("g", lambda g, tt=tt, dc=dc, tr=tr:
                                      g.dma_start(
                                          out=out_d[tt * 128:(tt + 1) * 128,
                                                    dc * 512:(dc + 1) * 512],
                                          in_=ostg[tr][:, :],
                                          accum_op=ALU.add), True))
                            return S

                        for q in range(4):
                            lists = [ysteps(2 * q), ysteps(2 * q + 1)]
                            interleave(blk, tracks, lists)
                        C.prev = tracks[0].join(tracks[1]).prev

        with ExitStack() as ph, nc.Block() as blk:
            C = CHAIN.bind(blk)
            xrt = ph.enter_context(nc.sbuf_tensor("p7x", [128, H], F32))
            for tt in range(8):
                C.dma("sy", lambda sy, tt=tt: sy.dma_start(
                    out=xrt[:, :], in_=x_d[tt * 128:(tt + 1) * 128, :]))
                C.dma("g", lambda g, tt=tt: g.dma_start(
                    out=out_d[tt * 128:(tt + 1) * 128, :], in_=xrt[:, :],
                    accum_op=ALU.add))
            if debug:
                C.dma("a", lambda a: a.dma_start(out=dbg["d_xno"], in_=xno_s))
                C.dma("a", lambda a: a.dma_start(out=dbg["d_kf"], in_=kf_s[0]))
                C.dma("a", lambda a: a.dma_start(out=dbg["d_qf"], in_=qf_s[0]))
                C.dma("a", lambda a: a.dma_start(out=dbg["d_vt"], in_=vt_s[0]))
                C.dma("a", lambda a: a.dma_start(out=dbg["d_w13t"],
                                                 in_=w13t_s[0]))
                C.dma("a", lambda a: a.dma_start(out=dbg["d_w2t"],
                                                 in_=w2t_s[0]))
            prev = list(C.prev)

            def fin(g):
                for ps, pv in prev:
                    g.wait_ge(ps.h, pv)
            blk.gpsimd(fin)

    return nc


_NC_CACHE = {}


def kernel(L0_attn_query, L0_attn_key, L0_attn_value, L0_attn_Wqkv,
           L0_attn_Wproj, L0_attn_bproj, L0_ffn_gate_w, L0_ffn_w13, L0_ffn_w2,
           L0_ffn_sw1, L0_ffn_sw2, L0_ffn_sw3, L0_ffn_expert_bias,
           _debug=False):
    x = np.ascontiguousarray(np.asarray(L0_attn_query, dtype=np.float32))
    wqkv = np.asarray(L0_attn_Wqkv, dtype=np.float32)
    wproj = np.asarray(L0_attn_Wproj, dtype=np.float32)
    bproj = np.asarray(L0_attn_bproj, dtype=np.float32).reshape(1, H)
    gate = np.asarray(L0_ffn_gate_w, dtype=np.float32)
    w13 = np.ascontiguousarray(np.asarray(L0_ffn_w13, dtype=np.float32))
    w2 = np.ascontiguousarray(np.asarray(L0_ffn_w2, dtype=np.float32))
    sw1 = np.asarray(L0_ffn_sw1, dtype=np.float32)
    sw2 = np.asarray(L0_ffn_sw2, dtype=np.float32)
    sw3 = np.asarray(L0_ffn_sw3, dtype=np.float32)
    eb = np.asarray(L0_ffn_expert_bias, dtype=np.float32)

    wqkv_t = np.ascontiguousarray(wqkv.T)
    wproj_t = np.ascontiguousarray(wproj.T)
    gate_t = np.ascontiguousarray(gate.T)
    sw1_t = np.ascontiguousarray(sw1.T)
    sw3_t = np.ascontiguousarray(sw3.T)
    sw2_t = np.ascontiguousarray(sw2.T)
    ebias_r = np.ascontiguousarray(np.broadcast_to(eb[None, :], (128, E)))

    in_maps = []
    for c in range(NCORE):
        b2, half = c // 2, c % 2
        bidx = (np.arange(B) + b2) % B
        sidx = (np.arange(S) + half * 1024) % S
        x_c = x[bidx][:, sidx, :].reshape(T, H)
        heads = [b2 + 4 * hi for hi in range(4)]
        cols = []
        for part in range(3):
            for h in heads:
                cols.extend(range(h * 384 + part * 128,
                                  h * 384 + part * 128 + 128))
        wqt_c = np.ascontiguousarray(wqkv_t[:, cols])
        rows = []
        for npb in range(16):
            bqp, hi = npb // 4, npb % 4
            n = 4 * ((bqp + b2) % 4) + hi
            rows.extend(range(n * 128, (n + 1) * 128))
        wpt_c = np.ascontiguousarray(wproj_t[rows, :])
        in_maps.append({
            "x": np.ascontiguousarray(x_c), "wqt": wqt_c, "wpt": wpt_c,
            "bproj": np.ascontiguousarray(bproj), "gwt": gate_t,
            "ebias": ebias_r, "sw1t": sw1_t, "sw3t": sw3_t, "sw2t": sw2_t,
            "w13": w13, "w2": w2,
        })

    key = "dbg" if _debug else "nc"
    if key not in _NC_CACHE:
        _NC_CACHE[key] = build_program(debug=_debug)
    nc = _NC_CACHE[key]
    res = run_bass_kernel_spmd(nc, in_maps, core_ids=list(range(NCORE)))

    out = np.empty((B, S, H), dtype=np.float32)
    for c in range(NCORE):
        b2, half = c // 2, c % 2
        out[b2, half * 1024:(half + 1) * 1024, :] = np.asarray(
            res.results[c]["out"])
    if _debug:
        return out, res
    return out


# version 18
# speedup vs baseline: 1.1904x; 1.0200x over previous
import math
import numpy as np
from contextlib import ExitStack

import concourse.bass as bass
import concourse.mybir as mybir
from concourse.bass_utils import run_bass_kernel_spmd

F32 = mybir.dt.float32
BF16 = mybir.dt.bfloat16
I32 = mybir.dt.int32
ALU = mybir.AluOpType
AF = mybir.ActivationFunctionType
AX = mybir.AxisListType

LN2 = math.log(2.0)
B, S, H = 4, 2048, 2048
NH, HN = 16, 128
E, HID = 8, 2048
T = B * S
TOWN = 1024
NCORE = 8
I2L2 = 1.0 / (2.0 * LN2)
MAGIC = 12582912.0


class Sem:
    def __init__(self, st, nc, name):
        self.h = st.enter_context(nc.semaphore(name))
        self.n = 0


class Chain:

    def __init__(self, sems, sfx="j"):
        self.blk = None
        self.s = sems
        self.sfx = sfx
        self.prev = []

    def bind(self, blk):
        self.blk = blk
        return self

    def fork(self, sfx=None):
        c = Chain(self.s, sfx if sfx is not None else self.sfx)
        c.blk = self.blk
        c.prev = list(self.prev)
        return c

    def join(self, other):
        seen = {}
        for s, v in self.prev + other.prev:
            seen[id(s)] = (s, max(v, seen.get(id(s), (s, -1))[1]))
        self.prev = [sv for sv in seen.values()]
        return self

    def _wrap(self, kind, fn, dma=False):
        sem = self.s[kind + "d" + self.sfx] if dma else self.s[kind]
        prev = list(self.prev)

        def body(eng):
            for ps, pv in prev:
                eng.wait_ge(ps.h, pv)
            ins = fn(eng)
            assert ins is not None, "chain step must return its last instruction"
            ins.then_inc(sem.h, 16 if dma else 1)
            sem.n += 16 if dma else 1
        getattr(self.blk, {"v": "vector", "a": "scalar", "g": "gpsimd",
                           "t": "tensor", "sy": "sync"}[kind])(body)
        self.prev = [(sem, sem.n)]

    def v(self, fn):
        self._wrap("v", fn)

    def a(self, fn):
        self._wrap("a", fn)

    def g(self, fn):
        self._wrap("g", fn)

    def t(self, fn):
        self._wrap("t", fn)

    def dma(self, kind, fn):
        self._wrap(kind, fn, dma=True)

    def edger(self, kind):
        sem = self.s[kind + "e"]

        def E(eng, ins):
            ins.then_inc(sem.h, 1)
            sem.n += 1
            eng.wait_ge(sem.h, sem.n)
        return E

    def sched(self, kind, fn, dma=False):
        sem = self.s[kind + "d" + self.sfx] if dma else self.s[kind]
        entry = (kind, fn, dma, list(self.prev), sem, 16 if dma else 1)
        sem.n += entry[5]
        self.prev = [(sem, sem.n)]
        return entry

    def run(self, steps):
        for kind, fn, dma in steps:
            self._wrap(kind, fn, dma=dma)


KINDMAP = {"v": "vector", "a": "scalar", "g": "gpsimd", "t": "tensor",
           "sy": "sync"}


def emit_plan(blk, plan):
    kinds = []
    for e in plan:
        if e[0] not in kinds:
            kinds.append(e[0])
    for k in kinds:
        group = [e for e in plan if e[0] == k]

        def body(eng, group=group):
            for kind, fn, dma, prevs, sem, inc in group:
                for ps, pv in prevs:
                    eng.wait_ge(ps.h, pv)
                ins = fn(eng)
                assert ins is not None
                ins.then_inc(sem.h, inc)
        getattr(blk, KINDMAP[k])(body)


def interleave(blk, chains, steplists, stagger=None):
    if stagger:
        steplists = [[None] * s + list(sl)
                     for s, sl in zip(stagger, steplists)]
    plan = []
    n = max(len(s) for s in steplists)
    for i in range(n):
        for c, steps in zip(chains, steplists):
            if i < len(steps) and steps[i] is not None:
                kind, fn, dma = steps[i]
                plan.append(c.sched(kind, fn, dma=dma))
    emit_plan(blk, plan)


def build_program(debug=False):
    nc = bass.Bass("TRN2", target_bir_lowering=False, debug=False)

    x_d = nc.dram_tensor("x", [T, H], F32, kind="ExternalInput").ap()
    wqt_d = nc.dram_tensor("wqt", [H, 1536], F32, kind="ExternalInput").ap()
    wpt_d = nc.dram_tensor("wpt", [H, H], F32, kind="ExternalInput").ap()
    bproj_d = nc.dram_tensor("bproj", [1, H], F32, kind="ExternalInput").ap()
    gwt_d = nc.dram_tensor("gwt", [H, E], F32, kind="ExternalInput").ap()
    ebias_d = nc.dram_tensor("ebias", [128, E], F32, kind="ExternalInput").ap()
    sw1t_d = nc.dram_tensor("sw1t", [H, HID], F32, kind="ExternalInput").ap()
    sw3t_d = nc.dram_tensor("sw3t", [H, HID], F32, kind="ExternalInput").ap()
    sw2t_d = nc.dram_tensor("sw2t", [HID, H], F32, kind="ExternalInput").ap()
    w13_d = nc.dram_tensor("w13", [E, 2 * HID, H], F32, kind="ExternalInput").ap()
    w2_d = nc.dram_tensor("w2", [E, H, HID], F32, kind="ExternalInput").ap()
    out_d = nc.dram_tensor("out", [TOWN, H], F32, kind="ExternalOutput").ap()

    w13t_s = nc.dram_tensor("s_w13t", [E, H, 2 * HID], BF16, kind="Internal").ap()
    w2t_s = nc.dram_tensor("s_w2t", [E, HID, H], BF16, kind="Internal").ap()
    kf_s = nc.dram_tensor("s_kf", [4, HN, T], BF16, kind="Internal").ap()
    vt_s = nc.dram_tensor("s_vt", [4, T, HN], BF16, kind="Internal").ap()
    qf_s = nc.dram_tensor("s_qf", [4, HN, 4096], BF16, kind="Internal").ap()
    xno_s = nc.dram_tensor("s_xno", [TOWN, H], F32, kind="Internal").ap()
    dbg = {}
    if debug:
        for nm, shp, dt in [("d_xno", [TOWN, H], F32),
                            ("d_m", [128, 8 * E], F32),
                            ("d_sc", [128, E], F32),
                            ("d_kf", [HN, T], BF16),
                            ("d_qf", [HN, 4096], BF16),
                            ("d_vt", [T, HN], BF16),
                            ("d_w13t", [H, 2 * HID], BF16),
                            ("d_w2t", [HID, H], BF16)]:
            dbg[nm] = nc.dram_tensor(nm, shp, dt, kind="ExternalOutput").ap()

    st = ExitStack()
    with st:
        def sb(name, shape, dt=F32):
            return st.enter_context(nc.sbuf_tensor(name, shape, dt))

        sems = {k: Sem(st, nc, "s_" + k)
                for k in ["v", "a", "g", "t", "sy", "ve", "ae", "ge"]}
        for q in ["vd", "ad", "gd", "syd"]:
            for sfx in ["j", "0", "1"]:
                sems[q + sfx] = Sem(st, nc, "s_" + q + sfx)
        CHAIN = Chain(sems)

        ph03 = st.enter_context(ExitStack())
        ident_bf = sb("ident_bf", [128, 128], BF16)
        ident_f = sb("ident_f", [128, 128], F32)
        iota_t = sb("iota_t", [128, 128], F32)
        iota_e = sb("iota_e", [128, E], F32)
        iota_b = sb("iota_b", [128, E], F32)
        ebias = sb("ebias_t", [128, E], F32)
        cst = {}
        for nm, val in [("ln2", LN2), ("nln2", -LN2), ("n9ln2", 9 * LN2),
                        ("i2048", 1.0 / 2048.0), ("eps", 1e-5),
                        ("nhalf", -0.5), ("anorm", 1.0 / math.sqrt(HN)),
                        ("i2l2", I2L2), ("tiny", 2.0 ** -100)]:
            cst[nm] = sb("c_" + nm, [128, 1], F32)
        ones1 = ph03.enter_context(nc.sbuf_tensor("ones1", [1, 128], F32))
        bprj = ph03.enter_context(nc.sbuf_tensor("bprj", [1, H], F32))

        LNEXP_SET = 6

        def preload_lnexp(a):
            return a.add_instruction(mybir.InstLoadActFuncSet(
                name=nc.get_next_instruction_name(),
                act_func_set_id=LNEXP_SET, ins=[], outs=[]))

        with nc.Block() as blk:
            C = CHAIN.bind(blk)
            C.g(lambda g: (g.iota(iota_t[:, :], [[1, 128]], channel_multiplier=-1,
                                  allow_small_or_imprecise_dtypes=True),
                           g.iota(iota_e[:, :], [[1, E]], channel_multiplier=0,
                                  allow_small_or_imprecise_dtypes=True))[-1])

            Ev0 = C.edger("v")

            def p0v(v):
                for nm, val in [("ln2", LN2), ("nln2", -LN2), ("n9ln2", 9 * LN2),
                                ("i2048", 1.0 / 2048.0), ("eps", 1e-5),
                                ("nhalf", -0.5), ("anorm", 1.0 / math.sqrt(HN)),
                                ("i2l2", I2L2), ("tiny", 2.0 ** -100)]:
                    v.memset(cst[nm][:, :], val)
                v.memset(ones1[:, :], 1.0)
                Ev0(v, v.tensor_single_scalar(ident_f[:, :], iota_t[:, :], 0.0,
                                              op=ALU.is_equal))
                v.tensor_single_scalar(iota_b[:, :], iota_e[:, :], 1.0e4,
                                       op=ALU.add)
                return v.tensor_copy(ident_bf[:, :], ident_f[:, :])
            C.v(p0v)
            C.dma("sy", lambda sy: sy.dma_start(out=bprj[:, :], in_=bproj_d))
            C.dma("sy", lambda sy: sy.dma_start(out=ebias[:, :], in_=ebias_d))



        def qdq_bufs(prefix, nfree, nbatch, alloc):
            nb = nfree // 32
            d = {"nfree": nfree, "nb": nb, "nbatch": nbatch}
            for nm in ["amx", "e0i", "bex", "aex", "Bi", "Ai", "tc", "cl",
                       "ebf", "jf", "inv"]:
                dt = I32 if nm in ("e0i", "Bi", "Ai") else F32
                d[nm] = alloc(prefix + nm, [128, nb * nbatch], dt)
            d["S1i"] = d["Bi"]
            d["S2i"] = d["Ai"]
            d["j1"] = d["bex"]
            d["j2"] = d["aex"]
            d["ebm"] = d["tc"]
            for nm, dt in [("u", F32), ("ei", I32), ("iv", I32),
                           ("ff", F32)]:
                d[nm] = alloc(prefix + nm, [128, nfree * nbatch], dt)
            d["es"] = d["u"]
            d["ri"] = d["iv"]
            return d

        def qdq_steps(items, bufs, c0, w):
            nfree, NB = bufs["nfree"], bufs["nb"]
            nb = w // 32
            b0 = c0 // 32
            EV = CHAIN.edger("v")
            EA = CHAIN.edger("a")
            EG = CHAIN.edger("g")

            def sm(name, k):
                return bufs[name][:, k * NB + b0:k * NB + b0 + nb]

            def fl(name, k):
                return bufs[name][:, k * nfree + c0:k * nfree + c0 + w]

            def xs(X):
                return X[:, c0:c0 + w]

            ks = list(range(len(items)))
            S = []

            def s1(v):
                for k in ks:
                    X, m, _ = items[k]
                    X3 = xs(X).rearrange("p (a b) -> p a b", b=32)
                    EV(v, v.tensor_reduce(sm("amx", k).unsqueeze(2), X3,
                                          axis=AX.X, op=ALU.max,
                                          apply_absolute_value=True))
                    if m is not None:
                        EV(v, v.tensor_single_scalar(sm("amx", k), sm("amx", k),
                                                     m, op=ALU.mult))
                    ins = v.tensor_single_scalar(sm("amx", k), sm("amx", k),
                                                 2.0 ** -40, op=ALU.max)
                return ins
            S.append(("v", s1, False))

            def s2(a):
                for k in ks:
                    EA(a, a.activation(sm("tc", k), sm("amx", k), AF.Square))
                    EA(a, a.activation(sm("tc", k), sm("tc", k), AF.Ln))
                    ins = a.activation(sm("tc", k), sm("tc", k), AF.Identity,
                                       scale=cst["i2l2"][:, :],
                                       bias=cst["nhalf"][:, :])
                return ins
            S.append(("a", s2, False))

            def s3(v):
                for k in ks:
                    EV(v, v.tensor_single_scalar(sm("tc", k), sm("tc", k),
                                                 -200.0, op=ALU.max))
                    EV(v, v.tensor_scalar(sm("e0i", k), sm("tc", k), MAGIC,
                                          -MAGIC, op0=ALU.add, op1=ALU.add))
                    EV(v, v.tensor_scalar(sm("bex", k), sm("e0i", k), 0.0, 14.0,
                                          op0=ALU.max, op1=ALU.min))
                    EV(v, v.tensor_tensor(sm("aex", k), sm("bex", k),
                                          sm("e0i", k), op=ALU.subtract))
                    ins = v.tensor_single_scalar(sm("aex", k), sm("aex", k),
                                                 14.0, op=ALU.min)
                return ins
            S.append(("v", s3, False))

            def s4(a):
                for k in ks:
                    EA(a, a.activation(sm("cl", k), sm("bex", k), AF.Exp,
                                       scale=cst["ln2"][:, :]))
                    ins = a.activation(sm("jf", k), sm("aex", k), AF.Exp,
                                       scale=cst["ln2"][:, :])
                return ins
            S.append(("a", s4, False))

            def s5(v):
                for k in ks:
                    X, m, _ = items[k]
                    EV(v, v.tensor_scalar(sm("Bi", k), sm("cl", k), MAGIC,
                                          -MAGIC, op0=ALU.add, op1=ALU.add))
                    EV(v, v.tensor_scalar(sm("Ai", k), sm("jf", k), MAGIC,
                                          -MAGIC, op0=ALU.add, op1=ALU.add))
                    EV(v, v.tensor_tensor(sm("tc", k), sm("amx", k),
                                          sm("Ai", k), op=ALU.mult))
                    EV(v, v.tensor_tensor(sm("cl", k), sm("tc", k), sm("Bi", k),
                                          op=ALU.is_lt))
                    EV(v, v.scalar_tensor_tensor(sm("jf", k), sm("tc", k), 0.5,
                                                 sm("Bi", k), op0=ALU.mult,
                                                 op1=ALU.is_ge))
                    EV(v, v.tensor_tensor(sm("ebf", k), sm("e0i", k),
                                          sm("cl", k), op=ALU.subtract))
                    EV(v, v.tensor_tensor(sm("ebf", k), sm("ebf", k),
                                          sm("jf", k), op=ALU.add))
                    EV(v, v.tensor_scalar(sm("jf", k), sm("ebf", k), -1.0, 8.0,
                                          op0=ALU.mult, op1=ALU.add))
                    EV(v, v.tensor_scalar(sm("jf", k), sm("jf", k), 0.0, 28.0,
                                          op0=ALU.max, op1=ALU.min))
                    EV(v, v.tensor_single_scalar(sm("j1", k), sm("jf", k), 14.0,
                                                 op=ALU.min))
                    ins = v.tensor_tensor(sm("j2", k), sm("jf", k), sm("j1", k),
                                          op=ALU.subtract)
                return ins
            S.append(("v", s5, False))

            def s6(a):
                for k in ks:
                    EA(a, a.activation(sm("cl", k), sm("j1", k), AF.Exp,
                                       scale=cst["ln2"][:, :]))
                    ins = a.activation(sm("jf", k), sm("j2", k), AF.Exp,
                                       scale=cst["ln2"][:, :])
                return ins
            S.append(("a", s6, False))

            def s7(v):
                for k in ks:
                    X, m, _ = items[k]
                    EV(v, v.tensor_scalar(sm("S1i", k), sm("cl", k), MAGIC,
                                          -MAGIC, op0=ALU.add, op1=ALU.add))
                    EV(v, v.tensor_scalar(sm("S2i", k), sm("jf", k), MAGIC,
                                          -MAGIC, op0=ALU.add, op1=ALU.add))
                    EV(v, v.tensor_tensor(sm("inv", k), sm("S1i", k),
                                          sm("S2i", k), op=ALU.mult))
                    if m is not None:
                        EV(v, v.tensor_single_scalar(sm("inv", k), sm("inv", k),
                                                     m, op=ALU.mult))
                    ins = v.tensor_single_scalar(sm("ebm", k), sm("ebf", k),
                                                 -11.0, op=ALU.add)
                return ins
            S.append(("v", s7, False))

            def s8(v):
                for k in ks:
                    X, m, _ = items[k]
                    uT = fl("u", k).rearrange("p (a b) -> p b a", b=32)
                    XT = xs(X).rearrange("p (a b) -> p b a", b=32)
                    invB = sm("inv", k).unsqueeze(1).broadcast_to([128, 32, nb])
                    ins = v.tensor_tensor(uT, XT, invB, op=ALU.mult)
                return ins
            S.append(("v", s8, False))

            def s9(g):
                for k in ks:
                    ins = g.tensor_scalar(fl("u", k), fl("u", k), 448.0, -448.0,
                                          op0=ALU.min, op1=ALU.max)
                return ins
            S.append(("g", s9, False))

            def s10(a):
                for k in ks:
                    EA(a, a.activation(fl("ff", k), fl("u", k), AF.Square))
                    ins = a.activation(fl("ff", k), fl("ff", k), AF.Ln,
                                       bias=cst["tiny"][:, :])
                return ins
            S.append(("a", s10, False))

            def s11(g):
                for k in ks:
                    ins = g.tensor_scalar(fl("ff", k), fl("ff", k), I2L2, -0.5,
                                          op0=ALU.mult, op1=ALU.add)
                return ins
            S.append(("g", s11, False))

            def s12(v):
                for k in ks:
                    EV(v, v.tensor_scalar(fl("ff", k), fl("ff", k), -6.499,
                                          MAGIC, op0=ALU.max, op1=ALU.add))
                    ins = v.tensor_single_scalar(fl("ei", k), fl("ff", k),
                                                 -MAGIC, op=ALU.add)
                return ins
            S.append(("v", s12, False))

            def s13(a):
                for k in ks:
                    ins = a.activation(fl("ff", k), fl("ei", k), AF.Exp,
                                       scale=cst["nln2"][:, :],
                                       bias=cst["n9ln2"][:, :])
                return ins
            S.append(("a", s13, False))

            def s14(v):
                for k in ks:
                    EV(v, v.tensor_scalar(fl("iv", k), fl("ff", k), MAGIC,
                                          -MAGIC, op0=ALU.add, op1=ALU.add))
                    EV(v, v.scalar_tensor_tensor(fl("ff", k), fl("iv", k),
                                                 2.0 ** -6, fl("u", k),
                                                 op0=ALU.mult, op1=ALU.mult))
                    EV(v, v.tensor_scalar(fl("ri", k), fl("ff", k), MAGIC,
                                          -MAGIC, op0=ALU.add, op1=ALU.add))
                    esT = fl("es", k).rearrange("p (a b) -> p b a", b=32)
                    eiT = fl("ei", k).rearrange("p (a b) -> p b a", b=32)
                    ebmB = sm("ebm", k).unsqueeze(1).broadcast_to([128, 32, nb])
                    ins = v.tensor_tensor(esT, eiT, ebmB, op=ALU.add)
                return ins
            S.append(("v", s14, False))

            def s16(a):
                for k in ks:
                    ins = a.activation(fl("es", k), fl("es", k), AF.Exp,
                                       scale=cst["ln2"][:, :])
                return ins
            S.append(("a", s16, False))

            def s17(v):
                for k in ks:
                    X, m, out_bf = items[k]
                    ins = v.tensor_tensor(out_bf[:, c0:c0 + w], fl("ri", k),
                                          fl("es", k), op=ALU.mult)
                return ins
            S.append(("v", s17, False))
            return S

        def emit_qdq(C, items, bufs):
            C.run(qdq_steps(items, bufs, 0, bufs["nfree"]))

        with ExitStack() as ph, nc.Block() as blk:
            C = CHAIN.bind(blk)

            def p1alloc(name, shape, dt=F32):
                return ph.enter_context(nc.sbuf_tensor(name, shape, dt))
            wld = [p1alloc(f"p1w{i}", [128, H]) for i in range(4)]
            dq = [p1alloc(f"p1d{i}", [128, H], BF16) for i in range(4)]
            stg = [p1alloc(f"p1s{i}", [128, H], BF16) for i in range(2)]
            qb = [qdq_bufs(f"p1q{i}", H, 2, p1alloc) for i in range(2)]
            tps = [ph.enter_context(nc.psum_tensor(f"p1tp{i}", [128, H], BF16))
                   for i in range(2)]

            tiles = []
            for e in range(E):
                for ot in range(2 * HID // 128):
                    tiles.append((w13_d[e, ot * 128:(ot + 1) * 128, :],
                                  w13t_s[e].rearrange("(ic r) o -> r ic o", r=128)
                                  [:, :, ot * 128:(ot + 1) * 128]))
                for ot in range(H // 128):
                    tiles.append((w2_d[e, ot * 128:(ot + 1) * 128, :],
                                  w2t_s[e].rearrange("(ic r) o -> r ic o", r=128)
                                  [:, :, ot * 128:(ot + 1) * 128]))

            def tile_steps(i, src_ap, dst_ap):
                tr = i % 2
                S = [("sy", lambda sy: sy.dma_start(out=wld[tr][:, :],
                                                    in_=src_ap), True)]
                S += qdq_steps([(wld[tr][:, :], None, dq[tr][:, :])], qb[tr],
                               0, H)

                def trf(t):
                    for ic in range(16):
                        ins = t.matmul(tps[tr][:, ic * 128:(ic + 1) * 128],
                                       dq[tr][:, ic * 128:(ic + 1) * 128],
                                       ident_bf[:, :], is_transpose=True,
                                       start=(ic % 8 == 0), stop=(ic % 8 == 7))
                    return ins
                S.append(("t", trf, False))
                S.append(("a", lambda a: a.copy(stg[tr][:, :], tps[tr][:, :]),
                          False))
                S.append(("a", lambda a: a.dma_start(
                    out=dst_ap,
                    in_=stg[tr].rearrange("r (ic o) -> r ic o", o=128)), True))
                return S

            tracks = [C.fork("0"), C.fork("1")]
            for base in range(0, len(tiles), 2):
                pair = tiles[base:base + 2]
                lists = [tile_steps(base + j, s, d)
                         for j, (s, d) in enumerate(pair)]

                interleave(blk, tracks, lists)
            C.prev = tracks[0].join(tracks[1]).prev

        with ExitStack() as ph, nc.Block() as blk:
            C = CHAIN.bind(blk)

            def p2a(name, shape, dt=F32):
                return ph.enter_context(nc.sbuf_tensor(name, shape, dt))
            xld = [p2a(f"p2x{i}", [128, H]) for i in range(2)]
            xnf = [p2a(f"p2xn{i}", [128, H]) for i in range(2)]
            xnb = [p2a(f"p2xnb{i}", [128, H], BF16) for i in range(2)]
            sqd = [p2a(f"p2sqd{i}", [128, H]) for i in range(2)]
            kst = [p2a(f"p2k{i}", [128, 512], BF16) for i in range(2)]
            mcol = [p2a(f"p2mc{i}", [128, 1]) for i in range(2)]
            rstd = [p2a(f"p2rs{i}", [128, 1]) for i in range(2)]
            nmr = [p2a(f"p2nmr{i}", [128, 1]) for i in range(2)]
            ssd = [p2a(f"p2ssd{i}", [128, 1]) for i in range(2)]
            wqbf = p2a("p2wq", [128, 16 * 1536], BF16)
            xnfm = p2a("p2fm", [128, 16 * 512], BF16)
            tpx = [ph.enter_context(nc.psum_tensor(f"p2tp{i}", [128, H], BF16))
                   for i in range(2)]
            pmm = [ph.enter_context(nc.psum_tensor(f"p2pm{i}", [128, 512],
                                                   F32)) for i in range(2)]

            C.dma("g", lambda g: g.dma_start(
                out=wqbf.rearrange("r (ic o) -> r ic o", o=1536),
                in_=wqt_d.rearrange("(ic r) o -> r ic o", r=128)))
            wq3 = wqbf.rearrange("r (ic o) -> r ic o", o=1536)
            EV2 = CHAIN.edger("v")
            EA2 = CHAIN.edger("a")

            def ln_steps(i):
                tr = i % 2
                tt = i % 4
                S = [("sy", lambda sy: sy.dma_start(
                    out=xld[tr][:, :], in_=x_d[i * 128:(i + 1) * 128, :]),
                    True)]

                def lnv(v):
                    EV2(v, v.tensor_reduce(mcol[tr][:, :], xld[tr][:, :],
                                           axis=AX.X, op=ALU.add))
                    return v.tensor_single_scalar(
                        mcol[tr][:, :], mcol[tr][:, :], -1.0 / 2048.0,
                        op=ALU.mult)
                S.append(("v", lnv, False))

                def lna(a):
                    EA2(a, a.activation(sqd[tr][:, :], xld[tr][:, :],
                                        AF.Square, bias=mcol[tr][:, :],
                                        accum_out=ssd[tr][:, :]))
                    EA2(a, a.activation(rstd[tr][:, :], ssd[tr][:, :], AF.Ln,
                                        scale=cst["i2048"][:, :],
                                        bias=cst["eps"][:, :]))
                    return a.activation(rstd[tr][:, :], rstd[tr][:, :], AF.Exp,
                                        scale=cst["nhalf"][:, :])
                S.append(("a", lna, False))
                S.append(("v", lambda v: v.tensor_tensor(
                    nmr[tr][:, :], mcol[tr][:, :], rstd[tr][:, :],
                    op=ALU.mult), False))
                if i < 8:
                    def xa(a):
                        EA2(a, a.activation(xnf[tr][:, :], xld[tr][:, :],
                                            AF.Identity, scale=rstd[tr][:, :],
                                            bias=nmr[tr][:, :]))
                        return a.copy(xnb[tr][:, :], xnf[tr][:, :])
                    S.append(("a", xa, False))
                    S.append(("g", lambda g: g.dma_start(
                        out=xno_s[i * 128:(i + 1) * 128, :],
                        in_=xnf[tr][:, :]), True))
                else:
                    S.append(("a", lambda a: a.activation(
                        xnb[tr][:, :], xld[tr][:, :], AF.Identity,
                        scale=rstd[tr][:, :], bias=nmr[tr][:, :]), False))

                def trx(t):
                    for ic in range(16):
                        ins = t.matmul(tpx[tr][:, ic * 128:(ic + 1) * 128],
                                       xnb[tr][:, ic * 128:(ic + 1) * 128],
                                       ident_bf[:, :], is_transpose=True,
                                       start=(ic % 8 == 0), stop=(ic % 8 == 7))
                    return ins
                S.append(("t", trx, False))

                def evx(a):
                    dst = xnfm.rearrange("r (ic c) -> r ic c", c=512)[
                        :, :, tt * 128:(tt + 1) * 128]
                    return a.copy(dst, tpx[tr].rearrange(
                        "r (ic c) -> r ic c", c=128))
                S.append(("a", evx, False))
                return S

            def out_steps(kind, idx, tr, gi, b2p, g4):
                S = []

                def mmf(t):
                    if kind in ("k", "q"):
                        col0 = idx * 128 + (512 if kind == "k" else 0)
                        for ic in range(16):
                            ins = t.matmul(pmm[tr][:, :],
                                           wq3[:, ic, col0:col0 + 128],
                                           xnfm[:, ic * 512:(ic + 1) * 512],
                                           start=(ic == 0), stop=(ic == 15))
                    else:
                        for ic in range(16):
                            ins = t.matmul(
                                pmm[tr][:, :],
                                xnfm[:, ic * 512 + idx * 128:
                                     ic * 512 + (idx + 1) * 128],
                                wq3[:, ic, 1024:1536],
                                start=(ic == 0), stop=(ic == 15))
                    return ins
                S.append(("t", mmf, False))
                S.append(("v", lambda v: v.tensor_copy(kst[tr][:, :],
                                                       pmm[tr][:, :]), False))

                def wrk(a):
                    if kind == "k":
                        return a.dma_start(
                            out=kf_s[idx][:, gi * 512:(gi + 1) * 512],
                            in_=kst[tr][:, :])
                    if kind == "q":
                        return a.dma_start(
                            out=qf_s[idx][:, b2p * 1024 + g4 * 512:
                                          b2p * 1024 + (g4 + 1) * 512],
                            in_=kst[tr][:, :])
                    return a.dma_start(
                        out=vt_s.rearrange("h t c -> t h c")[
                            gi * 512 + idx * 128:gi * 512 + (idx + 1) * 128],
                        in_=kst[tr].rearrange("r (h c) -> r h c", c=128))
                S.append(("a", wrk, True))
                return S

            for gi in range(16):
                tracks = [C.fork("0"), C.fork("1")]
                for q in range(2):
                    lists = [ln_steps(gi * 4 + 2 * q),
                             ln_steps(gi * 4 + 2 * q + 1)]
                    interleave(blk, tracks, lists, stagger=[0, 3])
                C.prev = tracks[0].join(tracks[1]).prev

                b2p, g4 = gi // 4, gi % 4
                outs = [("k", hi) for hi in range(4)]
                outs += [("v", s4) for s4 in range(4)]
                if g4 < 2:
                    outs += [("q", hi) for hi in range(4)]
                tracks = [C.fork("0"), C.fork("1")]
                for q in range(0, len(outs), 2):
                    lists = [out_steps(*outs[q], 0, gi, b2p, g4)]
                    if q + 1 < len(outs):
                        lists.append(out_steps(*outs[q + 1], 1, gi, b2p, g4))
                    interleave(blk, tracks[:len(lists)], lists,
                               stagger=[0, 1][:len(lists)])
                C.prev = tracks[0].join(tracks[1]).prev

        with ExitStack() as ph, nc.Block() as blk:
            C = CHAIN.bind(blk)
            ksb = ph.enter_context(nc.sbuf_tensor("p3k", [128, 2048], BF16))
            vsb = ph.enter_context(nc.sbuf_tensor("p3v", [128, 16 * 128], BF16))
            qsb = ph.enter_context(nc.sbuf_tensor("p3q", [128, 1024], BF16))
            pbf = ph.enter_context(nc.sbuf_tensor("p3p", [128, 2048], BF16))
            p2b = ph.enter_context(nc.sbuf_tensor("p3p2", [128, 2048], BF16))
            pts = ph.enter_context(nc.sbuf_tensor("p3pt", [128, 16 * 512], BF16))
            ctx = ph.enter_context(nc.sbuf_tensor("p3ctx", [128, 16 * 1024], BF16))
            nmx = ph.enter_context(nc.sbuf_tensor("p3nm", [128, 1], F32))
            ssum = ph.enter_context(nc.sbuf_tensor("p3ss", [128, 1], F32))
            rec = ph.enter_context(nc.sbuf_tensor("p3rc", [128, 1], F32))
            ostg = ph.enter_context(nc.sbuf_tensor("p3o", [128, 512], F32))
            wpbf = ph.enter_context(nc.sbuf_tensor("p3wp", [128, 16 * 2048], BF16))
            psc = ph.enter_context(nc.psum_tensor("p3sc", [128, 2048], F32))
            ppt = ph.enter_context(nc.psum_tensor("p3ppt", [128, 2048], BF16))
            pcx = ph.enter_context(nc.psum_tensor("p3cx", [128, 512], F32))

            C.dma("g", lambda g: g.dma_start(
                out=wpbf.rearrange("r (ic o) -> r ic o", o=2048),
                in_=wpt_d.rearrange("(ic r) o -> r ic o", r=128)))
            wp3 = wpbf.rearrange("r (ic o) -> r ic o", o=2048)

            for np_ in range(16):
                bqp, hi = np_ // 4, np_ % 4
                C.dma("sy", lambda sy, hi=hi, bqp=bqp: sy.dma_start(
                    out=ksb[:, :], in_=kf_s[hi][:, bqp * 2048:(bqp + 1) * 2048]))
                C.dma("sy", lambda sy, hi=hi, bqp=bqp: sy.dma_start(
                    out=vsb.rearrange("r (j c) -> r j c", c=128),
                    in_=vt_s[hi].rearrange("(j r) c -> r j c", r=128)
                    [:, bqp * 16:(bqp + 1) * 16, :]))
                C.dma("sy", lambda sy, hi=hi, bqp=bqp: sy.dma_start(
                    out=qsb[:, :], in_=qf_s[hi][:, bqp * 1024:(bqp + 1) * 1024]))

                for qt in range(8):
                    def qk(t, qt=qt):
                        for j in range(4):
                            ins = t.matmul(psc[:, j * 512:(j + 1) * 512],
                                           qsb[:, qt * 128:(qt + 1) * 128],
                                           ksb[:, j * 512:(j + 1) * 512],
                                           start=True, stop=True)
                        return ins
                    C.t(qk)

                    def nmax(v):
                        C.edger("v")(v, v.tensor_reduce(
                            nmx[:, :], psc[:, :], axis=AX.X,
                            op=ALU.max, negate=True))
                        return v.tensor_single_scalar(
                            nmx[:, :], nmx[:, :], 1.0 / math.sqrt(HN),
                            op=ALU.mult)
                    C.v(nmax)
                    C.a(lambda a: a.activation(pbf[:, :], psc[:, :], AF.Exp,
                                               scale=cst["anorm"][:, :],
                                               bias=nmx[:, :],
                                               accum_out=ssum[:, :]))
                    C.v(lambda v: v.reciprocal(rec[:, :], ssum[:, :]))
                    C.v(lambda v: v.tensor_single_scalar(p2b[:, :], pbf[:, :],
                                                         rec[:, :], op=ALU.mult))

                    def trp(t):
                        for j in range(16):
                            ins = t.matmul(ppt[:, j * 128:(j + 1) * 128],
                                           p2b[:, j * 128:(j + 1) * 128],
                                           ident_bf[:, :], is_transpose=True,
                                           start=(j % 8 == 0), stop=(j % 8 == 7))
                        return ins
                    C.t(trp)

                    def evp(v, qt=qt):
                        dst = pts.rearrange("r (j c) -> r j c", c=512)[
                            :, :, (qt % 4) * 128:(qt % 4 + 1) * 128]
                        return v.tensor_copy(dst, ppt.rearrange(
                            "r (j c) -> r j c", c=128))
                    C.v(evp)

                    if qt % 4 == 3:
                        def av(t):
                            for j in range(16):
                                ins = t.matmul(pcx[:, :],
                                               vsb[:, j * 128:(j + 1) * 128],
                                               pts[:, j * 512:(j + 1) * 512],
                                               start=(j == 0), stop=(j == 15))
                            return ins
                        C.t(av)

                        def evc(a, qt=qt, np_=np_):
                            dst = ctx[:, np_ * 1024 + (qt // 4) * 512:
                                      np_ * 1024 + (qt // 4 + 1) * 512]
                            return a.copy(dst, pcx[:, :])
                        C.a(evc)

            for qt in range(8):
                for dc in range(4):
                    def wpj(t, qt=qt, dc=dc):
                        for n2 in range(16):
                            t.matmul(pcx[:, :],
                                     ctx[:, n2 * 1024 + qt * 128:
                                         n2 * 1024 + (qt + 1) * 128],
                                     wp3[:, n2, dc * 512:(dc + 1) * 512],
                                     start=(n2 == 0), stop=False)
                        return t.matmul(pcx[:, :], ones1[0:1, 0:128],
                                        bprj[0:1, dc * 512:(dc + 1) * 512],
                                        start=False, stop=True)
                    C.t(wpj)
                    C.a(lambda a: a.copy(ostg[:, :], pcx[:, :]))
                    C.dma("g", lambda g, qt=qt, dc=dc: g.dma_start(
                        out=out_d[qt * 128:(qt + 1) * 128,
                                  dc * 512:(dc + 1) * 512],
                        in_=ostg[:, :]))

        ph03.close()
        mker = sb("mker", [128, 8 * E], F32)
        ph45 = st.enter_context(ExitStack())
        xnbfm = ph45.enter_context(nc.sbuf_tensor(
            "xnbfm", [128, 8 * H], BF16))
        with ExitStack() as ph, nc.Block() as blk:
            C = CHAIN.bind(blk)
            xot = ph.enter_context(nc.sbuf_tensor("p4x", [128, H], F32))
            gwsb = ph.enter_context(nc.sbuf_tensor("p4gw", [128, 16 * E], F32))
            xn32 = ph.enter_context(nc.sbuf_tensor("p4fm", [128, 2048], F32))
            sc8 = ph.enter_context(nc.sbuf_tensor("p4sc", [128, E], F32))
            bb8 = ph.enter_context(nc.sbuf_tensor("p4bb", [128, E], F32))
            t8 = ph.enter_context(nc.sbuf_tensor("p4t8", [128, E], F32))
            m1 = ph.enter_context(nc.sbuf_tensor("p4m1", [128, E], F32))
            m2 = ph.enter_context(nc.sbuf_tensor("p4m2", [128, E], F32))
            mx1 = ph.enter_context(nc.sbuf_tensor("p4mx", [128, 1], F32))
            sv1 = ph.enter_context(nc.sbuf_tensor("p4sv1", [128, 1], F32))
            sv2 = ph.enter_context(nc.sbuf_tensor("p4sv2", [128, 1], F32))
            den = ph.enter_context(nc.sbuf_tensor("p4den", [128, 1], F32))
            tpf = ph.enter_context(nc.psum_tensor("p4tp", [128, 2048], F32))
            pg = ph.enter_context(nc.psum_tensor("p4pg", [128, E], F32))

            C.dma("sy", lambda sy: sy.dma_start(
                out=gwsb.rearrange("r (ic e) -> r ic e", e=E),
                in_=gwt_d.rearrange("(ic r) e -> r ic e", r=128)))
            gw3 = gwsb.rearrange("r (ic e) -> r ic e", e=E)

            for tt in range(8):
                C.dma("sy", lambda sy, tt=tt: sy.dma_start(
                    out=xot[:, :], in_=xno_s[tt * 128:(tt + 1) * 128, :]))

                def trf(t):
                    for ic in range(16):
                        ins = t.matmul(tpf[:, ic * 128:(ic + 1) * 128],
                                       xot[:, ic * 128:(ic + 1) * 128],
                                       ident_f[:, :], is_transpose=True,
                                       start=(ic % 4 == 0), stop=(ic % 4 == 3))
                    return ins
                C.t(trf)

                def evf(a, tt=tt):
                    a.copy(xn32[:, :], tpf[:, :])
                    return a.copy(xnbfm[:, tt * H:(tt + 1) * H], tpf[:, :])
                C.a(evf)

                def gmm(t):
                    for ic in range(16):
                        ins = t.matmul(pg[:, :], xn32[:, ic * 128:(ic + 1) * 128],
                                       gw3[:, ic, :],
                                       start=(ic == 0), stop=(ic == 15))
                    return ins
                C.t(gmm)
                C.a(lambda a: a.activation(sc8[:, :], pg[:, :], AF.Sigmoid))

                def top2(v, tt=tt):
                    BIG = 1.0e4
                    Ev = C.edger("v")
                    Mt = mker[:, tt * E:(tt + 1) * E]
                    Ev(v, v.tensor_tensor(bb8[:, :], sc8[:, :], ebias[:, :],
                                          op=ALU.add))
                    Ev(v, v.tensor_reduce(mx1[:, :], bb8[:, :], axis=AX.X,
                                          op=ALU.max))
                    Ev(v, v.tensor_single_scalar(m1[:, :], bb8[:, :], mx1[:, :],
                                                 op=ALU.is_ge))
                    Ev(v, v.scalar_tensor_tensor(t8[:, :], m1[:, :], -BIG,
                                                 iota_b[:, :], op0=ALU.mult,
                                                 op1=ALU.add))
                    Ev(v, v.tensor_reduce(mx1[:, :], t8[:, :], axis=AX.X,
                                          op=ALU.min))
                    Ev(v, v.tensor_single_scalar(m1[:, :], iota_e[:, :],
                                                 mx1[:, :], op=ALU.is_equal))
                    Ev(v, v.tensor_tensor(t8[:, :], m1[:, :], sc8[:, :],
                                          op=ALU.mult))
                    Ev(v, v.tensor_reduce(sv1[:, :], t8[:, :], axis=AX.X,
                                          op=ALU.add))
                    Ev(v, v.scalar_tensor_tensor(t8[:, :], m1[:, :], -BIG,
                                                 bb8[:, :], op0=ALU.mult,
                                                 op1=ALU.add))
                    Ev(v, v.tensor_reduce(mx1[:, :], t8[:, :], axis=AX.X,
                                          op=ALU.max))
                    Ev(v, v.tensor_single_scalar(m2[:, :], t8[:, :], mx1[:, :],
                                                 op=ALU.is_ge))
                    Ev(v, v.scalar_tensor_tensor(t8[:, :], m2[:, :], -BIG,
                                                 iota_b[:, :], op0=ALU.mult,
                                                 op1=ALU.add))
                    Ev(v, v.tensor_reduce(mx1[:, :], t8[:, :], axis=AX.X,
                                          op=ALU.min))
                    Ev(v, v.tensor_single_scalar(m2[:, :], iota_e[:, :],
                                                 mx1[:, :], op=ALU.is_equal))
                    Ev(v, v.tensor_tensor(t8[:, :], m2[:, :], sc8[:, :],
                                          op=ALU.mult))
                    Ev(v, v.tensor_reduce(sv2[:, :], t8[:, :], axis=AX.X,
                                          op=ALU.add))
                    Ev(v, v.tensor_tensor(den[:, :], sv1[:, :], sv2[:, :],
                                          op=ALU.add))
                    Ev(v, v.tensor_single_scalar(den[:, :], den[:, :], 1e-20,
                                                 op=ALU.add))
                    Ev(v, v.reciprocal(den[:, :], den[:, :]))
                    Ev(v, v.tensor_single_scalar(sv1[:, :], sv1[:, :],
                                                 den[:, :], op=ALU.mult))
                    Ev(v, v.tensor_single_scalar(sv2[:, :], sv2[:, :],
                                                 den[:, :], op=ALU.mult))
                    Ev(v, v.tensor_single_scalar(m1[:, :], m1[:, :], sv1[:, :],
                                                 op=ALU.mult))
                    Ev(v, v.tensor_single_scalar(m2[:, :], m2[:, :], sv2[:, :],
                                                 op=ALU.mult))
                    return v.tensor_tensor(Mt, m1[:, :], m2[:, :], op=ALU.add)
                C.v(top2)

            if debug:
                C.dma("a", lambda a: a.dma_start(out=dbg["d_m"], in_=mker[:, :]))
                C.dma("a", lambda a: a.dma_start(out=dbg["d_sc"], in_=sc8[:, :]))

        with ExitStack() as ph, nc.Block() as blk:
            C = CHAIN.bind(blk)
            w1c = ph.enter_context(nc.sbuf_tensor("p5w1", [128, 16 * 512], BF16))
            w3c = ph.enter_context(nc.sbuf_tensor("p5w3", [128, 16 * 512], BF16))
            w2c = ph.enter_context(nc.sbuf_tensor("p5w2", [128, 16 * 512], BF16))
            s1f = [ph.enter_context(nc.sbuf_tensor(f"p5s1{i}", [128, 512], F32))
                   for i in range(2)]
            h3f = [ph.enter_context(nc.sbuf_tensor(f"p5h3{i}", [128, 512], F32))
                   for i in range(2)]
            hhb = [ph.enter_context(nc.sbuf_tensor(f"p5hh{i}", [128, 512], BF16))
                   for i in range(2)]
            hhf = ph.enter_context(nc.sbuf_tensor("p5hf", [128, 8 * 2048], BF16))
            ostg = s1f
            ph1 = [ph.enter_context(nc.psum_tensor(f"p5p1{i}", [128, 512], F32))
                   for i in range(2)]
            ph3 = [ph.enter_context(nc.psum_tensor(f"p5p3{i}", [128, 512], F32))
                   for i in range(2)]
            ptp = [ph.enter_context(nc.psum_tensor(f"p5tp{i}", [128, 512], BF16))
                   for i in range(2)]
            pyo = [ph.enter_context(nc.psum_tensor(f"p5py{i}", [128, 512], F32))
                   for i in range(2)]

            for hc in range(4):
                C.dma("g", lambda sy, hc=hc: sy.dma_start(
                    out=w1c.rearrange("r (ic o) -> r ic o", o=512),
                    in_=sw1t_d.rearrange("(ic r) o -> r ic o", r=128)
                    [:, :, hc * 512:(hc + 1) * 512]))
                C.dma("g", lambda sy, hc=hc: sy.dma_start(
                    out=w3c.rearrange("r (ic o) -> r ic o", o=512),
                    in_=sw3t_d.rearrange("(ic r) o -> r ic o", r=128)
                    [:, :, hc * 512:(hc + 1) * 512]))
                tracks = [C.fork("0"), C.fork("1")]

                def shsteps(tt, hc=hc):
                    tr = tt % 2
                    S = []

                    def mm13(t, tt=tt, tr=tr):
                        for wch, pp in ((w1c, ph1[tr]), (w3c, ph3[tr])):
                            w3_ = wch.rearrange("r (ic o) -> r ic o", o=512)
                            for ic in range(16):
                                ins = t.matmul(
                                    pp[:, :],
                                    xnbfm[:, tt * H + ic * 128:
                                          tt * H + (ic + 1) * 128],
                                    w3_[:, ic, :],
                                    start=(ic == 0), stop=(ic == 15))
                        return ins
                    S.append(("t", mm13, False))
                    S.append(("a", lambda a, tr=tr: a.activation(
                        s1f[tr][:, :], ph1[tr][:, :], AF.Silu), False))
                    S.append(("v", lambda v, tr=tr: v.tensor_copy(
                        h3f[tr][:, :], ph3[tr][:, :]), False))
                    S.append(("g", lambda g, tr=tr: g.tensor_tensor(
                        hhb[tr][:, :], s1f[tr][:, :], h3f[tr][:, :],
                        op=ALU.mult), False))

                    def trh(t, tr=tr):
                        for j in range(4):
                            ins = t.matmul(ptp[tr][:, j * 128:(j + 1) * 128],
                                           hhb[tr][:, j * 128:(j + 1) * 128],
                                           ident_bf[:, :], is_transpose=True,
                                           start=(j == 0), stop=(j == 3))
                        return ins
                    S.append(("t", trh, False))
                    S.append(("v", lambda v, hc=hc, tt=tt, tr=tr: v.tensor_copy(
                        hhf[:, tt * 2048 + hc * 512:tt * 2048 + (hc + 1) * 512],
                        ptp[tr][:, :]), False))
                    return S

                for q in range(4):
                    lists = [shsteps(2 * q), shsteps(2 * q + 1)]
                    interleave(blk, tracks, lists, stagger=[0, 3])
                C.prev = tracks[0].join(tracks[1]).prev

            for dc in range(4):
                C.dma("g", lambda sy, dc=dc: sy.dma_start(
                    out=w2c.rearrange("r (ic o) -> r ic o", o=512),
                    in_=sw2t_d.rearrange("(ic r) o -> r ic o", r=128)
                    [:, :, dc * 512:(dc + 1) * 512]))
                tracks = [C.fork("0"), C.fork("1")]

                def sh2(tt, dc=dc):
                    tr = tt % 2
                    S = []

                    def mm2(t, tt=tt, tr=tr):
                        w3_ = w2c.rearrange("r (ic o) -> r ic o", o=512)
                        for j in range(16):
                            ins = t.matmul(
                                pyo[tr][:, :],
                                hhf[:, tt * 2048 + j * 128:
                                    tt * 2048 + (j + 1) * 128],
                                w3_[:, j, :],
                                start=(j == 0), stop=(j == 15))
                        return ins
                    S.append(("t", mm2, False))
                    S.append(("a", lambda a, tr=tr: a.copy(ostg[tr][:, :],
                                                           pyo[tr][:, :]),
                              False))
                    S.append(("g", lambda g, dc=dc, tt=tt, tr=tr: g.dma_start(
                        out=out_d[tt * 128:(tt + 1) * 128,
                                  dc * 512:(dc + 1) * 512],
                        in_=ostg[tr][:, :], accum_op=ALU.add), True))
                    return S

                for q in range(4):
                    lists = [sh2(2 * q), sh2(2 * q + 1)]
                    interleave(blk, tracks, lists, stagger=[0, 1])
                C.prev = tracks[0].join(tracks[1]).prev

        ph45.close()

        with ExitStack() as ph, nc.Block() as blk:
            C = CHAIN.bind(blk)

            def pb(name, shape, dt=F32):
                return ph.enter_context(nc.sbuf_tensor(name, shape, dt))
            qbx = [qdq_bufs(f"p6q{i}", H, 1, pb) for i in range(2)]
            qbh = []
            for i in range(2):
                d = dict(qbx[i])
                d["nfree"], d["nb"] = 1024, 32
                qbh.append(d)
            xqf = pb("p6xqf", [128, 8 * H], BF16)
            w13c = [pb(f"p6w13{i}", [128, 16 * 512], BF16) for i in range(2)]

            s1f = [pb(f"p6s1{i}", [128, 512]) for i in range(2)]
            h3f = [pb(f"p6h3{i}", [128, 512]) for i in range(2)]
            hf = [pb(f"p6h{i}", [128, 1024]) for i in range(2)]
            hqb = [pb(f"p6hq{i}", [128, 1024], BF16) for i in range(2)]
            hqf = pb("p6hqf", [128, 8 * H], BF16)
            ostg = s1f

            for e in range(E):
                with ExitStack() as pp:
                    tpb = [pp.enter_context(nc.psum_tensor(
                        f"p6tp{e}_{i}", [128, H], BF16)) for i in range(2)]
                    xot = [pp.enter_context(nc.sbuf_tensor(
                        f"p6x{e}_{i}", [128, H], F32)) for i in range(2)]
                    xqb = [hqf[:, (6 + i) * H:(7 + i) * H] for i in range(2)]
                    tracks = [C.fork("0"), C.fork("1")]

                    def xsteps(tt, e=e):
                        tr = tt % 2
                        S = [("sy", lambda sy, tt=tt: sy.dma_start(
                            out=xot[tr][:, :],
                            in_=xno_s[tt * 128:(tt + 1) * 128, :]), True)]
                        S += qdq_steps([(xot[tr][:, :],
                                         mker[:, tt * E + e:tt * E + e + 1],
                                         xqb[tr])], qbx[tr], 0, H)

                        def trf(t, tr=tr):
                            for ic in range(16):
                                ins = t.matmul(
                                    tpb[tr][:, ic * 128:(ic + 1) * 128],
                                    xqb[tr][:, ic * 128:(ic + 1) * 128],
                                    ident_bf[:, :], is_transpose=True,
                                    start=(ic % 8 == 0), stop=(ic % 8 == 7))
                            return ins
                        S.append(("t", trf, False))
                        S.append(("a", lambda a, tt=tt, tr=tr: a.copy(
                            xqf[:, tt * H:(tt + 1) * H], tpb[tr][:, :]), False))
                        return S

                    for q in range(4):
                        lists = [xsteps(2 * q), xsteps(2 * q + 1)]
                        interleave(blk, tracks, lists)
                    C.prev = tracks[0].join(tracks[1]).prev

                with ExitStack() as pp:
                    ph1 = [[pp.enter_context(nc.psum_tensor(
                        f"p6p1{e}_{i}_{w}", [128, 512], F32))
                        for w in range(2)] for i in range(2)]
                    ptp2 = [pp.enter_context(nc.psum_tensor(
                        f"p6t2{e}_{i}", [128, 1024], BF16)) for i in range(2)]
                    for oc in range(4):
                        C.dma("sy", lambda sy, e=e, oc=oc: sy.dma_start(
                            out=w13c[0].rearrange("r (ic o) -> r ic o", o=512),
                            in_=w13t_s[e].rearrange("(ic r) o -> r ic o", r=128)
                            [:, :, oc * 512:(oc + 1) * 512]))
                        C.dma("sy", lambda sy, e=e, oc=oc: sy.dma_start(
                            out=w13c[1].rearrange("r (ic o) -> r ic o", o=512),
                            in_=w13t_s[e].rearrange("(ic r) o -> r ic o", r=128)
                            [:, :, 2048 + oc * 512:2048 + (oc + 1) * 512]))
                        tracks = [C.fork("0"), C.fork("1")]

                        def hsteps(t0, oc=oc):
                            tr = (t0 // 2) % 2
                            S = []
                            for half, tt in enumerate((t0, t0 + 1)):
                                def mm1(t, tt=tt, tr=tr):
                                    for w in range(2):
                                        w3_ = w13c[w].rearrange(
                                            "r (ic o) -> r ic o", o=512)
                                        for ic in range(16):
                                            ins = t.matmul(
                                                ph1[tr][w][:, :],
                                                xqf[:, tt * H + ic * 128:
                                                    tt * H + (ic + 1) * 128],
                                                w3_[:, ic, :],
                                                start=(ic == 0),
                                                stop=(ic == 15))
                                    return ins
                                S.append(("t", mm1, False))
                                S.append(("a", lambda a, tr=tr: a.activation(
                                    s1f[tr][:, :], ph1[tr][0][:, :], AF.Silu),
                                    False))
                                S.append(("v", lambda v, tr=tr: v.tensor_copy(
                                    h3f[tr][:, :], ph1[tr][1][:, :]), False))
                                S.append(("g", lambda g, tr=tr, half=half:
                                          g.tensor_tensor(
                                              hf[tr][:, half * 512:
                                                     (half + 1) * 512],
                                              s1f[tr][:, :], h3f[tr][:, :],
                                              op=ALU.mult), False))
                            S += qdq_steps([(hf[tr][:, :], None,
                                             hqb[tr][:, :])], qbh[tr], 0, 1024)

                            def trh(t, tr=tr):
                                for j in range(8):
                                    ins = t.matmul(
                                        ptp2[tr][:, j * 128:(j + 1) * 128],
                                        hqb[tr][:, j * 128:(j + 1) * 128],
                                        ident_bf[:, :], is_transpose=True,
                                        start=(j == 0), stop=(j == 7))
                                return ins
                            S.append(("t", trh, False))
                            S.append(("v", lambda v, t0=t0, oc=oc, tr=tr:
                                      v.tensor_copy(
                                          hqf.rearrange("r (t c) -> r t c",
                                                        c=H)
                                          [:, t0:t0 + 2,
                                           oc * 512:(oc + 1) * 512],
                                          ptp2[tr].rearrange(
                                              "r (t c) -> r t c", c=512)),
                                      False))
                            return S

                        for q in range(2):
                            lists = [hsteps(4 * q), hsteps(4 * q + 2)]
                            interleave(blk, tracks, lists)
                        C.prev = tracks[0].join(tracks[1]).prev

                with ExitStack() as pp:
                    pyo = [pp.enter_context(nc.psum_tensor(
                        f"p6py{e}_{i}", [128, 512], F32)) for i in range(2)]
                    for dc in range(4):
                        w2cb = w13c[0]
                        C.dma("sy", lambda sy, e=e, dc=dc: sy.dma_start(
                            out=w2cb.rearrange("r (ic o) -> r ic o", o=512),
                            in_=w2t_s[e].rearrange("(ic r) o -> r ic o", r=128)
                            [:, :, dc * 512:(dc + 1) * 512]))
                        tracks = [C.fork("0"), C.fork("1")]

                        def ysteps(tt, dc=dc):
                            tr = tt % 2
                            S = []

                            def mm2(t, tt=tt, tr=tr):
                                w3_ = w2cb.rearrange("r (ic o) -> r ic o", o=512)
                                for j in range(16):
                                    ins = t.matmul(
                                        pyo[tr][:, :],
                                        hqf[:, tt * H + j * 128:
                                            tt * H + (j + 1) * 128],
                                        w3_[:, j, :],
                                        start=(j == 0), stop=(j == 15))
                                return ins
                            S.append(("t", mm2, False))
                            S.append(("a", lambda a, tr=tr: a.copy(
                                ostg[tr][:, :], pyo[tr][:, :]), False))
                            S.append(("g", lambda g, tt=tt, dc=dc, tr=tr:
                                      g.dma_start(
                                          out=out_d[tt * 128:(tt + 1) * 128,
                                                    dc * 512:(dc + 1) * 512],
                                          in_=ostg[tr][:, :],
                                          accum_op=ALU.add), True))
                            return S

                        for q in range(4):
                            lists = [ysteps(2 * q), ysteps(2 * q + 1)]
                            interleave(blk, tracks, lists)
                        C.prev = tracks[0].join(tracks[1]).prev

        with ExitStack() as ph, nc.Block() as blk:
            C = CHAIN.bind(blk)
            xrt = ph.enter_context(nc.sbuf_tensor("p7x", [128, H], F32))
            for tt in range(8):
                C.dma("sy", lambda sy, tt=tt: sy.dma_start(
                    out=xrt[:, :], in_=x_d[tt * 128:(tt + 1) * 128, :]))
                C.dma("g", lambda g, tt=tt: g.dma_start(
                    out=out_d[tt * 128:(tt + 1) * 128, :], in_=xrt[:, :],
                    accum_op=ALU.add))
            if debug:
                C.dma("a", lambda a: a.dma_start(out=dbg["d_xno"], in_=xno_s))
                C.dma("a", lambda a: a.dma_start(out=dbg["d_kf"], in_=kf_s[0]))
                C.dma("a", lambda a: a.dma_start(out=dbg["d_qf"], in_=qf_s[0]))
                C.dma("a", lambda a: a.dma_start(out=dbg["d_vt"], in_=vt_s[0]))
                C.dma("a", lambda a: a.dma_start(out=dbg["d_w13t"],
                                                 in_=w13t_s[0]))
                C.dma("a", lambda a: a.dma_start(out=dbg["d_w2t"],
                                                 in_=w2t_s[0]))
            prev = list(C.prev)

            def fin(g):
                for ps, pv in prev:
                    g.wait_ge(ps.h, pv)
            blk.gpsimd(fin)

    return nc


_NC_CACHE = {}


def kernel(L0_attn_query, L0_attn_key, L0_attn_value, L0_attn_Wqkv,
           L0_attn_Wproj, L0_attn_bproj, L0_ffn_gate_w, L0_ffn_w13, L0_ffn_w2,
           L0_ffn_sw1, L0_ffn_sw2, L0_ffn_sw3, L0_ffn_expert_bias,
           _debug=False):
    x = np.ascontiguousarray(np.asarray(L0_attn_query, dtype=np.float32))
    wqkv = np.asarray(L0_attn_Wqkv, dtype=np.float32)
    wproj = np.asarray(L0_attn_Wproj, dtype=np.float32)
    bproj = np.asarray(L0_attn_bproj, dtype=np.float32).reshape(1, H)
    gate = np.asarray(L0_ffn_gate_w, dtype=np.float32)
    w13 = np.ascontiguousarray(np.asarray(L0_ffn_w13, dtype=np.float32))
    w2 = np.ascontiguousarray(np.asarray(L0_ffn_w2, dtype=np.float32))
    sw1 = np.asarray(L0_ffn_sw1, dtype=np.float32)
    sw2 = np.asarray(L0_ffn_sw2, dtype=np.float32)
    sw3 = np.asarray(L0_ffn_sw3, dtype=np.float32)
    eb = np.asarray(L0_ffn_expert_bias, dtype=np.float32)

    wqkv_t = np.ascontiguousarray(wqkv.T)
    wproj_t = np.ascontiguousarray(wproj.T)
    gate_t = np.ascontiguousarray(gate.T)
    sw1_t = np.ascontiguousarray(sw1.T)
    sw3_t = np.ascontiguousarray(sw3.T)
    sw2_t = np.ascontiguousarray(sw2.T)
    ebias_r = np.ascontiguousarray(np.broadcast_to(eb[None, :], (128, E)))

    in_maps = []
    for c in range(NCORE):
        b2, half = c // 2, c % 2
        bidx = (np.arange(B) + b2) % B
        sidx = (np.arange(S) + half * 1024) % S
        x_c = x[bidx][:, sidx, :].reshape(T, H)
        heads = [b2 + 4 * hi for hi in range(4)]
        cols = []
        for part in range(3):
            for h in heads:
                cols.extend(range(h * 384 + part * 128,
                                  h * 384 + part * 128 + 128))
        wqt_c = np.ascontiguousarray(wqkv_t[:, cols])
        rows = []
        for npb in range(16):
            bqp, hi = npb // 4, npb % 4
            n = 4 * ((bqp + b2) % 4) + hi
            rows.extend(range(n * 128, (n + 1) * 128))
        wpt_c = np.ascontiguousarray(wproj_t[rows, :])
        in_maps.append({
            "x": np.ascontiguousarray(x_c), "wqt": wqt_c, "wpt": wpt_c,
            "bproj": np.ascontiguousarray(bproj), "gwt": gate_t,
            "ebias": ebias_r, "sw1t": sw1_t, "sw3t": sw3_t, "sw2t": sw2_t,
            "w13": w13, "w2": w2,
        })

    key = "dbg" if _debug else "nc"
    if key not in _NC_CACHE:
        _NC_CACHE[key] = build_program(debug=_debug)
    nc = _NC_CACHE[key]
    res = run_bass_kernel_spmd(nc, in_maps, core_ids=list(range(NCORE)))

    out = np.empty((B, S, H), dtype=np.float32)
    for c in range(NCORE):
        b2, half = c // 2, c % 2
        out[b2, half * 1024:(half + 1) * 1024, :] = np.asarray(
            res.results[c]["out"])
    if _debug:
        return out, res
    return out
